# version 06
# baseline (speedup 1.0000x reference)
import numpy as np
import ml_dtypes
from contextlib import ExitStack

import concourse.bass as bass
import concourse.mybir as mybir
from concourse.bass_utils import run_bass_kernel_spmd

F32 = mybir.dt.float32
BF16 = mybir.dt.bfloat16
AF = mybir.ActivationFunctionType
ALU = mybir.AluOpType
AX = mybir.AxisListType
NPBF = ml_dtypes.bfloat16

NCH = 16
NQT = 8
NKB = 32
ISQ_D = float(1.0 / np.sqrt(128.0))
HAD_N1 = float(1.0 / np.sqrt(2048.0))
HAD_N2 = float(1.0 / np.sqrt(8192.0))
EPS = 1e-5
B_SAFETY = 1.05
NEG_BIG = -30000.0

_TAPS = set()
import os as _os
DBG_NO_W2 = bool(int(_os.environ.get("K_DBG_NO_W2", "0")))
DBG_NO_ROPE = bool(int(_os.environ.get("K_DBG_NO_ROPE", "0")))
DBG_NO_COLL = bool(int(_os.environ.get("K_DBG_NO_COLL", "0")))
DBG_KEEP_SELF = bool(int(_os.environ.get("K_DBG_KEEP_SELF", "0")))

ENGS = ("tensor", "vector", "scalar", "gpsimd", "sync")
SEMS = ("dg", "ds", "dc", "pe", "ve", "se", "cc")
DMA_SEMS = ("dg", "ds", "dc")


class Prog:
    def __init__(self):
        self.ops = {e: [] for e in ENGS}
        self.plan = {s: 0 for s in SEMS}
        self.pending = {e: {} for e in ENGS}
        self.guards = {}

    def op(self, eng, fn, waits=(), inc=None, amt=None):
        ws = dict(self.pending[eng])
        self.pending[eng] = {}
        for w in waits:
            if w is None:
                continue
            s, v = w
            ws[s] = max(ws.get(s, 0), v)
        ev = None
        if inc is not None:
            if amt is None:
                amt = 16 if inc in DMA_SEMS else 1
            self.plan[inc] += amt
            ev = (inc, self.plan[inc])
        self.ops[eng].append((fn, ws, inc, amt))
        return ev

    def barrier(self):
        evs = [(s, self.plan[s]) for s in SEMS if self.plan[s] > 0]
        for e in ("tensor", "vector", "scalar"):
            dd = self.pending[e]
            for s, v in evs:
                dd[s] = max(dd.get(s, 0), v)

    def gw(self, key):
        return self.guards.get(key)

    def gs(self, key, ev):
        self.guards[key] = ev

    def ps_w(self, lo, hi):
        return [self.gw(("psb", b)) for b in range(lo // 512, (hi + 511) // 512)]

    def ps_r(self, lo, hi, ev):
        for b in range(lo // 512, (hi + 511) // 512):
            self.gs(("psb", b), ev)



def _rope_tables(positions, base):
    inv = 1.0 / base ** (np.arange(0, 128, 2, dtype=np.float64) / 128.0)
    ang = np.outer(inv, positions.astype(np.float64))
    cos = np.concatenate([np.cos(ang), np.cos(ang)], axis=0)
    sin = np.concatenate([np.sin(ang), np.sin(ang)], axis=0)
    return cos.astype(NPBF), sin.astype(NPBF)


def _mask_tile(r):
    kk = np.arange(128)[:, None]
    qi = np.arange(128)[None, :]
    tiles = []
    for dd in range(4):
        allowed = (128 * dd + kk) <= (128 * r + qi)
        tiles.append(np.where(allowed, 0.0, NEG_BIG))
    return np.concatenate(tiles, axis=1).astype(NPBF)


def _prep_inputs(inp):
    asc = np.ascontiguousarray
    x = np.asarray(inp["L0_attn_a_normalized_resid_pre"], dtype=np.float32)
    perm = np.concatenate([np.arange(0, 128, 2), np.arange(1, 128, 2)])

    wq = np.asarray(inp["L0_attn_a_W_Q"], np.float32)[:, :, perm]
    wk = np.asarray(inp["L0_attn_a_W_K"], np.float32)[:, :, perm]
    wv = np.asarray(inp["L0_attn_a_W_V"], np.float32)
    shared = {
        "wq_a": asc(wq.transpose(1, 0, 2).reshape(2048, 2048)),
        "wk_a": asc(wk.transpose(1, 0, 2).reshape(2048, 2048)),
        "wv_a": asc(wv.transpose(1, 0, 2).reshape(2048, 2048)),
        "wo_a": asc(np.asarray(inp["L0_attn_a_W_O"], np.float32).reshape(2048, 2048)),
        "bq_a": asc(np.asarray(inp["L0_attn_a_b_Q"], np.float32)[:, perm].T),
        "bk_a": asc(np.asarray(inp["L0_attn_a_b_K"], np.float32)[:, perm].T),
        "bv_col": asc(np.asarray(inp["L0_attn_a_b_V"], np.float32).reshape(16, 128).T),
        "bo_col": asc(np.asarray(inp["L0_attn_a_b_O"], np.float32).reshape(16, 128).T),
        "wqkv_b": asc(np.asarray(inp["L0_attn_b_w_qkv"], np.float32).T),
        "wo_b": asc(np.asarray(inp["L0_attn_b_w_o"], np.float32).T),
        "qng": asc(np.asarray(inp["L0_attn_b_qn_g"], np.float32).reshape(128, 1)),
        "qnb": asc(np.asarray(inp["L0_attn_b_qn_b"], np.float32).reshape(128, 1)),
        "kng": asc(np.asarray(inp["L0_attn_b_kn_g"], np.float32).reshape(128, 1)),
        "knb": asc(np.asarray(inp["L0_attn_b_kn_b"], np.float32).reshape(128, 1)),
        "w_upT": asc(np.asarray(inp["L0_ffn_W_up"], np.float32).T),
        "w_gateT": asc(np.asarray(inp["L0_ffn_W_gate"], np.float32).T),
        "su_ug": asc(np.asarray(inp["L0_ffn_SU_ug"], np.float32).reshape(16, 128).T),
        "su_dp_re": asc(np.asarray(inp["L0_ffn_SU_dp"], np.float32).reshape(64, 128)),
        "wsu": asc(np.asarray(inp["L0_ffn_Wscale_ug"], np.float32)[:8192].reshape(64, 128).T),
        "wsg": asc(np.asarray(inp["L0_ffn_Wscale_ug"], np.float32)[8192:].reshape(64, 128).T),
        "wsdp": asc(np.asarray(inp["L0_ffn_Wscale_dp"], np.float32).reshape(16, 128).T),
    }
    w_downT = np.asarray(inp["L0_ffn_W_down"], np.float32).T
    maps = []
    for c in range(8):
        g, r = divmod(c, 4)
        blocks = [4 * m + r for m in range(8)]
        pos = np.concatenate([np.arange(128) + 128 * b for b in blocks])
        xc = x[g].reshape(32, 128, 2048)[blocks].reshape(1024, 2048)
        cos_a, sin_a = _rope_tables(pos, 10000.0)
        cos_b, sin_b = _rope_tables(pos, 1000000.0)
        m = dict(shared)
        m["xT"] = asc(xc.T)
        m["w_downT_sl"] = asc(w_downT[:, c * 256:(c + 1) * 256])
        m["cos_a"] = asc(cos_a); m["sin_a"] = asc(sin_a)
        m["cos_b"] = asc(cos_b); m["sin_b"] = asc(sin_b)
        m["maskT"] = _mask_tile(r)
        maps.append(m)
    return maps


def _assemble(results):
    out = np.empty((2, 4096, 2048), np.float32)
    for c in range(8):
        g, r = divmod(c, 4)
        xc = np.asarray(results[c]["out_x"]).T.reshape(8, 128, 2048)
        for mm in range(8):
            b = 4 * mm + r
            out[g, b * 128:(b + 1) * 128, :] = xc[mm]
    return out



def build_program():
    nc = bass.Bass("TRN2", target_bir_lowering=False, debug=False)
    p = Prog()

    ap = {}
    def ein(name, shape, dt=F32):
        ap[name] = nc.dram_tensor(name, list(shape), dt, kind="ExternalInput").ap()
    ein("xT", (2048, 1024))
    for n in ("wq_a", "wk_a", "wv_a", "wo_a"):
        ein(n, (2048, 2048))
    ein("bq_a", (128, 16)); ein("bk_a", (128, 16))
    ein("bv_col", (128, 16)); ein("bo_col", (128, 16))
    ein("wqkv_b", (2048, 3072)); ein("wo_b", (2048, 2048))
    for n in ("qng", "qnb", "kng", "knb"):
        ein(n, (128, 1))
    ein("w_upT", (2048, 8192)); ein("w_gateT", (2048, 8192))
    ein("w_downT_sl", (8192, 256))
    ein("su_ug", (128, 16)); ein("su_dp_re", (64, 128))
    ein("wsu", (128, 64)); ein("wsg", (128, 64)); ein("wsdp", (128, 16))
    for n in ("cos_a", "sin_a", "cos_b", "sin_b"):
        ein(n, (128, 1024), BF16)
    ein("maskT", (128, 512), BF16)

    out_x = nc.dram_tensor("out_x", [2048, 1024], F32, kind="ExternalOutput").ap()

    kvA_own = nc.dram_tensor("kvA_own", [4224, 1024], BF16, kind="Internal")
    kvA_all = nc.dram_tensor("kvA_all", [4 * 4224, 1024], BF16, kind="Internal")
    kvB_own = nc.dram_tensor("kvB_own", [1152, 1024], BF16, kind="Internal")
    kvB_all = nc.dram_tensor("kvB_all", [4 * 1152, 1024], BF16, kind="Internal")
    vd = nc.dram_tensor("vd", [8192, 256], BF16, kind="Internal")
    wpp_own = nc.dram_tensor("wpp_own", [8192, 256], BF16, kind="Internal")
    wpp_all = nc.dram_tensor("wpp_all", [8 * 8192, 256], BF16, kind="Internal")

    ic = lambda a, n: nc.inline_tensor(np.ascontiguousarray(a), name=n).ap()
    ones_col_f = ic(np.ones((128, 1), np.float32), "c_ocf")
    ones_col_b = ic(np.ones((128, 1), NPBF), "c_ocb")
    ones_row_f = ic(np.ones((1, 128), np.float32), "c_orf")
    ident_b = ic(np.eye(128, dtype=NPBF), "c_id")
    rotT_np = np.zeros((128, 128), np.float32)
    for i in range(64):
        rotT_np[i, 64 + i] = 1.0
        rotT_np[64 + i, i] = -1.0
    rotT_c = ic(rotT_np.astype(NPBF), "c_rotT")

    def _hmat(n):
        h = np.array([[1.0]])
        while h.shape[0] < n:
            h = np.block([[h, h], [h, -h]])
        return h
    H16 = _hmat(16)
    Hp_c = ic(_hmat(128).astype(NPBF), "c_hp")
    Hm_c = ic((-_hmat(128)).astype(NPBF), "c_hm")
    H64_c = ic(_hmat(64).astype(NPBF), "c_h64")

    dbg = {}
    es = ExitStack()
    ec = es.enter_context

    buf_R = ec(nc.sbuf_tensor("buf_R", [128, 16384], F32))
    buf_L = ec(nc.sbuf_tensor("buf_L", [128, 16384], BF16))
    buf_A = ec(nc.sbuf_tensor("buf_A", [128, 16384], BF16))
    buf_B = ec(nc.sbuf_tensor("buf_B", [128, 16384], BF16))
    buf_W = ec(nc.sbuf_tensor("buf_W", [128, 8192], BF16))
    buf_S = ec(nc.sbuf_tensor("buf_S", [128, 3072], BF16))
    buf_K = ec(nc.sbuf_tensor("buf_K", [128, 2048], BF16))
    scr = ec(nc.sbuf_tensor("scr", [128, 1024], F32))
    rows = ec(nc.sbuf_tensor("rows", [1, 2048], F32))
    tabs = ec(nc.sbuf_tensor("tabs", [128, 2048], BF16))
    mask_sb = ec(nc.sbuf_tensor("mask_sb", [128, 512], BF16))
    bo_sb = ec(nc.sbuf_tensor("bo_sb", [128, 16], F32))
    bv_sb = ec(nc.sbuf_tensor("bv_sb", [128, 16], F32))
    vec16 = ec(nc.sbuf_tensor("vec16", [1, 128], F32))
    bq_sb = ec(nc.sbuf_tensor("bq_sb", [128, 16], F32))
    bk_sb = ec(nc.sbuf_tensor("bk_sb", [128, 16], F32))
    qn_sb = ec(nc.sbuf_tensor("qn_sb", [128, 4], F32))
    su_sb = ec(nc.sbuf_tensor("su_sb", [128, 16], F32))
    sudp_sb = ec(nc.sbuf_tensor("sudp_sb", [64, 128], F32))
    wsu_sb = ec(nc.sbuf_tensor("wsu_sb", [128, 64], F32))
    wsg_sb = ec(nc.sbuf_tensor("wsg_sb", [128, 64], F32))
    wsdp_sb = ec(nc.sbuf_tensor("wsdp_sb", [128, 16], F32))
    negB_sb = ec(nc.sbuf_tensor("negB_sb", [128, 16], F32))
    eps_sb = ec(nc.sbuf_tensor("eps_sb", [1, 1], F32))
    oc_f = ec(nc.sbuf_tensor("oc_f", [128, 1], F32))
    oc_b = ec(nc.sbuf_tensor("oc_b", [128, 1], BF16))
    or_f = ec(nc.sbuf_tensor("or_f", [1, 128], F32))
    id_sb = ec(nc.sbuf_tensor("id_sb", [128, 128], BF16))
    rot_sb = ec(nc.sbuf_tensor("rot_sb", [128, 128], BF16))
    hp_sb = ec(nc.sbuf_tensor("hp_sb", [128, 128], BF16))
    hm_sb = ec(nc.sbuf_tensor("hm_sb", [128, 128], BF16))
    h64_sb = ec(nc.sbuf_tensor("h64_sb", [64, 64], BF16))

    ps = ec(nc.psum_tensor("ps", [128, 4096], F32))
    sems = {s: ec(nc.semaphore("sem_" + s)) for s in SEMS}

    def Rk(k): return buf_R[:, k * 1024:(k + 1) * 1024]
    def Lk(k): return buf_L[:, k * 1024:(k + 1) * 1024]
    def Ak(k): return buf_A[:, k * 1024:(k + 1) * 1024]
    def Bk(k): return buf_B[:, k * 1024:(k + 1) * 1024]
    def Wslot(i): return buf_W[:, i * 2048:(i + 1) * 2048]
    def Sslot(i): return buf_S[:, i * 1024:(i + 1) * 1024]
    def Kslot(j): return buf_K[:, j * 1024:j * 1024 + 512]
    def Vslot(j): return buf_K[:, j * 1024 + 512:(j + 1) * 1024]

    resid_ev = [None] * NCH
    ln_ev = [None] * NCH

    def d(eng, out, in_, waits=()):
        sem = {"gpsimd": "dg", "sync": "ds", "scalar": "dc"}[eng]
        w = list(waits) + [p.guards.get("coll_last")]
        return p.op(eng, lambda e, o=out, i=in_: e.dma_start(out=o, in_=i),
                    waits=w, inc=sem)

    def coll(kind, op, groups, in_t, out_t, waits):
        if DBG_NO_COLL:
            n = out_t.ap().shape[0] // in_t.ap().shape[0]
            rows = in_t.ap().shape[0]
            ev = None
            for i in range(n):
                ev = d("gpsimd", out_t.ap()[i * rows:(i + 1) * rows, :], in_t.ap(),
                       waits=list(waits))
            return p.op("gpsimd", lambda g: g.engine_nop(), waits=[ev], inc="cc")
        w = list(waits) + [(s, p.plan[s]) for s in DMA_SEMS if p.plan[s] > 0]
        ev = p.op("gpsimd", lambda g, k=kind, o=op, gr=groups, it=in_t, ot=out_t:
                  g.collective_compute(k, o, replica_groups=gr,
                                       ins=[it.ap()], outs=[ot.ap()]),
                  waits=w, inc="cc")
        p.gs("coll_last", ev)
        return ev

    def tap(tag, src_ap, shape, waits, dt=F32):
        if tag not in _TAPS or tag in dbg:
            return
        t = nc.dram_tensor("dbg_" + tag, list(shape), dt, kind="ExternalOutput").ap()
        dbg[tag] = True
        d("sync", t, src_ap, waits=list(waits))

    ev_init = []
    for dst, src in ((oc_f[:, :], ones_col_f), (oc_b[:, :], ones_col_b),
                     (or_f[:, :], ones_row_f),
                     (id_sb[:, :], ident_b), (rot_sb[:, :], rotT_c),
                     (hp_sb[:, :], Hp_c), (hm_sb[:, :], Hm_c),
                     (h64_sb[:, :], H64_c),
                     (bq_sb[:, :], ap["bq_a"]), (bk_sb[:, :], ap["bk_a"]),
                     (qn_sb[:, 0:1], ap["qng"]), (qn_sb[:, 1:2], ap["qnb"]),
                     (qn_sb[:, 2:3], ap["kng"]), (qn_sb[:, 3:4], ap["knb"]),
                     (su_sb[:, :], ap["su_ug"]), (sudp_sb[:, :], ap["su_dp_re"]),
                     (wsu_sb[:, :], ap["wsu"]), (wsg_sb[:, :], ap["wsg"]),
                     (wsdp_sb[:, :], ap["wsdp"]), (mask_sb[:, :], ap["maskT"]),
                     (bo_sb[:, :], ap["bo_col"]), (bv_sb[:, :], ap["bv_col"])):
        ev_init.append(d("scalar", dst, src))
    ev_memset = p.op("vector", lambda v: v.memset(eps_sb[:, :], EPS), inc="ve")
    ev_init.append(ev_memset)
    EVI = list(ev_init)

    for k in range(NCH):
        resid_ev[k] = d("sync", Rk(k), ap["xT"][k * 128:(k + 1) * 128, :])

    def w2_precompute():
        ev_st1 = []
        for bi in range(8):
            half = bi % 2
            src = ap["w_downT_sl"].rearrange("(b a pp) r -> b pp a r", b=8, pp=128)[bi]
            dstA = buf_A[:, half * 2048:(half + 1) * 2048]
            evw = d("gpsimd", dstA.rearrange("pp (a r) -> pp a r", r=256), src,
                    waits=[p.gw(("w2in", half))] + EVI)
            evac_evs = []
            for a8 in range(8):
                lo = (a8 % 4) * 256 + (a8 // 4) * 1024
                evm = p.op("tensor", lambda t, h=half, a=a8, lo=lo: t.matmul(
                    ps[:, lo:lo + 256], hp_sb[:, :],
                    buf_A[:, h * 2048 + a * 256:h * 2048 + (a + 1) * 256],
                    start=True, stop=True),
                    waits=[evw] + p.ps_w(lo, lo + 256), inc="pe")
                eve = p.op("vector", lambda v, h=half, a=a8, lo=lo: v.tensor_scalar(
                    buf_A[:, 4096 + h * 2048 + a * 256:4096 + h * 2048 + (a + 1) * 256],
                    ps[:, lo:lo + 256], HAD_N2, None, op0=ALU.mult),
                    waits=[evm, p.gw(("w2out", half))], inc="ve")
                p.ps_r(lo, lo + 256, eve)
                evac_evs.append(eve)
            dstv = vd.ap().rearrange("(b a pp) r -> b pp a r", b=8, pp=128)[bi]
            srcv = buf_A[:, 4096 + half * 2048:4096 + (half + 1) * 2048]
            evd = d("sync", dstv, srcv.rearrange("pp (a r) -> pp a r", r=256),
                    waits=evac_evs)
            p.gs(("w2in", half), evd)
            p.gs(("w2out", half), evd)
            ev_st1.append(evd)
        ev_out = []
        for pb in range(32):
            half = pb % 2
            src = vd.ap().rearrange("(a pp) r -> pp a r", pp=128)[pb * 4:(pb + 1) * 4]
            src = src.rearrange("q a r -> a q r")
            dst = buf_A[0:64, 8192 + half * 1024:8192 + (half + 1) * 1024]
            evw = d("sync", dst.rearrange("a (q r) -> a q r", r=256), src,
                    waits=[p.gw(("w2in2", half))] + ev_st1)
            lo = 2048 + half * 1024
            mevs = []
            for hh in range(2):
                mevs.append(p.op("tensor", lambda t, h=half, hh=hh, lo=lo: t.matmul(
                    ps[0:64, lo + hh * 512:lo + (hh + 1) * 512], h64_sb[:, :],
                    buf_A[0:64, 8192 + h * 1024 + hh * 512:8192 + h * 1024 + (hh + 1) * 512],
                    start=True, stop=True),
                    waits=[evw] + p.ps_w(lo, lo + 1024) + EVI, inc="pe"))
            evs = []
            for q in range(4):
                P = pb * 4 + q
                eve = p.op("vector", lambda v, h=half, q=q, P=P, lo=lo: v.tensor_scalar(
                    buf_A[0:64, 12288 + h * 1024 + q * 256:12288 + h * 1024 + (q + 1) * 256],
                    ps[0:64, lo + q * 256:lo + (q + 1) * 256],
                    sudp_sb[:, P:P + 1], None, op0=ALU.mult),
                    waits=mevs + [p.gw(("w2out2", half))] + EVI, inc="ve")
                evs.append(eve)
            p.ps_r(lo, lo + 1024, evs[-1])
            for q in range(4):
                P = pb * 4 + q
                dstw = wpp_own.ap().rearrange("(a pp) r -> pp a r", pp=128)[P]
                evd = d("sync", dstw,
                        buf_A[0:64, 12288 + half * 1024 + q * 256:12288 + half * 1024 + (q + 1) * 256].rearrange(
                            "a (o r) -> a o r", o=1)[:, 0, :] if False else
                        buf_A[0:64, 12288 + half * 1024 + q * 256:12288 + half * 1024 + (q + 1) * 256],
                        waits=evs)
                ev_out.append(evd)
            p.gs(("w2in2", half), ev_out[-1])
            p.gs(("w2out2", half), ev_out[-1])
        return ev_out

    w2_out_evs = [] if DBG_NO_W2 else w2_precompute()
    W2 = {"ev": None}

    def emit_layernorm(su_fold):
        mm_last = None
        for k in range(NCH):
            sl = k % 3
            evq = p.op("scalar", lambda s, k=k, sl=sl: s.square(Sslot(sl), Rk(k)),
                       waits=[resid_ev[k], p.gw(("sslot", sl))], inc="se")
            for hh in range(2):
                w1 = [resid_ev[k]] + (p.ps_w(2048, 3072) if k == 0 else [])
                p.op("tensor", lambda t, k=k, hh=hh: t.matmul(
                    ps[0:1, 2048 + hh * 512:2048 + (hh + 1) * 512], oc_f[:, :],
                    Rk(k)[:, hh * 512:(hh + 1) * 512],
                    start=(k == 0), stop=(k == NCH - 1)), waits=w1, inc="pe")
                w2 = [evq] + (p.ps_w(3072, 4096) if k == 0 else [])
                mm_last = p.op("tensor", lambda t, k=k, hh=hh, sl=sl: t.matmul(
                    ps[0:1, 3072 + hh * 512:3072 + (hh + 1) * 512], oc_b[:, :],
                    Sslot(sl)[:, hh * 512:(hh + 1) * 512],
                    start=(k == 0), stop=(k == NCH - 1)), waits=w2, inc="pe")
            p.gs(("sslot", sl), mm_last)
        ev_mu = p.op("vector", lambda v: v.tensor_scalar(
            rows[0:1, 0:1024], ps[0:1, 2048:3072], 1.0 / 2048.0, None, op0=ALU.mult),
            waits=[mm_last, p.gw("rows")], inc="ve")
        ev_var = p.op("vector", lambda v: v.tensor_scalar(
            rows[0:1, 1024:2048], ps[0:1, 3072:4096], 1.0 / 2048.0, None, op0=ALU.mult),
            waits=[mm_last], inc="ve")
        p.ps_r(2048, 4096, ev_var)
        ev_t = p.op("vector", lambda v: v.tensor_mul(
            scr[0:1, 0:1024], rows[0:1, 0:1024], rows[0:1, 0:1024]),
            waits=[ev_mu, p.gw("scr")], inc="ve")
        ev_var2 = p.op("vector", lambda v: v.tensor_sub(
            rows[0:1, 1024:2048], rows[0:1, 1024:2048], scr[0:1, 0:1024]),
            waits=[ev_var, ev_t], inc="ve")
        ev_sd = p.op("scalar", lambda s: s.activation(
            out=scr[0:1, 0:1024], in_=rows[0:1, 1024:2048], func=AF.Sqrt,
            bias=eps_sb[0:1, 0:1]),
            waits=[ev_var2, ev_memset], inc="se")
        ev_rstd = p.op("vector", lambda v: v.reciprocal(
            rows[0:1, 1024:2048], scr[0:1, 0:1024]), waits=[ev_sd], inc="ve")
        bevs = []
        for hh in range(2):
            bevs.append(p.op("tensor", lambda t, hh=hh: t.matmul(
                ps[:, hh * 512:(hh + 1) * 512], or_f[0:1, :],
                rows[0:1, hh * 512:(hh + 1) * 512], start=True, stop=True),
                waits=[ev_mu] + p.ps_w(hh * 512, (hh + 1) * 512), inc="pe"))
            bevs.append(p.op("tensor", lambda t, hh=hh: t.matmul(
                ps[:, 1024 + hh * 512:1024 + (hh + 1) * 512], or_f[0:1, :],
                rows[0:1, 1024 + hh * 512:1024 + (hh + 1) * 512], start=True, stop=True),
                waits=[ev_rstd] + p.ps_w(1024 + hh * 512, 1024 + (hh + 1) * 512),
                inc="pe"))
        p.gs("rows", bevs[-1])
        for k in range(NCH):
            ev1 = p.op("vector", lambda v, k=k: v.tensor_sub(
                scr[:, :], Rk(k), ps[:, 0:1024]),
                waits=bevs + [resid_ev[k], p.gw("scr")], inc="ve")
            if su_fold:
                ev2 = p.op("vector", lambda v, k=k: v.scalar_tensor_tensor(
                    Lk(k), scr[:, :], su_sb[:, k:k + 1], ps[:, 1024:2048],
                    op0=ALU.mult, op1=ALU.mult),
                    waits=[ev1] + EVI, inc="ve")
            else:
                ev2 = p.op("vector", lambda v, k=k: v.tensor_tensor(
                    Lk(k), scr[:, :], ps[:, 1024:2048], op=ALU.mult),
                    waits=[ev1], inc="ve")
            p.gs("scr", ev2)
            ln_ev[k] = ev2
        p.ps_r(0, 2048, ln_ev[NCH - 1])

    def rope_chunk(dst_ap, src_ev, tab_ev):
        if DBG_NO_ROPE:
            return src_ev
        lo = 2048
        rot_evs = []
        for hh in range(2):
            rot_evs.append(p.op("tensor", lambda t, hh=hh: t.matmul(
                ps[:, lo + hh * 512:lo + (hh + 1) * 512], rot_sb[:, :],
                dst_ap[:, hh * 512:(hh + 1) * 512], start=True, stop=True),
                waits=[src_ev] + p.ps_w(lo, lo + 1024) + EVI, inc="pe"))
        ev1 = p.op("vector", lambda v: v.tensor_tensor(
            Sslot(2), ps[:, lo:lo + 1024], tabs[:, 1024:2048], op=ALU.mult),
            waits=rot_evs + [p.gw(("sslot", 2))] + tab_ev, inc="ve")
        p.ps_r(lo, lo + 1024, ev1)
        ev2 = p.op("vector", lambda v: v.tensor_tensor(
            dst_ap, dst_ap, tabs[:, 0:1024], op=ALU.mult),
            waits=rot_evs + tab_ev, inc="ve")
        ev3 = p.op("vector", lambda v: v.tensor_add(dst_ap, dst_ap, Sslot(2)),
                   waits=[ev1, ev2], inc="ve")
        p.gs(("sslot", 2), ev3)
        return ev3

    def norm2max_chunk(src_ap, src_ev, vec_off):
        evq = p.op("scalar", lambda s: s.square(Sslot(2), src_ap),
                   waits=[src_ev, p.gw(("sslot", 2))], inc="se")
        mevs = []
        for hh in range(2):
            mevs.append(p.op("tensor", lambda t, hh=hh: t.matmul(
                ps[0:1, 3072 + hh * 512:3072 + (hh + 1) * 512], oc_b[:, :],
                Sslot(2)[:, hh * 512:(hh + 1) * 512], start=True, stop=True),
                waits=[evq] + p.ps_w(3072, 4096), inc="pe"))
        ev = p.op("vector", lambda v, vo=vec_off: v.reduce_max(
            vec16[0:1, vo:vo + 1], ps[0:1, 3072:4096], axis=AX.X),
            waits=mevs, inc="ve")
        p.ps_r(3072, 4096, ev)
        p.gs(("sslot", 2), ev)
        return ev

    def attn_sublayer(var):
        is_a = (var == "a")
        SEG = 4224 if is_a else 1152
        VOFF = 2048 if is_a else 512
        VROWS = 2048 if is_a else 512
        NR = 4096 if is_a else 1024
        n_kch = 16 if is_a else 4
        fw = 512 if is_a else 128
        kv_own = kvA_own if is_a else kvB_own
        kv_all = kvA_all if is_a else kvB_all

        tev1 = d("sync", tabs[:, 0:1024], ap["cos_" + var], waits=[p.gw("tabs")])
        tev2 = d("sync", tabs[:, 1024:2048], ap["sin_" + var], waits=[p.gw("tabs")])
        tab_ev = [tev1, tev2]

        emit_layernorm(False)
        tap("ln_" + var, buf_L[:, :], (128, 16384), list(ln_ev), BF16)
        p.barrier()

        if is_a:
            w_k_src = lambda m: ap["wk_a"].rearrange(
                "(k pp) (mm f) -> pp k mm f", pp=128, f=128)[:, :, m, :]
            w_q_src = lambda m: ap["wq_a"].rearrange(
                "(k pp) (mm f) -> pp k mm f", pp=128, f=128)[:, :, m, :]
        else:
            w_k_src = lambda m: ap["wqkv_b"].rearrange(
                "(k pp) (mm f) -> pp k mm f", pp=128, f=128)[:, :, 16 + m, :]
            w_q_src = lambda m: ap["wqkv_b"].rearrange(
                "(k pp) (mm f) -> pp k mm f", pp=128, f=128)[:, :, m, :]

        def proj_mms(lhs_fn, evw, po):
            evs = []
            ln_last = ln_ev[NCH - 1]
            for k in range(NCH):
                for hh in range(2):
                    w = [evw, ln_last]
                    if k == 0:
                        w += p.ps_w(po + hh * 512, po + (hh + 1) * 512)
                    evs.append(p.op("tensor", lambda t, k=k, hh=hh, po=po: t.matmul(
                        ps[:, po + hh * 512:po + (hh + 1) * 512],
                        lhs_fn(k),
                        Lk(k)[:, hh * 512:(hh + 1) * 512],
                        start=(k == 0), stop=(k == NCH - 1)), waits=w, inc="pe"))
            return evs

        def qk_ln_ip(buf, g_col, b_col, ev_in):
            evq = p.op("scalar", lambda s: s.square(Sslot(2), buf),
                       waits=[ev_in, p.gw(("sslot", 2))], inc="se")
            mevs = []
            for hh in range(2):
                mevs.append(p.op("tensor", lambda t, hh=hh: t.matmul(
                    ps[0:1, 2048 + hh * 512:2048 + (hh + 1) * 512], oc_b[:, :],
                    buf[:, hh * 512:(hh + 1) * 512], start=True, stop=True),
                    waits=[ev_in] + p.ps_w(2048, 3072), inc="pe"))
                mevs.append(p.op("tensor", lambda t, hh=hh: t.matmul(
                    ps[0:1, 3072 + hh * 512:3072 + (hh + 1) * 512], oc_b[:, :],
                    Sslot(2)[:, hh * 512:(hh + 1) * 512], start=True, stop=True),
                    waits=[evq] + p.ps_w(3072, 4096), inc="pe"))
            ev_mu = p.op("vector", lambda v: v.tensor_scalar(
                rows[0:1, 0:1024], ps[0:1, 2048:3072], 1.0 / 128.0, None,
                op0=ALU.mult), waits=mevs + [p.gw("rows")], inc="ve")
            ev_msq = p.op("vector", lambda v: v.tensor_scalar(
                rows[0:1, 1024:2048], ps[0:1, 3072:4096], 1.0 / 128.0, None,
                op0=ALU.mult), waits=mevs, inc="ve")
            p.gs(("sslot", 2), ev_msq)
            ev_t = p.op("vector", lambda v: v.tensor_mul(
                scr[0:1, 0:1024], rows[0:1, 0:1024], rows[0:1, 0:1024]),
                waits=[ev_mu, p.gw("scr")], inc="ve")
            ev_var = p.op("vector", lambda v: v.tensor_sub(
                rows[0:1, 1024:2048], rows[0:1, 1024:2048], scr[0:1, 0:1024]),
                waits=[ev_msq, ev_t], inc="ve")
            ev_sd = p.op("scalar", lambda s: s.activation(
                out=scr[0:1, 0:1024], in_=rows[0:1, 1024:2048], func=AF.Sqrt,
                bias=eps_sb[0:1, 0:1]), waits=[ev_var, ev_memset], inc="se")
            ev_rstd = p.op("vector", lambda v: v.reciprocal(
                rows[0:1, 1024:2048], scr[0:1, 0:1024]), waits=[ev_sd], inc="ve")
            bevs = []
            for hh in range(2):
                bevs.append(p.op("tensor", lambda t, hh=hh: t.matmul(
                    ps[:, 2048 + hh * 512:2048 + (hh + 1) * 512], or_f[0:1, :],
                    rows[0:1, hh * 512:(hh + 1) * 512], start=True, stop=True),
                    waits=[ev_mu] + p.ps_w(2048, 3072), inc="pe"))
                bevs.append(p.op("tensor", lambda t, hh=hh: t.matmul(
                    ps[:, 3072 + hh * 512:3072 + (hh + 1) * 512], or_f[0:1, :],
                    rows[0:1, 1024 + hh * 512:1024 + (hh + 1) * 512],
                    start=True, stop=True),
                    waits=[ev_rstd] + p.ps_w(3072, 4096), inc="pe"))
            p.gs("rows", bevs[-1])
            ev1 = p.op("vector", lambda v: v.tensor_sub(
                scr[:, :], buf, ps[:, 2048:3072]), waits=bevs, inc="ve")
            ev2 = p.op("vector", lambda v: v.tensor_tensor(
                scr[:, :], scr[:, :], ps[:, 3072:4096], op=ALU.mult),
                waits=[ev1], inc="ve")
            p.ps_r(2048, 4096, ev2)
            ev3 = p.op("vector", lambda v, g=g_col, b=b_col: v.tensor_scalar(
                buf, scr[:, :], qn_sb[:, g:g + 1], qn_sb[:, b:b + 1],
                op0=ALU.mult, op1=ALU.add),
                waits=[ev2] + EVI, inc="ve")
            p.gs("scr", ev3)
            return ev3

        ship_evs = []
        nrm_evs = []
        for m in range(n_kch):
            wsl = m % 4
            evw = d("gpsimd", Wslot(wsl).rearrange("pp (k f) -> pp k f", f=128),
                    w_k_src(m), waits=[p.gw(("wslot", wsl))])
            po = (m % 2) * 1024
            mm_evs = proj_mms(
                lambda k, wsl=wsl: Wslot(wsl)[:, k * 128:(k + 1) * 128], evw, po)
            p.gs(("wslot", wsl), mm_evs[-1])
            raw = m % 2
            if is_a:
                ev_evac = p.op("vector", lambda v, m=m, po=po, raw=raw: v.tensor_scalar(
                    Sslot(raw), ps[:, po:po + 1024], bk_sb[:, m:m + 1], None,
                    op0=ALU.add),
                    waits=mm_evs + [p.gw(("sslot", raw))] + EVI, inc="ve")
                p.ps_r(po, po + 1024, ev_evac)
            else:
                ev_e0 = p.op("vector", lambda v, po=po, raw=raw: v.tensor_copy(
                    Sslot(raw), ps[:, po:po + 1024]),
                    waits=mm_evs + [p.gw(("sslot", raw))], inc="ve")
                p.ps_r(po, po + 1024, ev_e0)
                ev_evac = qk_ln_ip(Sslot(raw), 2, 3, ev_e0)
            ev_rope = rope_chunk(Sslot(raw), ev_evac, tab_ev)
            ev_nrm = norm2max_chunk(Sslot(raw), ev_rope, 16 + m)
            nrm_evs.append(ev_nrm)
            ev_ship = d("sync", kv_own.ap()[m * 128:(m + 1) * 128, :], Sslot(raw),
                        waits=[ev_rope, ev_nrm])
            p.gs(("sslot", raw), ev_ship)
            ship_evs.append(ev_ship)

        if is_a:
            v_ship_view = kv_own.ap()[VOFF:VOFF + VROWS, :].rearrange(
                "(t h) c -> t (h c)", h=2)
        else:
            v_ship_view = kv_own.ap()[VOFF:VOFF + VROWS, :].rearrange(
                "t (h c) -> (t h) c", h=2)
        vship_evs = []
        last_v_mm = [None, None]
        for sl in range(4):
            reg = sl % 2
            if is_a:
                vsrc = ap["wv_a"].rearrange(
                    "(k pp) (s f) -> pp k s f", pp=128, f=512)[:, :, sl, :]
            else:
                vsrc = ap["wqkv_b"].rearrange(
                    "(k pp) (s f) -> pp k s f", pp=128, f=128)[:, :, 20 + sl, :]
            vstg = buf_B[:, reg * 8192:reg * 8192 + 16 * fw]
            evw = d("gpsimd", vstg.rearrange("pp (k f) -> pp k f", f=fw), vsrc,
                    waits=[p.gw(("bufB", reg))])
            for tt in range(NQT):
                lo = 2048 + reg * 512
                mm = []
                for k in range(NCH):
                    w = [evw, ln_ev[NCH - 1]]
                    if k == 0:
                        w += p.ps_w(lo, lo + fw)
                    mm.append(p.op("tensor", lambda t, tt=tt, k=k, lo=lo, vstg=vstg: t.matmul(
                        ps[:, lo:lo + fw], Lk(k)[:, tt * 128:(tt + 1) * 128],
                        vstg[:, k * fw:(k + 1) * fw],
                        start=(k == 0), stop=(k == NCH - 1)), waits=w, inc="pe"))
                last_v_mm[reg] = mm[-1]
                sslot = tt % 2
                ev_evac = p.op("vector", lambda v, lo=lo, sslot=sslot: v.tensor_copy(
                    Sslot(sslot)[:, 0:fw], ps[:, lo:lo + fw]),
                    waits=[mm[-1], p.gw(("sslot", sslot))], inc="ve")
                p.ps_r(lo, lo + fw, ev_evac)
                ev_ship = d("sync",
                            v_ship_view[tt * 128:(tt + 1) * 128, sl * fw:(sl + 1) * fw],
                            Sslot(sslot)[:, 0:fw], waits=[ev_evac])
                p.gs(("sslot", sslot), ev_ship)
                vship_evs.append(ev_ship)
            p.gs(("bufB", reg), mm[-1])
        ev_nrm_ship = d("gpsimd", kv_own.ap()[NR:NR + 1, 0:n_kch],
                        vec16[0:1, 16:16 + n_kch], waits=nrm_evs)

        pre_evs = []
        for k in range(8):
            reg = k // 4
            dst = buf_B[:, k * 2048:(k + 1) * 2048]
            pre_evs.append(d("gpsimd", dst.rearrange("pp (kk f) -> pp kk f", f=128),
                             w_q_src(k),
                             waits=[last_v_mm[reg], p.gw(("bufB", reg))]))

        ev_ag = coll("AllGather", ALU.bypass, [[0, 1, 2, 3], [4, 5, 6, 7]],
                     kv_own, kv_all, ship_evs + vship_evs + [ev_nrm_ship])
        if is_a and not DBG_NO_W2:
            W2["ev"] = coll("AllGather", ALU.bypass, [list(range(8))],
                            wpp_own, wpp_all, w2_out_evs)

        rope_evs = []
        nq_evs = []
        for m in range(16):
            po = (m % 2) * 1024
            if m < 8:
                lhs_fn = lambda k, m=m: buf_B[:, m * 2048 + k * 128:m * 2048 + (k + 1) * 128]
                mm_evs = proj_mms(lhs_fn, pre_evs[m], po)
            else:
                wsl = m % 4
                evw = d("gpsimd", Wslot(wsl).rearrange("pp (k f) -> pp k f", f=128),
                        w_q_src(m), waits=[p.gw(("wslot", wsl))])
                mm_evs = proj_mms(
                    lambda k, wsl=wsl: Wslot(wsl)[:, k * 128:(k + 1) * 128], evw, po)
                p.gs(("wslot", wsl), mm_evs[-1])
            if is_a:
                ev_evac = p.op("vector", lambda v, m=m, po=po: v.tensor_scalar(
                    Ak(m), ps[:, po:po + 1024], bq_sb[:, m:m + 1], None,
                    op0=ALU.add),
                    waits=mm_evs + [p.gw(("abuf", m))] + EVI, inc="ve")
                p.ps_r(po, po + 1024, ev_evac)
            else:
                ev_e0 = p.op("vector", lambda v, m=m, po=po: v.tensor_copy(
                    Ak(m), ps[:, po:po + 1024]),
                    waits=mm_evs + [p.gw(("abuf", m))], inc="ve")
                p.ps_r(po, po + 1024, ev_e0)
                ev_evac = qk_ln_ip(Ak(m), 0, 1, ev_e0)
            ev_rope = rope_chunk(Ak(m), ev_evac, tab_ev)
            rope_evs.append(ev_rope)
            nq_evs.append(norm2max_chunk(Ak(m), ev_rope, m))
        p.gs("tabs", rope_evs[-1])
        tap("q_" + var, buf_A[:, :], (128, 16384), rope_evs, BF16)

        ev_mkload = d("gpsimd", vec16[0:1, 64:64 + 4 * n_kch],
                      kv_all.ap().rearrange("(r s) q -> s r q", s=SEG)[
                          NR:NR + 1, :, 0:n_kch],
                      waits=[ev_ag])
        ev_mk = p.op("vector", lambda v: v.reduce_max(
            vec16[0:1, 32:33], vec16[0:1, 64:64 + 4 * n_kch], axis=AX.X),
            waits=[ev_mkload], inc="ve")
        ev_b1 = p.op("vector", lambda v: v.tensor_scalar(
            vec16[0:1, 48:64], vec16[0:1, 0:16], vec16[0:1, 32:33], None,
            op0=ALU.mult), waits=[ev_mk] + nq_evs, inc="ve")
        ev_b2 = p.op("scalar", lambda s: s.activation(
            out=vec16[0:1, 0:16], in_=vec16[0:1, 48:64], func=AF.Sqrt,
            bias=eps_sb[0:1, 0:1]), waits=[ev_b1, ev_memset], inc="se")
        ev_b3 = p.op("vector", lambda v: v.tensor_scalar(
            vec16[0:1, 48:64], vec16[0:1, 0:16], -B_SAFETY * ISQ_D, None,
            op0=ALU.mult), waits=[ev_b2], inc="ve")
        ev_b4 = p.op("tensor", lambda t: t.matmul(
            ps[:, 3072:3088], or_f[0:1, :], vec16[0:1, 48:64],
            start=True, stop=True),
            waits=[ev_b3] + p.ps_w(3072, 3584), inc="pe")
        ev_negB = p.op("vector", lambda v: v.tensor_copy(
            negB_sb[:, :], ps[:, 3072:3088]), waits=[ev_b4], inc="ve")
        p.ps_r(3072, 3584, ev_negB)
        tap("negb_" + var, negB_sb[:, :], (128, 16), [ev_negB])
        p.barrier()

        z_evs = []
        k_view = kv_all.ap().rearrange("(r s) q -> s r q", s=SEG)
        if is_a:
            v_view = kv_all.ap().rearrange("(r s) q -> r s q", s=SEG)[
                :, VOFF:VOFF + VROWS, :].rearrange("r (t h) c -> t r (h c)", h=2)
        else:
            v_view = kv_all.ap().rearrange("(r s) q -> r s q", s=SEG)[
                :, VOFF:VOFF + VROWS, :].rearrange("r t (h c) -> (t h) r c", h=2)
        for h in range(16):
            kh = h if is_a else h // 4
            kv_evs = [None] * 8

            def emit_kv_load(kb4, kh=kh, kv_evs=kv_evs):
                j = kb4 % 2
                ksrc = k_view[kh * 128:(kh + 1) * 128, :, kb4 * 128:(kb4 + 1) * 128]
                vsrc = v_view[kb4 * 128:(kb4 + 1) * 128, :, kh * 128:(kh + 1) * 128]
                gwk = p.gw(("kpair", j))
                evk = d("sync", Kslot(j).rearrange("pp (r q) -> pp r q", q=128),
                        ksrc, waits=[gwk, ev_ag])
                evv = d("sync", Vslot(j).rearrange("pp (r f) -> pp r f", f=128),
                        vsrc, waits=[gwk, ev_ag])
                kv_evs[kb4] = (evk, evv)

            emit_kv_load(0)
            emit_kv_load(1)
            last_sums = None
            last_pv = None
            for kb in range(NKB):
                m0 = kb // 4
                dd = kb % 4
                j = m0 % 2
                so = 0 if kb % 2 == 0 else 2048
                live0 = m0 * 128
                regions = [(live0, 512), (512, 1024)] if live0 < 512 else [(live0, 1024)]
                evk, evv = kv_evs[m0]
                sc_evs = []
                for (r0, r1) in regions:
                    w = [evk] + p.ps_w(so + r0, so + r1)
                    sc_evs.append(p.op("tensor", lambda t, so=so, r0=r0, r1=r1, j=j, dd=dd, h=h: t.matmul(
                        ps[:, so + r0:so + r1], Kslot(j)[:, dd * 128:(dd + 1) * 128],
                        Ak(h)[:, r0:r1], start=True, stop=False),
                        waits=w, inc="pe"))
                ev_mask = p.op("tensor", lambda t, so=so, live0=live0, dd=dd: t.matmul(
                    ps[:, so + live0:so + live0 + 128], id_sb[:, :],
                    mask_sb[:, dd * 128:(dd + 1) * 128], start=False, stop=True),
                    waits=sc_evs + EVI, inc="pe")
                pslot = kb % 3
                exp_evs = []
                for (r0, r1) in regions:
                    exp_evs.append(p.op("scalar", lambda s, so=so, r0=r0, r1=r1, pslot=pslot, h=h: s.activation(
                        out=Sslot(pslot)[:, r0:r1], in_=ps[:, so + r0:so + r1],
                        func=AF.Exp, bias=negB_sb[:, h:h + 1], scale=ISQ_D),
                        waits=[ev_mask, ev_negB, p.gw(("sslot", pslot))], inc="se"))
                for (r0, r1) in regions:
                    p.ps_r(so + r0, so + r1, exp_evs[-1])
                for (r0, r1) in regions:
                    w = exp_evs + (p.ps_w(3072 + r0, 3072 + r1) if kb == 0 else [])
                    last_sums = p.op("tensor", lambda t, r0=r0, r1=r1, pslot=pslot, kb=kb: t.matmul(
                        ps[0:1, 3072 + r0:3072 + r1], oc_b[:, :],
                        Sslot(pslot)[:, r0:r1],
                        start=(kb == 0), stop=(kb == NKB - 1)), waits=w, inc="pe")
                for (r0, r1) in regions:
                    w = exp_evs + [evv] + (p.ps_w(1024 + r0, 1024 + r1) if kb == 0 else [])
                    last_pv = p.op("tensor", lambda t, r0=r0, r1=r1, pslot=pslot, j=j, dd=dd, kb=kb: t.matmul(
                        ps[:, 1024 + r0:1024 + r1], Vslot(j)[:, dd * 128:(dd + 1) * 128],
                        Sslot(pslot)[:, r0:r1],
                        start=(kb == 0), stop=(kb == NKB - 1)), waits=w, inc="pe")
                p.gs(("sslot", pslot), last_pv)
                if dd == 3:
                    p.gs(("kpair", j), last_pv)
                    if m0 + 2 < 8:
                        emit_kv_load(m0 + 2)
            ev_sc = p.op("scalar", lambda s: s.copy(rows[0:1, 0:1024],
                                                    ps[0:1, 3072:4096]),
                         waits=[last_sums, p.gw("rows")], inc="se")
            rb_evs = []
            for hh in range(2):
                rb_evs.append(p.op("tensor", lambda t, hh=hh: t.matmul(
                    ps[:, 2048 + hh * 512:2048 + (hh + 1) * 512], or_f[0:1, :],
                    rows[0:1, hh * 512:(hh + 1) * 512], start=True, stop=True),
                    waits=[ev_sc] + p.ps_w(2048, 3072), inc="pe"))
            p.gs("rows", rb_evs[-1])
            ev_ln = p.op("scalar", lambda s: s.activation(
                out=ps[:, 3072:4096], in_=ps[:, 2048:3072], func=AF.Ln),
                waits=rb_evs, inc="se")
            ev_rcp = p.op("scalar", lambda s: s.activation(
                out=ps[:, 2048:3072], in_=ps[:, 3072:4096], func=AF.Exp, scale=-1.0),
                waits=[ev_ln], inc="se")
            p.ps_r(3072, 4096, ev_rcp)
            ev_zc = p.op("vector", lambda v: v.tensor_copy(scr[:, :], ps[:, 1024:2048]),
                         waits=[last_pv, p.gw("scr")], inc="ve")
            p.ps_r(1024, 2048, ev_zc)
            if is_a:
                ev_zm = p.op("vector", lambda v: v.tensor_tensor(
                    scr[:, :], scr[:, :], ps[:, 2048:3072], op=ALU.mult),
                    waits=[ev_zc, ev_rcp], inc="ve")
                ev_zn = p.op("vector", lambda v, h=h: v.tensor_scalar(
                    Bk(h), scr[:, :], bv_sb[:, h:h + 1], None, op0=ALU.add),
                    waits=[ev_zm] + EVI, inc="ve")
            else:
                ev_zn = p.op("vector", lambda v, h=h: v.tensor_tensor(
                    Bk(h), scr[:, :], ps[:, 2048:3072], op=ALU.mult),
                    waits=[ev_zc, ev_rcp], inc="ve")
            p.gs("scr", ev_zn)
            p.ps_r(2048, 3072, ev_zn)
            z_evs.append(ev_zn)
        tap("z_" + var, buf_B[:, :], (128, 16384), z_evs, BF16)
        p.barrier()

        wo_ap = ap["wo_a"] if is_a else ap["wo_b"]
        for m in range(16):
            wsl = m % 4
            wsrc = wo_ap.rearrange("(k pp) (mm f) -> pp k mm f", pp=128, f=128)[:, :, m, :]
            evw = d("gpsimd", Wslot(wsl).rearrange("pp (k f) -> pp k f", f=128),
                    wsrc, waits=[p.gw(("wslot", wsl))])
            po = (m % 2) * 1024
            mm_evs = []
            for k in range(16):
                for hh in range(2):
                    w = [evw, z_evs[15]]
                    if k == 0:
                        w += p.ps_w(po + hh * 512, po + (hh + 1) * 512)
                    mm_evs.append(p.op("tensor", lambda t, k=k, hh=hh, po=po, wsl=wsl: t.matmul(
                        ps[:, po + hh * 512:po + (hh + 1) * 512],
                        Wslot(wsl)[:, k * 128:(k + 1) * 128],
                        Bk(k)[:, hh * 512:(hh + 1) * 512],
                        start=(k == 0), stop=(k == 15)), waits=w, inc="pe"))
            p.gs(("wslot", wsl), mm_evs[-1])
            if is_a:
                ev_add = p.op("vector", lambda v, m=m, po=po: v.scalar_tensor_tensor(
                    Rk(m), ps[:, po:po + 1024], bo_sb[:, m:m + 1], Rk(m),
                    op0=ALU.add, op1=ALU.add),
                    waits=mm_evs + [resid_ev[m]] + EVI, inc="ve")
            else:
                ev_add = p.op("vector", lambda v, m=m, po=po: v.tensor_tensor(
                    Rk(m), Rk(m), ps[:, po:po + 1024], op=ALU.add),
                    waits=mm_evs + [resid_ev[m]], inc="ve")
            p.ps_r(po, po + 1024, ev_add)
            resid_ev[m] = ev_add
        p.barrier()
        for reg in range(2):
            p.gs(("bufB", reg), resid_ev[15])

    def ffn_sublayer():
        emit_layernorm(True)
        p.barrier()
        xp_evs = []
        for a2 in range(16):
            po = (a2 % 2) * 1024
            mm_evs = []
            for a in range(16):
                hsb = hp_sb if H16[a2, a] > 0 else hm_sb
                for hh in range(2):
                    w = [ln_ev[NCH - 1]] + EVI
                    if a == 0:
                        w += p.ps_w(po + hh * 512, po + (hh + 1) * 512)
                    mm_evs.append(p.op("tensor", lambda t, a=a, hh=hh, po=po, hsb=hsb: t.matmul(
                        ps[:, po + hh * 512:po + (hh + 1) * 512], hsb[:, :],
                        Lk(a)[:, hh * 512:(hh + 1) * 512],
                        start=(a == 0), stop=(a == 15)), waits=w, inc="pe"))
            ev_e = p.op("vector", lambda v, a2=a2, po=po: v.tensor_scalar(
                Bk(a2), ps[:, po:po + 1024], HAD_N1, None, op0=ALU.mult),
                waits=mm_evs + [p.gw(("bbuf", a2))], inc="ve")
            p.ps_r(po, po + 1024, ev_e)
            xp_evs.append(ev_e)
        tap("xp", buf_B[:, :], (128, 16384), xp_evs, BF16)
        p.barrier()

        for q in range(4):
            h_evs = []
            for icl in range(16):
                I = q * 16 + icl
                upsrc = ap["w_upT"].rearrange(
                    "(k pp) (i f) -> pp k i f", pp=128, f=128)[:, :, I, :]
                gtsrc = ap["w_gateT"].rearrange(
                    "(k pp) (i f) -> pp k i f", pp=128, f=128)[:, :, I, :]
                su_ = icl % 2
                sg_ = 2 + icl % 2
                evwu = d("gpsimd", Wslot(su_).rearrange("pp (k f) -> pp k f", f=128),
                         upsrc, waits=[p.gw(("wslot", su_))])
                evwg = d("gpsimd", Wslot(sg_).rearrange("pp (k f) -> pp k f", f=128),
                         gtsrc, waits=[p.gw(("wslot", sg_))])
                po_u = (icl % 2) * 1024
                po_g = 2048 + (icl % 2) * 1024
                mmu, mmg = [], []
                for k in range(16):
                    for hh in range(2):
                        wu = [evwu, xp_evs[15]]
                        wg = [evwg, xp_evs[15]]
                        if k == 0:
                            wu += p.ps_w(po_u + hh * 512, po_u + (hh + 1) * 512)
                            wg += p.ps_w(po_g + hh * 512, po_g + (hh + 1) * 512)
                        mmu.append(p.op("tensor", lambda t, k=k, hh=hh, po=po_u, su_=su_: t.matmul(
                            ps[:, po + hh * 512:po + (hh + 1) * 512],
                            Wslot(su_)[:, k * 128:(k + 1) * 128],
                            Bk(k)[:, hh * 512:(hh + 1) * 512],
                            start=(k == 0), stop=(k == 15)), waits=wu, inc="pe"))
                        mmg.append(p.op("tensor", lambda t, k=k, hh=hh, po=po_g, sg_=sg_: t.matmul(
                            ps[:, po + hh * 512:po + (hh + 1) * 512],
                            Wslot(sg_)[:, k * 128:(k + 1) * 128],
                            Bk(k)[:, hh * 512:(hh + 1) * 512],
                            start=(k == 0), stop=(k == 15)), waits=wg, inc="pe"))
                p.gs(("wslot", su_), mmu[-1])
                p.gs(("wslot", sg_), mmg[-1])
                ssl = icl % 3
                ev_silu = p.op("scalar", lambda s, po=po_g, I=I, ssl=ssl: s.activation(
                    out=Sslot(ssl), in_=ps[:, po:po + 1024], func=AF.Silu,
                    scale=wsg_sb[:, I:I + 1]),
                    waits=mmg + [p.gw(("sslot", ssl))] + EVI, inc="se")
                ev_h = p.op("vector", lambda v, po=po_u, I=I, icl=icl, ssl=ssl: v.scalar_tensor_tensor(
                    Ak(icl), ps[:, po:po + 1024], wsu_sb[:, I:I + 1], Sslot(ssl),
                    op0=ALU.mult, op1=ALU.mult),
                    waits=[ev_silu, p.gw(("abuf", icl))] + mmu + EVI, inc="ve")
                p.ps_r(po_u, po_u + 1024, ev_h)
                p.ps_r(po_g, po_g + 1024, ev_h)
                p.gs(("sslot", ssl), ev_h)
                h_evs.append(ev_h)
            for m in range(16):
                wsl = m % 4
                s_rank = m // 2
                wsrc = wpp_all.ap()[s_rank * 8192 + q * 2048:s_rank * 8192 + (q + 1) * 2048,
                                    (m % 2) * 128:(m % 2) * 128 + 128]
                wsrc = wsrc.rearrange("(i pp) f -> pp i f", pp=128)
                evw = d("sync", Wslot(wsl).rearrange("pp (i f) -> pp i f", f=128),
                        wsrc, waits=[p.gw(("wslot", wsl)), W2["ev"]])
                po = (m % 2) * 1024
                mm_evs = []
                for icl in range(16):
                    for hh in range(2):
                        w = [evw, h_evs[15]]
                        if icl == 0:
                            w += p.ps_w(po + hh * 512, po + (hh + 1) * 512)
                        mm_evs.append(p.op("tensor", lambda t, icl=icl, hh=hh, po=po, wsl=wsl: t.matmul(
                            ps[:, po + hh * 512:po + (hh + 1) * 512],
                            Wslot(wsl)[:, icl * 128:(icl + 1) * 128],
                            Ak(icl)[:, hh * 512:(hh + 1) * 512],
                            start=(icl == 0), stop=(icl == 15)), waits=w, inc="pe"))
                p.gs(("wslot", wsl), mm_evs[-1])
                ev_add = p.op("vector", lambda v, m=m, po=po: v.scalar_tensor_tensor(
                    Rk(m), ps[:, po:po + 1024], wsdp_sb[:, m:m + 1], Rk(m),
                    op0=ALU.mult, op1=ALU.add),
                    waits=mm_evs + [resid_ev[m]] + EVI, inc="ve")
                p.ps_r(po, po + 1024, ev_add)
                resid_ev[m] = ev_add
            for icl in range(16):
                p.gs(("abuf", icl), resid_ev[15])
        for reg in range(2):
            p.gs(("bufB", reg), resid_ev[15])
        p.barrier()

    attn_sublayer("a")
    tap("r_a", buf_R[:, :], (128, 16384), list(resid_ev))
    ffn_sublayer()
    tap("r_f1", buf_R[:, :], (128, 16384), list(resid_ev))
    attn_sublayer("b")
    tap("r_b", buf_R[:, :], (128, 16384), list(resid_ev))
    ffn_sublayer()

    st_evs = [d("sync", out_x[k * 128:(k + 1) * 128, :], Rk(k),
                waits=[resid_ev[k]]) for k in range(NCH)]
    p.op("gpsimd", lambda g: g.engine_nop(), waits=st_evs)

    with nc.Block() as block:
        OWN = {} if DBG_KEEP_SELF else {"tensor": "pe"}

        def emitter(name):
            own = OWN.get(name)
            def run(eng):
                seen = {}
                for fn, ws, inc, amt in p.ops[name]:
                    for s, v in sorted(ws.items()):
                        if s == own:
                            continue
                        if seen.get(s, 0) < v:
                            eng.wait_ge(sems[s], v)
                            seen[s] = v
                    inst = fn(eng)
                    if inc is not None:
                        inst.then_inc(sems[inc], amt)
            return run
        block.tensor(emitter("tensor"))
        block.vector(emitter("vector"))
        block.scalar(emitter("scalar"))
        block.gpsimd(emitter("gpsimd"))
        block.sync(emitter("sync"))

    es.close()
    return nc, dbg


_cached = {}


def kernel(**inputs):
    maps = _prep_inputs(inputs)
    key = tuple(sorted(_TAPS))
    if key not in _cached:
        _cached[key] = build_program()
    nc, dbg = _cached[key]
    res = run_bass_kernel_spmd(nc, maps, core_ids=list(range(8)))
    out = _assemble(res.results)
    if dbg:
        kernel._dbg = [{t: np.asarray(r["dbg_" + t]) for t in dbg} for r in res.results]
    return out


kernel._dbg = None


# version 08
# speedup vs baseline: 1.0042x; 1.0042x over previous
import numpy as np
import ml_dtypes
from contextlib import ExitStack

import concourse.bass as bass
import concourse.mybir as mybir
from concourse.bass_utils import run_bass_kernel_spmd

F32 = mybir.dt.float32
BF16 = mybir.dt.bfloat16
AF = mybir.ActivationFunctionType
ALU = mybir.AluOpType
AX = mybir.AxisListType
NPBF = ml_dtypes.bfloat16

NCH = 16
NQT = 8
NKB = 32
ISQ_D = float(1.0 / np.sqrt(128.0))
HAD_N1 = float(1.0 / np.sqrt(2048.0))
HAD_N2 = float(1.0 / np.sqrt(8192.0))
EPS = 1e-5
B_SAFETY = 1.05
NEG_BIG = -30000.0

_TAPS = set()
import os as _os
DBG_NO_W2 = bool(int(_os.environ.get("K_DBG_NO_W2", "0")))
DBG_NO_ROPE = bool(int(_os.environ.get("K_DBG_NO_ROPE", "0")))
DBG_NO_COLL = bool(int(_os.environ.get("K_DBG_NO_COLL", "0")))
DBG_KEEP_SELF = bool(int(_os.environ.get("K_DBG_KEEP_SELF", "0")))

ENGS = ("tensor", "vector", "scalar", "gpsimd", "sync")
SEMS = ("dg", "ds", "dc", "pe", "ve", "se", "cc")
DMA_SEMS = ("dg", "ds", "dc")


class Prog:
    def __init__(self):
        self.ops = {e: [] for e in ENGS}
        self.plan = {s: 0 for s in SEMS}
        self.pending = {e: {} for e in ENGS}
        self.guards = {}

    def op(self, eng, fn, waits=(), inc=None, amt=None):
        ws = dict(self.pending[eng])
        self.pending[eng] = {}
        for w in waits:
            if w is None:
                continue
            s, v = w
            ws[s] = max(ws.get(s, 0), v)
        ev = None
        if inc is not None:
            if amt is None:
                amt = 16 if inc in DMA_SEMS else 1
            self.plan[inc] += amt
            ev = (inc, self.plan[inc])
        self.ops[eng].append((fn, ws, inc, amt))
        return ev

    def barrier(self):
        evs = [(s, self.plan[s]) for s in SEMS if self.plan[s] > 0]
        for e in ("tensor", "vector", "scalar"):
            dd = self.pending[e]
            for s, v in evs:
                dd[s] = max(dd.get(s, 0), v)

    def gw(self, key):
        return self.guards.get(key)

    def gs(self, key, ev):
        self.guards[key] = ev

    def ps_w(self, lo, hi):
        return [self.gw(("psb", b)) for b in range(lo // 512, (hi + 511) // 512)]

    def ps_r(self, lo, hi, ev):
        for b in range(lo // 512, (hi + 511) // 512):
            self.gs(("psb", b), ev)



def _rope_tables(positions, base):
    inv = 1.0 / base ** (np.arange(0, 128, 2, dtype=np.float64) / 128.0)
    ang = np.outer(inv, positions.astype(np.float64))
    cos = np.concatenate([np.cos(ang), np.cos(ang)], axis=0)
    sin = np.concatenate([np.sin(ang), np.sin(ang)], axis=0)
    return cos.astype(NPBF), sin.astype(NPBF)


def _mask_tile(r):
    kk = np.arange(128)[:, None]
    qi = np.arange(128)[None, :]
    tiles = []
    for dd in range(4):
        allowed = (128 * dd + kk) <= (128 * r + qi)
        tiles.append(np.where(allowed, 0.0, NEG_BIG))
    return np.concatenate(tiles, axis=1).astype(NPBF)


def _prep_inputs(inp):
    asc = np.ascontiguousarray
    x = np.asarray(inp["L0_attn_a_normalized_resid_pre"], dtype=np.float32)
    perm = np.concatenate([np.arange(0, 128, 2), np.arange(1, 128, 2)])

    wq = np.asarray(inp["L0_attn_a_W_Q"], np.float32)[:, :, perm]
    wk = np.asarray(inp["L0_attn_a_W_K"], np.float32)[:, :, perm]
    wv = np.asarray(inp["L0_attn_a_W_V"], np.float32)
    shared = {
        "wq_a": asc(wq.transpose(1, 0, 2).reshape(2048, 2048)),
        "wk_a": asc(wk.transpose(1, 0, 2).reshape(2048, 2048)),
        "wv_a": asc(wv.transpose(1, 0, 2).reshape(2048, 2048)),
        "wo_a": asc(np.asarray(inp["L0_attn_a_W_O"], np.float32).reshape(2048, 2048)),
        "bq_a": asc(np.asarray(inp["L0_attn_a_b_Q"], np.float32)[:, perm].T),
        "bk_a": asc(np.asarray(inp["L0_attn_a_b_K"], np.float32)[:, perm].T),
        "bv_col": asc(np.asarray(inp["L0_attn_a_b_V"], np.float32).reshape(16, 128).T),
        "bo_col": asc(np.asarray(inp["L0_attn_a_b_O"], np.float32).reshape(16, 128).T),
        "wqkv_b": asc(np.asarray(inp["L0_attn_b_w_qkv"], np.float32).T),
        "wo_b": asc(np.asarray(inp["L0_attn_b_w_o"], np.float32).T),
        "qng": asc(np.asarray(inp["L0_attn_b_qn_g"], np.float32).reshape(128, 1)),
        "qnb": asc(np.asarray(inp["L0_attn_b_qn_b"], np.float32).reshape(128, 1)),
        "kng": asc(np.asarray(inp["L0_attn_b_kn_g"], np.float32).reshape(128, 1)),
        "knb": asc(np.asarray(inp["L0_attn_b_kn_b"], np.float32).reshape(128, 1)),
        "w_upT": asc(np.asarray(inp["L0_ffn_W_up"], np.float32).T),
        "w_gateT": asc(np.asarray(inp["L0_ffn_W_gate"], np.float32).T),
        "su_ug": asc(np.asarray(inp["L0_ffn_SU_ug"], np.float32).reshape(16, 128).T),
        "su_dp_re": asc(np.asarray(inp["L0_ffn_SU_dp"], np.float32).reshape(64, 128)),
        "wsu": asc(np.asarray(inp["L0_ffn_Wscale_ug"], np.float32)[:8192].reshape(64, 128).T),
        "wsg": asc(np.asarray(inp["L0_ffn_Wscale_ug"], np.float32)[8192:].reshape(64, 128).T),
        "wsdp": asc(np.asarray(inp["L0_ffn_Wscale_dp"], np.float32).reshape(16, 128).T),
    }
    w_downT = np.asarray(inp["L0_ffn_W_down"], np.float32).T
    maps = []
    for c in range(8):
        g, r = divmod(c, 4)
        blocks = [4 * m + r for m in range(8)]
        pos = np.concatenate([np.arange(128) + 128 * b for b in blocks])
        xc = x[g].reshape(32, 128, 2048)[blocks].reshape(1024, 2048)
        cos_a, sin_a = _rope_tables(pos, 10000.0)
        cos_b, sin_b = _rope_tables(pos, 1000000.0)
        m = dict(shared)
        m["xT"] = asc(xc.T)
        m["w_downT_sl"] = asc(w_downT[:, c * 256:(c + 1) * 256])
        m["cos_a"] = asc(cos_a); m["sin_a"] = asc(sin_a)
        m["cos_b"] = asc(cos_b); m["sin_b"] = asc(sin_b)
        m["maskT"] = _mask_tile(r)
        maps.append(m)
    return maps


def _assemble(results):
    out = np.empty((2, 4096, 2048), np.float32)
    for c in range(8):
        g, r = divmod(c, 4)
        xc = np.asarray(results[c]["out_x"]).T.reshape(8, 128, 2048)
        for mm in range(8):
            b = 4 * mm + r
            out[g, b * 128:(b + 1) * 128, :] = xc[mm]
    return out



def build_program():
    nc = bass.Bass("TRN2", target_bir_lowering=False, debug=False)
    p = Prog()

    ap = {}
    def ein(name, shape, dt=F32):
        ap[name] = nc.dram_tensor(name, list(shape), dt, kind="ExternalInput").ap()
    ein("xT", (2048, 1024))
    for n in ("wq_a", "wk_a", "wv_a", "wo_a"):
        ein(n, (2048, 2048))
    ein("bq_a", (128, 16)); ein("bk_a", (128, 16))
    ein("bv_col", (128, 16)); ein("bo_col", (128, 16))
    ein("wqkv_b", (2048, 3072)); ein("wo_b", (2048, 2048))
    for n in ("qng", "qnb", "kng", "knb"):
        ein(n, (128, 1))
    ein("w_upT", (2048, 8192)); ein("w_gateT", (2048, 8192))
    ein("w_downT_sl", (8192, 256))
    ein("su_ug", (128, 16)); ein("su_dp_re", (64, 128))
    ein("wsu", (128, 64)); ein("wsg", (128, 64)); ein("wsdp", (128, 16))
    for n in ("cos_a", "sin_a", "cos_b", "sin_b"):
        ein(n, (128, 1024), BF16)
    ein("maskT", (128, 512), BF16)

    out_x = nc.dram_tensor("out_x", [2048, 1024], F32, kind="ExternalOutput").ap()

    kvA_own = nc.dram_tensor("kvA_own", [4224, 1024], BF16, kind="Internal")
    kA_all = nc.dram_tensor("kA_all", [4 * 2176, 1024], BF16, kind="Internal")
    vA_all = nc.dram_tensor("vA_all", [4 * 2048, 1024], BF16, kind="Internal")
    kvB_own = nc.dram_tensor("kvB_own", [1152, 1024], BF16, kind="Internal")
    kvB_all = nc.dram_tensor("kvB_all", [4 * 1152, 1024], BF16, kind="Internal")
    vd = nc.dram_tensor("vd", [8192, 256], BF16, kind="Internal")
    wpp_own = nc.dram_tensor("wpp_own", [8192, 256], BF16, kind="Internal")
    wpp_all = nc.dram_tensor("wpp_all", [8 * 8192, 256], BF16, kind="Internal")

    ic = lambda a, n: nc.inline_tensor(np.ascontiguousarray(a), name=n).ap()
    ones_col_f = ic(np.ones((128, 1), np.float32), "c_ocf")
    ones_col_b = ic(np.ones((128, 1), NPBF), "c_ocb")
    ones_row_f = ic(np.ones((1, 128), np.float32), "c_orf")
    ident_b = ic(np.eye(128, dtype=NPBF), "c_id")
    rotT_np = np.zeros((128, 128), np.float32)
    for i in range(64):
        rotT_np[i, 64 + i] = 1.0
        rotT_np[64 + i, i] = -1.0
    rotT_c = ic(rotT_np.astype(NPBF), "c_rotT")

    def _hmat(n):
        h = np.array([[1.0]])
        while h.shape[0] < n:
            h = np.block([[h, h], [h, -h]])
        return h
    H16 = _hmat(16)
    Hp_c = ic(_hmat(128).astype(NPBF), "c_hp")
    Hm_c = ic((-_hmat(128)).astype(NPBF), "c_hm")
    H64_c = ic(_hmat(64).astype(NPBF), "c_h64")

    dbg = {}
    es = ExitStack()
    ec = es.enter_context

    buf_R = ec(nc.sbuf_tensor("buf_R", [128, 16384], F32))
    buf_L = ec(nc.sbuf_tensor("buf_L", [128, 16384], BF16))
    buf_A = ec(nc.sbuf_tensor("buf_A", [128, 16384], BF16))
    buf_B = ec(nc.sbuf_tensor("buf_B", [128, 16384], BF16))
    buf_W = ec(nc.sbuf_tensor("buf_W", [128, 8192], BF16))
    buf_S = ec(nc.sbuf_tensor("buf_S", [128, 3072], BF16))
    buf_K = ec(nc.sbuf_tensor("buf_K", [128, 2048], BF16))
    scr = ec(nc.sbuf_tensor("scr", [128, 1024], F32))
    rows = ec(nc.sbuf_tensor("rows", [1, 2048], F32))
    tabs = ec(nc.sbuf_tensor("tabs", [128, 2048], BF16))
    mask_sb = ec(nc.sbuf_tensor("mask_sb", [128, 512], BF16))
    bo_sb = ec(nc.sbuf_tensor("bo_sb", [128, 16], F32))
    bv_sb = ec(nc.sbuf_tensor("bv_sb", [128, 16], F32))
    vec16 = ec(nc.sbuf_tensor("vec16", [1, 128], F32))
    bq_sb = ec(nc.sbuf_tensor("bq_sb", [128, 16], F32))
    bk_sb = ec(nc.sbuf_tensor("bk_sb", [128, 16], F32))
    qn_sb = ec(nc.sbuf_tensor("qn_sb", [128, 4], F32))
    su_sb = ec(nc.sbuf_tensor("su_sb", [128, 16], F32))
    sudp_sb = ec(nc.sbuf_tensor("sudp_sb", [64, 128], F32))
    wsu_sb = ec(nc.sbuf_tensor("wsu_sb", [128, 64], F32))
    wsg_sb = ec(nc.sbuf_tensor("wsg_sb", [128, 64], F32))
    wsdp_sb = ec(nc.sbuf_tensor("wsdp_sb", [128, 16], F32))
    negB_sb = ec(nc.sbuf_tensor("negB_sb", [128, 16], F32))
    eps_sb = ec(nc.sbuf_tensor("eps_sb", [1, 1], F32))
    oc_f = ec(nc.sbuf_tensor("oc_f", [128, 1], F32))
    oc_b = ec(nc.sbuf_tensor("oc_b", [128, 1], BF16))
    or_f = ec(nc.sbuf_tensor("or_f", [1, 128], F32))
    id_sb = ec(nc.sbuf_tensor("id_sb", [128, 128], BF16))
    rot_sb = ec(nc.sbuf_tensor("rot_sb", [128, 128], BF16))
    hp_sb = ec(nc.sbuf_tensor("hp_sb", [128, 128], BF16))
    hm_sb = ec(nc.sbuf_tensor("hm_sb", [128, 128], BF16))
    h64_sb = ec(nc.sbuf_tensor("h64_sb", [64, 64], BF16))

    ps = ec(nc.psum_tensor("ps", [128, 4096], F32))
    sems = {s: ec(nc.semaphore("sem_" + s)) for s in SEMS}

    def Rk(k): return buf_R[:, k * 1024:(k + 1) * 1024]
    def Lk(k): return buf_L[:, k * 1024:(k + 1) * 1024]
    def Ak(k): return buf_A[:, k * 1024:(k + 1) * 1024]
    def Bk(k): return buf_B[:, k * 1024:(k + 1) * 1024]
    def Wslot(i): return buf_W[:, i * 2048:(i + 1) * 2048]
    def Sslot(i): return buf_S[:, i * 1024:(i + 1) * 1024]
    def Kslot(j): return buf_K[:, j * 1024:j * 1024 + 512]
    def Vslot(j): return buf_K[:, j * 1024 + 512:(j + 1) * 1024]

    resid_ev = [None] * NCH
    ln_ev = [None] * NCH

    def d(eng, out, in_, waits=()):
        sem = {"gpsimd": "dg", "sync": "ds", "scalar": "dc"}[eng]
        w = list(waits) + [p.guards.get("coll_last")]
        return p.op(eng, lambda e, o=out, i=in_: e.dma_start(out=o, in_=i),
                    waits=w, inc=sem)

    def coll(kind, op, groups, in_t, out_t, waits, in_ap=None, out_ap=None):
        ia = in_ap if in_ap is not None else in_t.ap()
        oa = out_ap if out_ap is not None else out_t.ap()
        if DBG_NO_COLL:
            n = oa.shape[0] // ia.shape[0]
            rows_ = ia.shape[0]
            ev = None
            for i in range(n):
                ev = d("gpsimd", oa[i * rows_:(i + 1) * rows_, :], ia,
                       waits=list(waits))
            return p.op("gpsimd", lambda g: g.engine_nop(), waits=[ev], inc="cc")
        w = list(waits) + [(s, p.plan[s]) for s in DMA_SEMS if p.plan[s] > 0]
        ev = p.op("gpsimd", lambda g, k=kind, o=op, gr=groups, ia=ia, oa=oa:
                  g.collective_compute(k, o, replica_groups=gr,
                                       ins=[ia], outs=[oa]),
                  waits=w, inc="cc")
        p.gs("coll_last", ev)
        return ev

    def tap(tag, src_ap, shape, waits, dt=F32):
        if tag not in _TAPS or tag in dbg:
            return
        t = nc.dram_tensor("dbg_" + tag, list(shape), dt, kind="ExternalOutput").ap()
        dbg[tag] = True
        d("sync", t, src_ap, waits=list(waits))

    ev_init = []
    for dst, src in ((oc_f[:, :], ones_col_f), (oc_b[:, :], ones_col_b),
                     (or_f[:, :], ones_row_f),
                     (id_sb[:, :], ident_b), (rot_sb[:, :], rotT_c),
                     (hp_sb[:, :], Hp_c), (hm_sb[:, :], Hm_c),
                     (h64_sb[:, :], H64_c),
                     (bq_sb[:, :], ap["bq_a"]), (bk_sb[:, :], ap["bk_a"]),
                     (qn_sb[:, 0:1], ap["qng"]), (qn_sb[:, 1:2], ap["qnb"]),
                     (qn_sb[:, 2:3], ap["kng"]), (qn_sb[:, 3:4], ap["knb"]),
                     (su_sb[:, :], ap["su_ug"]), (sudp_sb[:, :], ap["su_dp_re"]),
                     (wsu_sb[:, :], ap["wsu"]), (wsg_sb[:, :], ap["wsg"]),
                     (wsdp_sb[:, :], ap["wsdp"]), (mask_sb[:, :], ap["maskT"]),
                     (bo_sb[:, :], ap["bo_col"]), (bv_sb[:, :], ap["bv_col"])):
        ev_init.append(d("scalar", dst, src))
    ev_memset = p.op("vector", lambda v: v.memset(eps_sb[:, :], EPS), inc="ve")
    ev_init.append(ev_memset)
    EVI = list(ev_init)

    for k in range(NCH):
        resid_ev[k] = d("sync", Rk(k), ap["xT"][k * 128:(k + 1) * 128, :])

    def w2_precompute():
        ev_st1 = []
        for bi in range(8):
            half = bi % 2
            src = ap["w_downT_sl"].rearrange("(b a pp) r -> b pp a r", b=8, pp=128)[bi]
            dstA = buf_A[:, half * 2048:(half + 1) * 2048]
            evw = d("gpsimd", dstA.rearrange("pp (a r) -> pp a r", r=256), src,
                    waits=[p.gw(("w2in", half))] + EVI)
            evac_evs = []
            for a8 in range(8):
                lo = (a8 % 4) * 256 + (a8 // 4) * 1024
                evm = p.op("tensor", lambda t, h=half, a=a8, lo=lo: t.matmul(
                    ps[:, lo:lo + 256], hp_sb[:, :],
                    buf_A[:, h * 2048 + a * 256:h * 2048 + (a + 1) * 256],
                    start=True, stop=True),
                    waits=[evw] + p.ps_w(lo, lo + 256), inc="pe")
                eve = p.op("vector", lambda v, h=half, a=a8, lo=lo: v.tensor_scalar(
                    buf_A[:, 4096 + h * 2048 + a * 256:4096 + h * 2048 + (a + 1) * 256],
                    ps[:, lo:lo + 256], HAD_N2, None, op0=ALU.mult),
                    waits=[evm, p.gw(("w2out", half))], inc="ve")
                p.ps_r(lo, lo + 256, eve)
                evac_evs.append(eve)
            dstv = vd.ap().rearrange("(b a pp) r -> b pp a r", b=8, pp=128)[bi]
            srcv = buf_A[:, 4096 + half * 2048:4096 + (half + 1) * 2048]
            evd = d("sync", dstv, srcv.rearrange("pp (a r) -> pp a r", r=256),
                    waits=evac_evs)
            p.gs(("w2in", half), evd)
            p.gs(("w2out", half), evd)
            ev_st1.append(evd)
        ev_out = []
        for pb in range(32):
            half = pb % 2
            src = vd.ap().rearrange("(a pp) r -> pp a r", pp=128)[pb * 4:(pb + 1) * 4]
            src = src.rearrange("q a r -> a q r")
            dst = buf_A[0:64, 8192 + half * 1024:8192 + (half + 1) * 1024]
            evw = d("sync", dst.rearrange("a (q r) -> a q r", r=256), src,
                    waits=[p.gw(("w2in2", half))] + ev_st1)
            lo = 2048 + half * 1024
            mevs = []
            for hh in range(2):
                mevs.append(p.op("tensor", lambda t, h=half, hh=hh, lo=lo: t.matmul(
                    ps[0:64, lo + hh * 512:lo + (hh + 1) * 512], h64_sb[:, :],
                    buf_A[0:64, 8192 + h * 1024 + hh * 512:8192 + h * 1024 + (hh + 1) * 512],
                    start=True, stop=True),
                    waits=[evw] + p.ps_w(lo, lo + 1024) + EVI, inc="pe"))
            evs = []
            for q in range(4):
                P = pb * 4 + q
                eve = p.op("vector", lambda v, h=half, q=q, P=P, lo=lo: v.tensor_scalar(
                    buf_A[0:64, 12288 + h * 1024 + q * 256:12288 + h * 1024 + (q + 1) * 256],
                    ps[0:64, lo + q * 256:lo + (q + 1) * 256],
                    sudp_sb[:, P:P + 1], None, op0=ALU.mult),
                    waits=mevs + [p.gw(("w2out2", half))] + EVI, inc="ve")
                evs.append(eve)
            p.ps_r(lo, lo + 1024, evs[-1])
            for q in range(4):
                P = pb * 4 + q
                dstw = wpp_own.ap().rearrange("(a pp) r -> pp a r", pp=128)[P]
                evd = d("sync", dstw,
                        buf_A[0:64, 12288 + half * 1024 + q * 256:12288 + half * 1024 + (q + 1) * 256].rearrange(
                            "a (o r) -> a o r", o=1)[:, 0, :] if False else
                        buf_A[0:64, 12288 + half * 1024 + q * 256:12288 + half * 1024 + (q + 1) * 256],
                        waits=evs)
                ev_out.append(evd)
            p.gs(("w2in2", half), ev_out[-1])
            p.gs(("w2out2", half), ev_out[-1])
        return ev_out

    w2_out_evs = [] if DBG_NO_W2 else w2_precompute()
    W2 = {"ev": None}

    def emit_layernorm(su_fold):
        mm_last = None
        for k in range(NCH):
            sl = k % 3
            evq = p.op("scalar", lambda s, k=k, sl=sl: s.square(Sslot(sl), Rk(k)),
                       waits=[resid_ev[k], p.gw(("sslot", sl))], inc="se")
            for hh in range(2):
                w1 = [resid_ev[k]] + (p.ps_w(2048, 3072) if k == 0 else [])
                p.op("tensor", lambda t, k=k, hh=hh: t.matmul(
                    ps[0:1, 2048 + hh * 512:2048 + (hh + 1) * 512], oc_f[:, :],
                    Rk(k)[:, hh * 512:(hh + 1) * 512],
                    start=(k == 0), stop=(k == NCH - 1)), waits=w1, inc="pe")
                w2 = [evq] + (p.ps_w(3072, 4096) if k == 0 else [])
                mm_last = p.op("tensor", lambda t, k=k, hh=hh, sl=sl: t.matmul(
                    ps[0:1, 3072 + hh * 512:3072 + (hh + 1) * 512], oc_b[:, :],
                    Sslot(sl)[:, hh * 512:(hh + 1) * 512],
                    start=(k == 0), stop=(k == NCH - 1)), waits=w2, inc="pe")
            p.gs(("sslot", sl), mm_last)
        ev_mu = p.op("vector", lambda v: v.tensor_scalar(
            rows[0:1, 0:1024], ps[0:1, 2048:3072], 1.0 / 2048.0, None, op0=ALU.mult),
            waits=[mm_last, p.gw("rows")], inc="ve")
        ev_var = p.op("vector", lambda v: v.tensor_scalar(
            rows[0:1, 1024:2048], ps[0:1, 3072:4096], 1.0 / 2048.0, None, op0=ALU.mult),
            waits=[mm_last], inc="ve")
        p.ps_r(2048, 4096, ev_var)
        ev_t = p.op("vector", lambda v: v.tensor_mul(
            scr[0:1, 0:1024], rows[0:1, 0:1024], rows[0:1, 0:1024]),
            waits=[ev_mu, p.gw("scr")], inc="ve")
        ev_var2 = p.op("vector", lambda v: v.tensor_sub(
            rows[0:1, 1024:2048], rows[0:1, 1024:2048], scr[0:1, 0:1024]),
            waits=[ev_var, ev_t], inc="ve")
        ev_sd = p.op("scalar", lambda s: s.activation(
            out=scr[0:1, 0:1024], in_=rows[0:1, 1024:2048], func=AF.Sqrt,
            bias=eps_sb[0:1, 0:1]),
            waits=[ev_var2, ev_memset], inc="se")
        ev_rstd = p.op("vector", lambda v: v.reciprocal(
            rows[0:1, 1024:2048], scr[0:1, 0:1024]), waits=[ev_sd], inc="ve")
        bevs = []
        for hh in range(2):
            bevs.append(p.op("tensor", lambda t, hh=hh: t.matmul(
                ps[:, hh * 512:(hh + 1) * 512], or_f[0:1, :],
                rows[0:1, hh * 512:(hh + 1) * 512], start=True, stop=True),
                waits=[ev_mu] + p.ps_w(hh * 512, (hh + 1) * 512), inc="pe"))
            bevs.append(p.op("tensor", lambda t, hh=hh: t.matmul(
                ps[:, 1024 + hh * 512:1024 + (hh + 1) * 512], or_f[0:1, :],
                rows[0:1, 1024 + hh * 512:1024 + (hh + 1) * 512], start=True, stop=True),
                waits=[ev_rstd] + p.ps_w(1024 + hh * 512, 1024 + (hh + 1) * 512),
                inc="pe"))
        p.gs("rows", bevs[-1])
        for k in range(NCH):
            ev1 = p.op("vector", lambda v, k=k: v.tensor_sub(
                scr[:, :], Rk(k), ps[:, 0:1024]),
                waits=bevs + [resid_ev[k], p.gw("scr")], inc="ve")
            if su_fold:
                ev2 = p.op("vector", lambda v, k=k: v.scalar_tensor_tensor(
                    Lk(k), scr[:, :], su_sb[:, k:k + 1], ps[:, 1024:2048],
                    op0=ALU.mult, op1=ALU.mult),
                    waits=[ev1] + EVI, inc="ve")
            else:
                ev2 = p.op("vector", lambda v, k=k: v.tensor_tensor(
                    Lk(k), scr[:, :], ps[:, 1024:2048], op=ALU.mult),
                    waits=[ev1], inc="ve")
            p.gs("scr", ev2)
            ln_ev[k] = ev2
        p.ps_r(0, 2048, ln_ev[NCH - 1])

    def rope_chunk(dst_ap, src_ev, tab_ev):
        if DBG_NO_ROPE:
            return src_ev
        lo = 2048
        rot_evs = []
        for hh in range(2):
            rot_evs.append(p.op("tensor", lambda t, hh=hh: t.matmul(
                ps[:, lo + hh * 512:lo + (hh + 1) * 512], rot_sb[:, :],
                dst_ap[:, hh * 512:(hh + 1) * 512], start=True, stop=True),
                waits=[src_ev] + p.ps_w(lo, lo + 1024) + EVI, inc="pe"))
        ev1 = p.op("vector", lambda v: v.tensor_tensor(
            Sslot(2), ps[:, lo:lo + 1024], tabs[:, 1024:2048], op=ALU.mult),
            waits=rot_evs + [p.gw(("sslot", 2))] + tab_ev, inc="ve")
        p.ps_r(lo, lo + 1024, ev1)
        ev2 = p.op("vector", lambda v: v.tensor_tensor(
            dst_ap, dst_ap, tabs[:, 0:1024], op=ALU.mult),
            waits=rot_evs + tab_ev, inc="ve")
        ev3 = p.op("vector", lambda v: v.tensor_add(dst_ap, dst_ap, Sslot(2)),
                   waits=[ev1, ev2], inc="ve")
        p.gs(("sslot", 2), ev3)
        return ev3

    def norm2max_chunk(src_ap, src_ev, vec_off):
        evq = p.op("scalar", lambda s: s.square(Sslot(2), src_ap),
                   waits=[src_ev, p.gw(("sslot", 2))], inc="se")
        mevs = []
        for hh in range(2):
            mevs.append(p.op("tensor", lambda t, hh=hh: t.matmul(
                ps[0:1, 3072 + hh * 512:3072 + (hh + 1) * 512], oc_b[:, :],
                Sslot(2)[:, hh * 512:(hh + 1) * 512], start=True, stop=True),
                waits=[evq] + p.ps_w(3072, 4096), inc="pe"))
        ev = p.op("vector", lambda v, vo=vec_off: v.reduce_max(
            vec16[0:1, vo:vo + 1], ps[0:1, 3072:4096], axis=AX.X),
            waits=mevs, inc="ve")
        p.ps_r(3072, 4096, ev)
        p.gs(("sslot", 2), ev)
        return ev

    def attn_sublayer(var):
        is_a = (var == "a")
        n_kch = 16 if is_a else 4
        fw = 512 if is_a else 128
        NR = 2048 if is_a else 512
        VOFF = NR + 128
        VROWS = 2048 if is_a else 512
        kv_own = kvA_own if is_a else kvB_own

        tev1 = d("sync", tabs[:, 0:1024], ap["cos_" + var], waits=[p.gw("tabs")])
        tev2 = d("sync", tabs[:, 1024:2048], ap["sin_" + var], waits=[p.gw("tabs")])
        tab_ev = [tev1, tev2]

        emit_layernorm(False)
        tap("ln_" + var, buf_L[:, :], (128, 16384), list(ln_ev), BF16)
        p.barrier()

        if is_a:
            w_k_src = lambda m: ap["wk_a"].rearrange(
                "(k pp) (mm f) -> pp k mm f", pp=128, f=128)[:, :, m, :]
            w_q_src = lambda m: ap["wq_a"].rearrange(
                "(k pp) (mm f) -> pp k mm f", pp=128, f=128)[:, :, m, :]
        else:
            w_k_src = lambda m: ap["wqkv_b"].rearrange(
                "(k pp) (mm f) -> pp k mm f", pp=128, f=128)[:, :, 16 + m, :]
            w_q_src = lambda m: ap["wqkv_b"].rearrange(
                "(k pp) (mm f) -> pp k mm f", pp=128, f=128)[:, :, m, :]

        def proj_mms(lhs_fn, evw, po):
            evs = []
            ln_last = ln_ev[NCH - 1]
            for k in range(NCH):
                for hh in range(2):
                    w = [evw, ln_last]
                    if k == 0:
                        w += p.ps_w(po + hh * 512, po + (hh + 1) * 512)
                    evs.append(p.op("tensor", lambda t, k=k, hh=hh, po=po: t.matmul(
                        ps[:, po + hh * 512:po + (hh + 1) * 512],
                        lhs_fn(k),
                        Lk(k)[:, hh * 512:(hh + 1) * 512],
                        start=(k == 0), stop=(k == NCH - 1)), waits=w, inc="pe"))
            return evs

        def qk_ln_ip(buf, g_col, b_col, ev_in):
            evq = p.op("scalar", lambda s: s.square(Sslot(2), buf),
                       waits=[ev_in, p.gw(("sslot", 2))], inc="se")
            mevs = []
            for hh in range(2):
                mevs.append(p.op("tensor", lambda t, hh=hh: t.matmul(
                    ps[0:1, 2048 + hh * 512:2048 + (hh + 1) * 512], oc_b[:, :],
                    buf[:, hh * 512:(hh + 1) * 512], start=True, stop=True),
                    waits=[ev_in] + p.ps_w(2048, 3072), inc="pe"))
                mevs.append(p.op("tensor", lambda t, hh=hh: t.matmul(
                    ps[0:1, 3072 + hh * 512:3072 + (hh + 1) * 512], oc_b[:, :],
                    Sslot(2)[:, hh * 512:(hh + 1) * 512], start=True, stop=True),
                    waits=[evq] + p.ps_w(3072, 4096), inc="pe"))
            ev_mu = p.op("vector", lambda v: v.tensor_scalar(
                rows[0:1, 0:1024], ps[0:1, 2048:3072], 1.0 / 128.0, None,
                op0=ALU.mult), waits=mevs + [p.gw("rows")], inc="ve")
            ev_msq = p.op("vector", lambda v: v.tensor_scalar(
                rows[0:1, 1024:2048], ps[0:1, 3072:4096], 1.0 / 128.0, None,
                op0=ALU.mult), waits=mevs, inc="ve")
            p.gs(("sslot", 2), ev_msq)
            ev_t = p.op("vector", lambda v: v.tensor_mul(
                scr[0:1, 0:1024], rows[0:1, 0:1024], rows[0:1, 0:1024]),
                waits=[ev_mu, p.gw("scr")], inc="ve")
            ev_var = p.op("vector", lambda v: v.tensor_sub(
                rows[0:1, 1024:2048], rows[0:1, 1024:2048], scr[0:1, 0:1024]),
                waits=[ev_msq, ev_t], inc="ve")
            ev_sd = p.op("scalar", lambda s: s.activation(
                out=scr[0:1, 0:1024], in_=rows[0:1, 1024:2048], func=AF.Sqrt,
                bias=eps_sb[0:1, 0:1]), waits=[ev_var, ev_memset], inc="se")
            ev_rstd = p.op("vector", lambda v: v.reciprocal(
                rows[0:1, 1024:2048], scr[0:1, 0:1024]), waits=[ev_sd], inc="ve")
            bevs = []
            for hh in range(2):
                bevs.append(p.op("tensor", lambda t, hh=hh: t.matmul(
                    ps[:, 2048 + hh * 512:2048 + (hh + 1) * 512], or_f[0:1, :],
                    rows[0:1, hh * 512:(hh + 1) * 512], start=True, stop=True),
                    waits=[ev_mu] + p.ps_w(2048, 3072), inc="pe"))
                bevs.append(p.op("tensor", lambda t, hh=hh: t.matmul(
                    ps[:, 3072 + hh * 512:3072 + (hh + 1) * 512], or_f[0:1, :],
                    rows[0:1, 1024 + hh * 512:1024 + (hh + 1) * 512],
                    start=True, stop=True),
                    waits=[ev_rstd] + p.ps_w(3072, 4096), inc="pe"))
            p.gs("rows", bevs[-1])
            ev1 = p.op("vector", lambda v: v.tensor_sub(
                scr[:, :], buf, ps[:, 2048:3072]), waits=bevs, inc="ve")
            ev2 = p.op("vector", lambda v: v.tensor_tensor(
                scr[:, :], scr[:, :], ps[:, 3072:4096], op=ALU.mult),
                waits=[ev1], inc="ve")
            p.ps_r(2048, 4096, ev2)
            ev3 = p.op("vector", lambda v, g=g_col, b=b_col: v.tensor_scalar(
                buf, scr[:, :], qn_sb[:, g:g + 1], qn_sb[:, b:b + 1],
                op0=ALU.mult, op1=ALU.add),
                waits=[ev2] + EVI, inc="ve")
            p.gs("scr", ev3)
            return ev3

        def post_chunk(buf, m, is_q):
            if is_a:
                ev = p.gw(("evac", is_q, m))
            else:
                ev = qk_ln_ip(buf, 0 if is_q else 2, 1 if is_q else 3,
                              p.gw(("evac", is_q, m)))
            ev_rope = rope_chunk(buf, ev, tab_ev)
            ev_nrm = norm2max_chunk(buf, ev_rope, (m if is_q else 16 + m))
            return ev_rope, ev_nrm

        ship_evs = []
        nrm_evs = []
        for m in range(n_kch + 1):
            if m < n_kch:
                wsl = m % 4
                evw = d("gpsimd", Wslot(wsl).rearrange("pp (k f) -> pp k f", f=128),
                        w_k_src(m), waits=[p.gw(("wslot", wsl))])
                po = (m % 2) * 1024
                mm_evs = proj_mms(
                    lambda k, wsl=wsl: Wslot(wsl)[:, k * 128:(k + 1) * 128], evw, po)
                p.gs(("wslot", wsl), mm_evs[-1])
                raw = m % 2
                if is_a:
                    ev_evac = p.op("vector", lambda v, m=m, po=po, raw=raw: v.tensor_scalar(
                        Sslot(raw), ps[:, po:po + 1024], bk_sb[:, m:m + 1], None,
                        op0=ALU.add),
                        waits=mm_evs + [p.gw(("sslot", raw))] + EVI, inc="ve")
                else:
                    ev_evac = p.op("vector", lambda v, po=po, raw=raw: v.tensor_copy(
                        Sslot(raw), ps[:, po:po + 1024]),
                        waits=mm_evs + [p.gw(("sslot", raw))], inc="ve")
                p.ps_r(po, po + 1024, ev_evac)
                p.gs(("evac", False, m), ev_evac)
            if m >= 1:
                mp = m - 1
                raw = mp % 2
                ev_rope, ev_nrm = post_chunk(Sslot(raw), mp, False)
                nrm_evs.append(ev_nrm)
                ev_ship = d("sync", kv_own.ap()[mp * 128:(mp + 1) * 128, :],
                            Sslot(raw), waits=[ev_rope, ev_nrm])
                p.gs(("sslot", raw), ev_ship)
                ship_evs.append(ev_ship)
        ev_nrm_ship = d("gpsimd", kv_own.ap()[NR:NR + 1, 0:n_kch],
                        vec16[0:1, 16:16 + n_kch], waits=nrm_evs)

        def v_stage(sl, extra=()):
            reg = sl % 2
            if is_a:
                vsrc = ap["wv_a"].rearrange(
                    "(k pp) (s f) -> pp k s f", pp=128, f=512)[:, :, sl, :]
            else:
                vsrc = ap["wqkv_b"].rearrange(
                    "(k pp) (s f) -> pp k s f", pp=128, f=128)[:, :, 20 + sl, :]
            vstg = buf_B[:, reg * 8192:reg * 8192 + 16 * fw]
            return vstg, d(
                "gpsimd", vstg.rearrange("pp (k f) -> pp k f", f=fw), vsrc,
                waits=[p.gw(("bufB", reg)), p.gw(("bufBv", reg))] + list(extra))
        vstg_evs = {0: v_stage(0), 1: v_stage(1)}
        if is_a:
            ev_agK = coll("AllGather", ALU.bypass, [[0, 1, 2, 3], [4, 5, 6, 7]],
                          None, None, ship_evs + [ev_nrm_ship],
                          in_ap=kv_own.ap()[0:VOFF, :], out_ap=kA_all.ap())
        v_tile_evs = []
        for sl in range(4):
            reg = sl % 2
            vstg, evw = vstg_evs[sl]
            for tt in range(NQT):
                lo = 2048 + reg * 512
                mm = []
                for k in range(NCH):
                    w = [evw, ln_ev[NCH - 1]]
                    if k == 0:
                        w += p.ps_w(lo, lo + fw)
                    mm.append(p.op("tensor", lambda t, tt=tt, k=k, lo=lo, vstg=vstg: t.matmul(
                        ps[:, lo:lo + fw], Lk(k)[:, tt * 128:(tt + 1) * 128],
                        vstg[:, k * fw:(k + 1) * fw],
                        start=(k == 0), stop=(k == NCH - 1)), waits=w, inc="pe"))
                ac = tt * 2048 + sl * fw
                ev_evac = p.op("vector", lambda v, lo=lo, ac=ac: v.tensor_copy(
                    buf_A[:, ac:ac + fw], ps[:, lo:lo + fw]),
                    waits=[mm[-1], p.gw(("abuf", tt * 2))], inc="ve")
                p.ps_r(lo, lo + fw, ev_evac)
                v_tile_evs.append((tt, sl, ev_evac))
            p.gs(("bufBv", reg), mm[-1])
            if sl + 2 < 4:
                vstg_evs[sl + 2] = v_stage(sl + 2, extra=[mm[-1]])
        if is_a:
            v_ship_view = kv_own.ap()[VOFF:VOFF + VROWS, :].rearrange(
                "(t h) c -> t (h c)", h=2)
        else:
            v_ship_view = kv_own.ap()[VOFF:VOFF + VROWS, :].rearrange(
                "t (h c) -> (t h) c", h=2)
        vship_evs = []
        for tt, sl, ev in v_tile_evs:
            ac = tt * 2048 + sl * fw
            ev_ship = d("sync",
                        v_ship_view[tt * 128:(tt + 1) * 128, sl * fw:(sl + 1) * fw],
                        buf_A[:, ac:ac + fw], waits=[ev])
            vship_evs.append(ev_ship)
            p.gs(("abuf", tt * 2), ev_ship)
            p.gs(("abuf", tt * 2 + 1), ev_ship)

        pre_evs = []
        for k in range(8):
            reg = k // 4
            dst = buf_B[:, k * 2048:(k + 1) * 2048]
            pre_evs.append(d("gpsimd", dst.rearrange("pp (kk f) -> pp kk f", f=128),
                             w_q_src(k), waits=[p.gw(("bufBv", reg))]))

        if is_a:
            ev_agV = coll("AllGather", ALU.bypass, [[0, 1, 2, 3], [4, 5, 6, 7]],
                          None, None, vship_evs,
                          in_ap=kv_own.ap()[VOFF:VOFF + VROWS, :], out_ap=vA_all.ap())
            ev_ag = ev_agV
            if not DBG_NO_W2:
                W2["ev"] = coll("AllGather", ALU.bypass, [list(range(8))],
                                wpp_own, wpp_all, w2_out_evs)
        else:
            ev_ag = coll("AllGather", ALU.bypass, [[0, 1, 2, 3], [4, 5, 6, 7]],
                         kv_own, kvB_all, ship_evs + vship_evs + [ev_nrm_ship])
            ev_agK = ev_ag
            ev_agV = ev_ag

        rope_evs = []
        nq_evs = []
        for m in range(17):
            if m < 16:
                po = (m % 2) * 1024
                if m < 8:
                    lhs_fn = lambda k, m=m: buf_B[:, m * 2048 + k * 128:m * 2048 + (k + 1) * 128]
                    mm_evs = proj_mms(lhs_fn, pre_evs[m], po)
                else:
                    wsl = m % 4
                    evw = d("gpsimd", Wslot(wsl).rearrange("pp (k f) -> pp k f", f=128),
                            w_q_src(m), waits=[p.gw(("wslot", wsl))])
                    mm_evs = proj_mms(
                        lambda k, wsl=wsl: Wslot(wsl)[:, k * 128:(k + 1) * 128], evw, po)
                    p.gs(("wslot", wsl), mm_evs[-1])
                if is_a:
                    ev_evac = p.op("vector", lambda v, m=m, po=po: v.tensor_scalar(
                        Ak(m), ps[:, po:po + 1024], bq_sb[:, m:m + 1], None,
                        op0=ALU.add),
                        waits=mm_evs + [p.gw(("abuf", m))] + EVI, inc="ve")
                else:
                    ev_evac = p.op("vector", lambda v, m=m, po=po: v.tensor_copy(
                        Ak(m), ps[:, po:po + 1024]),
                        waits=mm_evs + [p.gw(("abuf", m))], inc="ve")
                p.ps_r(po, po + 1024, ev_evac)
                p.gs(("evac", True, m), ev_evac)
            if m >= 1:
                mp = m - 1
                ev_rope, ev_nq = post_chunk(Ak(mp), mp, True)
                rope_evs.append(ev_rope)
                nq_evs.append(ev_nq)
        p.gs("tabs", rope_evs[-1])
        tap("q_" + var, buf_A[:, :], (128, 16384), rope_evs, BF16)

        SEGK = VOFF
        k_all_t = kA_all if is_a else kvB_all
        SEGF = VOFF if is_a else 1152
        ev_mkload = d("gpsimd", vec16[0:1, 64:64 + 4 * n_kch],
                      k_all_t.ap().rearrange("(r s) q -> s r q", s=SEGF)[
                          NR:NR + 1, :, 0:n_kch],
                      waits=[ev_agK])
        ev_mk = p.op("vector", lambda v: v.reduce_max(
            vec16[0:1, 32:33], vec16[0:1, 64:64 + 4 * n_kch], axis=AX.X),
            waits=[ev_mkload], inc="ve")
        ev_b1 = p.op("vector", lambda v: v.tensor_scalar(
            vec16[0:1, 48:64], vec16[0:1, 0:16], vec16[0:1, 32:33], None,
            op0=ALU.mult), waits=[ev_mk] + nq_evs, inc="ve")
        ev_b2 = p.op("scalar", lambda s: s.activation(
            out=vec16[0:1, 0:16], in_=vec16[0:1, 48:64], func=AF.Sqrt,
            bias=eps_sb[0:1, 0:1]), waits=[ev_b1, ev_memset], inc="se")
        ev_b3 = p.op("vector", lambda v: v.tensor_scalar(
            vec16[0:1, 48:64], vec16[0:1, 0:16], -B_SAFETY * ISQ_D, None,
            op0=ALU.mult), waits=[ev_b2], inc="ve")
        ev_b4 = p.op("tensor", lambda t: t.matmul(
            ps[:, 3072:3088], or_f[0:1, :], vec16[0:1, 48:64],
            start=True, stop=True),
            waits=[ev_b3] + p.ps_w(3072, 3584), inc="pe")
        ev_negB = p.op("vector", lambda v: v.tensor_copy(
            negB_sb[:, :], ps[:, 3072:3088]), waits=[ev_b4], inc="ve")
        p.ps_r(3072, 3584, ev_negB)
        tap("negb_" + var, negB_sb[:, :], (128, 16), [ev_negB])
        p.barrier()

        z_evs = []
        k_view = k_all_t.ap().rearrange("(r s) q -> s r q", s=SEGF)
        if is_a:
            v_view = vA_all.ap().rearrange("(r s) q -> r s q", s=2048).rearrange(
                "r (t h) c -> t r (h c)", h=2)
        else:
            v_view = kvB_all.ap().rearrange("(r s) q -> r s q", s=1152)[
                :, VOFF:VOFF + VROWS, :].rearrange("r t (h c) -> (t h) r c", h=2)
        for h in range(16):
            kh = h if is_a else h // 4
            kv_evs = [None] * 8

            def emit_kv_load(kb4, kh=kh, kv_evs=kv_evs):
                j = kb4 % 2
                ksrc = k_view[kh * 128:(kh + 1) * 128, :, kb4 * 128:(kb4 + 1) * 128]
                vsrc = v_view[kb4 * 128:(kb4 + 1) * 128, :, kh * 128:(kh + 1) * 128]
                gwk = p.gw(("kpair", j))
                evk = d("sync", Kslot(j).rearrange("pp (r q) -> pp r q", q=128),
                        ksrc, waits=[gwk, ev_agK])
                evv = d("sync", Vslot(j).rearrange("pp (r f) -> pp r f", f=128),
                        vsrc, waits=[gwk, ev_agV])
                kv_evs[kb4] = (evk, evv)

            emit_kv_load(0)
            emit_kv_load(1)
            last_sums = None
            last_pv = None
            for kb in range(NKB):
                m0 = kb // 4
                dd = kb % 4
                j = m0 % 2
                so = 0 if kb % 2 == 0 else 2048
                live0 = m0 * 128
                regions = [(live0, 512), (512, 1024)] if live0 < 512 else [(live0, 1024)]
                evk, evv = kv_evs[m0]
                sc_evs = []
                for (r0, r1) in regions:
                    w = [evk] + p.ps_w(so + r0, so + r1)
                    sc_evs.append(p.op("tensor", lambda t, so=so, r0=r0, r1=r1, j=j, dd=dd, h=h: t.matmul(
                        ps[:, so + r0:so + r1], Kslot(j)[:, dd * 128:(dd + 1) * 128],
                        Ak(h)[:, r0:r1], start=True, stop=False),
                        waits=w, inc="pe"))
                ev_mask = p.op("tensor", lambda t, so=so, live0=live0, dd=dd: t.matmul(
                    ps[:, so + live0:so + live0 + 128], id_sb[:, :],
                    mask_sb[:, dd * 128:(dd + 1) * 128], start=False, stop=True),
                    waits=sc_evs + EVI, inc="pe")
                pslot = kb % 3
                exp_evs = []
                for (r0, r1) in regions:
                    exp_evs.append(p.op("scalar", lambda s, so=so, r0=r0, r1=r1, pslot=pslot, h=h: s.activation(
                        out=Sslot(pslot)[:, r0:r1], in_=ps[:, so + r0:so + r1],
                        func=AF.Exp, bias=negB_sb[:, h:h + 1], scale=ISQ_D),
                        waits=[ev_mask, ev_negB, p.gw(("sslot", pslot))], inc="se"))
                for (r0, r1) in regions:
                    p.ps_r(so + r0, so + r1, exp_evs[-1])
                for (r0, r1) in regions:
                    w = exp_evs + (p.ps_w(3072 + r0, 3072 + r1) if kb == 0 else [])
                    last_sums = p.op("tensor", lambda t, r0=r0, r1=r1, pslot=pslot, kb=kb: t.matmul(
                        ps[0:1, 3072 + r0:3072 + r1], oc_b[:, :],
                        Sslot(pslot)[:, r0:r1],
                        start=(kb == 0), stop=(kb == NKB - 1)), waits=w, inc="pe")
                for (r0, r1) in regions:
                    w = exp_evs + [evv] + (p.ps_w(1024 + r0, 1024 + r1) if kb == 0 else [])
                    last_pv = p.op("tensor", lambda t, r0=r0, r1=r1, pslot=pslot, j=j, dd=dd, kb=kb: t.matmul(
                        ps[:, 1024 + r0:1024 + r1], Vslot(j)[:, dd * 128:(dd + 1) * 128],
                        Sslot(pslot)[:, r0:r1],
                        start=(kb == 0), stop=(kb == NKB - 1)), waits=w, inc="pe")
                p.gs(("sslot", pslot), last_pv)
                if dd == 3:
                    p.gs(("kpair", j), last_pv)
                    if m0 + 2 < 8:
                        emit_kv_load(m0 + 2)
            ev_sc = p.op("scalar", lambda s: s.copy(rows[0:1, 0:1024],
                                                    ps[0:1, 3072:4096]),
                         waits=[last_sums, p.gw("rows")], inc="se")
            rb_evs = []
            for hh in range(2):
                rb_evs.append(p.op("tensor", lambda t, hh=hh: t.matmul(
                    ps[:, 2048 + hh * 512:2048 + (hh + 1) * 512], or_f[0:1, :],
                    rows[0:1, hh * 512:(hh + 1) * 512], start=True, stop=True),
                    waits=[ev_sc] + p.ps_w(2048, 3072), inc="pe"))
            p.gs("rows", rb_evs[-1])
            ev_ln = p.op("scalar", lambda s: s.activation(
                out=ps[:, 3072:4096], in_=ps[:, 2048:3072], func=AF.Ln),
                waits=rb_evs, inc="se")
            ev_rcp = p.op("scalar", lambda s: s.activation(
                out=ps[:, 2048:3072], in_=ps[:, 3072:4096], func=AF.Exp, scale=-1.0),
                waits=[ev_ln], inc="se")
            p.ps_r(3072, 4096, ev_rcp)
            ev_zc = p.op("vector", lambda v: v.tensor_copy(scr[:, :], ps[:, 1024:2048]),
                         waits=[last_pv, p.gw("scr")], inc="ve")
            p.ps_r(1024, 2048, ev_zc)
            if is_a:
                ev_zm = p.op("vector", lambda v: v.tensor_tensor(
                    scr[:, :], scr[:, :], ps[:, 2048:3072], op=ALU.mult),
                    waits=[ev_zc, ev_rcp], inc="ve")
                ev_zn = p.op("vector", lambda v, h=h: v.tensor_scalar(
                    Bk(h), scr[:, :], bv_sb[:, h:h + 1], None, op0=ALU.add),
                    waits=[ev_zm] + EVI, inc="ve")
            else:
                ev_zn = p.op("vector", lambda v, h=h: v.tensor_tensor(
                    Bk(h), scr[:, :], ps[:, 2048:3072], op=ALU.mult),
                    waits=[ev_zc, ev_rcp], inc="ve")
            p.gs("scr", ev_zn)
            p.ps_r(2048, 3072, ev_zn)
            z_evs.append(ev_zn)
        tap("z_" + var, buf_B[:, :], (128, 16384), z_evs, BF16)
        p.barrier()

        wo_ap = ap["wo_a"] if is_a else ap["wo_b"]
        for m in range(16):
            wsl = m % 4
            wsrc = wo_ap.rearrange("(k pp) (mm f) -> pp k mm f", pp=128, f=128)[:, :, m, :]
            evw = d("gpsimd", Wslot(wsl).rearrange("pp (k f) -> pp k f", f=128),
                    wsrc, waits=[p.gw(("wslot", wsl))])
            po = (m % 2) * 1024
            mm_evs = []
            for k in range(16):
                for hh in range(2):
                    w = [evw, z_evs[15]]
                    if k == 0:
                        w += p.ps_w(po + hh * 512, po + (hh + 1) * 512)
                    mm_evs.append(p.op("tensor", lambda t, k=k, hh=hh, po=po, wsl=wsl: t.matmul(
                        ps[:, po + hh * 512:po + (hh + 1) * 512],
                        Wslot(wsl)[:, k * 128:(k + 1) * 128],
                        Bk(k)[:, hh * 512:(hh + 1) * 512],
                        start=(k == 0), stop=(k == 15)), waits=w, inc="pe"))
            p.gs(("wslot", wsl), mm_evs[-1])
            if is_a:
                ev_add = p.op("vector", lambda v, m=m, po=po: v.scalar_tensor_tensor(
                    Rk(m), ps[:, po:po + 1024], bo_sb[:, m:m + 1], Rk(m),
                    op0=ALU.add, op1=ALU.add),
                    waits=mm_evs + [resid_ev[m]] + EVI, inc="ve")
            else:
                ev_add = p.op("vector", lambda v, m=m, po=po: v.tensor_tensor(
                    Rk(m), Rk(m), ps[:, po:po + 1024], op=ALU.add),
                    waits=mm_evs + [resid_ev[m]], inc="ve")
            p.ps_r(po, po + 1024, ev_add)
            resid_ev[m] = ev_add
        p.barrier()
        for reg in range(2):
            p.gs(("bufB", reg), resid_ev[15])
            p.gs(("bufBv", reg), resid_ev[15])
        for m in range(16):
            p.gs(("abuf", m), resid_ev[15])

    def ffn_sublayer():
        emit_layernorm(True)
        p.barrier()
        xp_evs = []
        for a2 in range(16):
            po = (a2 % 2) * 1024
            mm_evs = []
            for a in range(16):
                hsb = hp_sb if H16[a2, a] > 0 else hm_sb
                for hh in range(2):
                    w = [ln_ev[NCH - 1]] + EVI
                    if a == 0:
                        w += p.ps_w(po + hh * 512, po + (hh + 1) * 512)
                    mm_evs.append(p.op("tensor", lambda t, a=a, hh=hh, po=po, hsb=hsb: t.matmul(
                        ps[:, po + hh * 512:po + (hh + 1) * 512], hsb[:, :],
                        Lk(a)[:, hh * 512:(hh + 1) * 512],
                        start=(a == 0), stop=(a == 15)), waits=w, inc="pe"))
            ev_e = p.op("vector", lambda v, a2=a2, po=po: v.tensor_scalar(
                Bk(a2), ps[:, po:po + 1024], HAD_N1, None, op0=ALU.mult),
                waits=mm_evs + [p.gw(("bbuf", a2))], inc="ve")
            p.ps_r(po, po + 1024, ev_e)
            xp_evs.append(ev_e)
        tap("xp", buf_B[:, :], (128, 16384), xp_evs, BF16)
        p.barrier()

        for q in range(4):
            h_evs = []
            for icl in range(16):
                I = q * 16 + icl
                upsrc = ap["w_upT"].rearrange(
                    "(k pp) (i f) -> pp k i f", pp=128, f=128)[:, :, I, :]
                gtsrc = ap["w_gateT"].rearrange(
                    "(k pp) (i f) -> pp k i f", pp=128, f=128)[:, :, I, :]
                su_ = icl % 2
                sg_ = 2 + icl % 2
                evwu = d("gpsimd", Wslot(su_).rearrange("pp (k f) -> pp k f", f=128),
                         upsrc, waits=[p.gw(("wslot", su_))])
                evwg = d("gpsimd", Wslot(sg_).rearrange("pp (k f) -> pp k f", f=128),
                         gtsrc, waits=[p.gw(("wslot", sg_))])
                po_u = (icl % 2) * 1024
                po_g = 2048 + (icl % 2) * 1024
                mmu, mmg = [], []
                for k in range(16):
                    for hh in range(2):
                        wu = [evwu, xp_evs[15]]
                        wg = [evwg, xp_evs[15]]
                        if k == 0:
                            wu += p.ps_w(po_u + hh * 512, po_u + (hh + 1) * 512)
                            wg += p.ps_w(po_g + hh * 512, po_g + (hh + 1) * 512)
                        mmu.append(p.op("tensor", lambda t, k=k, hh=hh, po=po_u, su_=su_: t.matmul(
                            ps[:, po + hh * 512:po + (hh + 1) * 512],
                            Wslot(su_)[:, k * 128:(k + 1) * 128],
                            Bk(k)[:, hh * 512:(hh + 1) * 512],
                            start=(k == 0), stop=(k == 15)), waits=wu, inc="pe"))
                        mmg.append(p.op("tensor", lambda t, k=k, hh=hh, po=po_g, sg_=sg_: t.matmul(
                            ps[:, po + hh * 512:po + (hh + 1) * 512],
                            Wslot(sg_)[:, k * 128:(k + 1) * 128],
                            Bk(k)[:, hh * 512:(hh + 1) * 512],
                            start=(k == 0), stop=(k == 15)), waits=wg, inc="pe"))
                p.gs(("wslot", su_), mmu[-1])
                p.gs(("wslot", sg_), mmg[-1])
                ssl = icl % 3
                ev_silu = p.op("scalar", lambda s, po=po_g, I=I, ssl=ssl: s.activation(
                    out=Sslot(ssl), in_=ps[:, po:po + 1024], func=AF.Silu,
                    scale=wsg_sb[:, I:I + 1]),
                    waits=mmg + [p.gw(("sslot", ssl))] + EVI, inc="se")
                ev_h = p.op("vector", lambda v, po=po_u, I=I, icl=icl, ssl=ssl: v.scalar_tensor_tensor(
                    Ak(icl), ps[:, po:po + 1024], wsu_sb[:, I:I + 1], Sslot(ssl),
                    op0=ALU.mult, op1=ALU.mult),
                    waits=[ev_silu, p.gw(("abuf", icl))] + mmu + EVI, inc="ve")
                p.ps_r(po_u, po_u + 1024, ev_h)
                p.ps_r(po_g, po_g + 1024, ev_h)
                p.gs(("sslot", ssl), ev_h)
                h_evs.append(ev_h)
            for m in range(16):
                wsl = m % 4
                s_rank = m // 2
                wsrc = wpp_all.ap()[s_rank * 8192 + q * 2048:s_rank * 8192 + (q + 1) * 2048,
                                    (m % 2) * 128:(m % 2) * 128 + 128]
                wsrc = wsrc.rearrange("(i pp) f -> pp i f", pp=128)
                evw = d("sync", Wslot(wsl).rearrange("pp (i f) -> pp i f", f=128),
                        wsrc, waits=[p.gw(("wslot", wsl)), W2["ev"]])
                po = (m % 2) * 1024
                mm_evs = []
                for icl in range(16):
                    for hh in range(2):
                        w = [evw, h_evs[15]]
                        if icl == 0:
                            w += p.ps_w(po + hh * 512, po + (hh + 1) * 512)
                        mm_evs.append(p.op("tensor", lambda t, icl=icl, hh=hh, po=po, wsl=wsl: t.matmul(
                            ps[:, po + hh * 512:po + (hh + 1) * 512],
                            Wslot(wsl)[:, icl * 128:(icl + 1) * 128],
                            Ak(icl)[:, hh * 512:(hh + 1) * 512],
                            start=(icl == 0), stop=(icl == 15)), waits=w, inc="pe"))
                p.gs(("wslot", wsl), mm_evs[-1])
                ev_add = p.op("vector", lambda v, m=m, po=po: v.scalar_tensor_tensor(
                    Rk(m), ps[:, po:po + 1024], wsdp_sb[:, m:m + 1], Rk(m),
                    op0=ALU.mult, op1=ALU.add),
                    waits=mm_evs + [resid_ev[m]] + EVI, inc="ve")
                p.ps_r(po, po + 1024, ev_add)
                resid_ev[m] = ev_add
            for icl in range(16):
                p.gs(("abuf", icl), resid_ev[15])
        for reg in range(2):
            p.gs(("bufB", reg), resid_ev[15])
        p.barrier()

    attn_sublayer("a")
    tap("r_a", buf_R[:, :], (128, 16384), list(resid_ev))
    ffn_sublayer()
    tap("r_f1", buf_R[:, :], (128, 16384), list(resid_ev))
    attn_sublayer("b")
    tap("r_b", buf_R[:, :], (128, 16384), list(resid_ev))
    ffn_sublayer()

    st_evs = [d("sync", out_x[k * 128:(k + 1) * 128, :], Rk(k),
                waits=[resid_ev[k]]) for k in range(NCH)]
    p.op("gpsimd", lambda g: g.engine_nop(), waits=st_evs)

    with nc.Block() as block:
        OWN = {} if DBG_KEEP_SELF else {"tensor": "pe"}

        def emitter(name):
            own = OWN.get(name)
            def run(eng):
                seen = {}
                for fn, ws, inc, amt in p.ops[name]:
                    for s, v in sorted(ws.items()):
                        if s == own:
                            continue
                        if seen.get(s, 0) < v:
                            eng.wait_ge(sems[s], v)
                            seen[s] = v
                    inst = fn(eng)
                    if inc is not None:
                        inst.then_inc(sems[inc], amt)
            return run
        block.tensor(emitter("tensor"))
        block.vector(emitter("vector"))
        block.scalar(emitter("scalar"))
        block.gpsimd(emitter("gpsimd"))
        block.sync(emitter("sync"))

    es.close()
    return nc, dbg


_cached = {}


def kernel(**inputs):
    maps = _prep_inputs(inputs)
    key = tuple(sorted(_TAPS))
    if key not in _cached:
        _cached[key] = build_program()
    nc, dbg = _cached[key]
    res = run_bass_kernel_spmd(nc, maps, core_ids=list(range(8)))
    out = _assemble(res.results)
    if dbg:
        kernel._dbg = [{t: np.asarray(r["dbg_" + t]) for t in dbg} for r in res.results]
    return out


kernel._dbg = None


# version 09
# speedup vs baseline: 1.0173x; 1.0130x over previous
import numpy as np
import ml_dtypes
from contextlib import ExitStack

import concourse.bass as bass
import concourse.mybir as mybir
from concourse.bass_utils import run_bass_kernel_spmd

F32 = mybir.dt.float32
BF16 = mybir.dt.bfloat16
AF = mybir.ActivationFunctionType
ALU = mybir.AluOpType
AX = mybir.AxisListType
NPBF = ml_dtypes.bfloat16

NCH = 16
NQT = 8
NKB = 32
ISQ_D = float(1.0 / np.sqrt(128.0))
HAD_N1 = float(1.0 / np.sqrt(2048.0))
HAD_N2 = float(1.0 / np.sqrt(8192.0))
EPS = 1e-5
B_SAFETY = 1.05
NEG_BIG = -30000.0

_TAPS = set()
import os as _os
DBG_NO_W2 = bool(int(_os.environ.get("K_DBG_NO_W2", "0")))
DBG_NO_ROPE = bool(int(_os.environ.get("K_DBG_NO_ROPE", "0")))
DBG_NO_COLL = bool(int(_os.environ.get("K_DBG_NO_COLL", "0")))
DBG_KEEP_SELF = bool(int(_os.environ.get("K_DBG_KEEP_SELF", "0")))

ENGS = ("tensor", "vector", "scalar", "gpsimd", "sync")
SEMS = ("dg", "ds", "dc", "pe", "ve", "se", "cc")
DMA_SEMS = ("dg", "ds", "dc")


class Prog:
    def __init__(self):
        self.ops = {e: [] for e in ENGS}
        self.plan = {s: 0 for s in SEMS}
        self.pending = {e: {} for e in ENGS}
        self.guards = {}

    def op(self, eng, fn, waits=(), inc=None, amt=None):
        ws = dict(self.pending[eng])
        self.pending[eng] = {}
        for w in waits:
            if w is None:
                continue
            s, v = w
            ws[s] = max(ws.get(s, 0), v)
        ev = None
        if inc is not None:
            if amt is None:
                amt = 16 if inc in DMA_SEMS else 1
            self.plan[inc] += amt
            ev = (inc, self.plan[inc])
        self.ops[eng].append((fn, ws, inc, amt))
        return ev

    def barrier(self):
        evs = [(s, self.plan[s]) for s in SEMS if self.plan[s] > 0]
        for e in ("tensor", "vector", "scalar"):
            dd = self.pending[e]
            for s, v in evs:
                dd[s] = max(dd.get(s, 0), v)

    def gw(self, key):
        return self.guards.get(key)

    def gs(self, key, ev):
        self.guards[key] = ev

    def ps_w(self, lo, hi):
        return [self.gw(("psb", b)) for b in range(lo // 512, (hi + 511) // 512)]

    def ps_r(self, lo, hi, ev):
        for b in range(lo // 512, (hi + 511) // 512):
            self.gs(("psb", b), ev)



def _rope_tables(positions, base):
    inv = 1.0 / base ** (np.arange(0, 128, 2, dtype=np.float64) / 128.0)
    ang = np.outer(inv, positions.astype(np.float64))
    cos = np.concatenate([np.cos(ang), np.cos(ang)], axis=0)
    sin = np.concatenate([np.sin(ang), np.sin(ang)], axis=0)
    return cos.astype(NPBF), sin.astype(NPBF)


def _mask_tile(r):
    kk = np.arange(128)[:, None]
    qi = np.arange(128)[None, :]
    tiles = []
    for dd in range(4):
        allowed = (128 * dd + kk) <= (128 * r + qi)
        tiles.append(np.where(allowed, 0.0, NEG_BIG))
    return np.concatenate(tiles, axis=1).astype(NPBF)


def _prep_inputs(inp):
    asc = np.ascontiguousarray
    x = np.asarray(inp["L0_attn_a_normalized_resid_pre"], dtype=np.float32)
    perm = np.concatenate([np.arange(0, 128, 2), np.arange(1, 128, 2)])

    wq = np.asarray(inp["L0_attn_a_W_Q"], np.float32)[:, :, perm]
    wk = np.asarray(inp["L0_attn_a_W_K"], np.float32)[:, :, perm]
    wv = np.asarray(inp["L0_attn_a_W_V"], np.float32)
    shared = {
        "wq_a": asc(wq.transpose(1, 0, 2).reshape(2048, 2048)),
        "wk_a": asc(wk.transpose(1, 0, 2).reshape(2048, 2048)),
        "wv_a": asc(wv.transpose(1, 0, 2).reshape(2048, 2048)),
        "wo_a": asc(np.asarray(inp["L0_attn_a_W_O"], np.float32).reshape(2048, 2048)),
        "bq_a": asc(np.asarray(inp["L0_attn_a_b_Q"], np.float32)[:, perm].T),
        "bk_a": asc(np.asarray(inp["L0_attn_a_b_K"], np.float32)[:, perm].T),
        "bv_col": asc(np.asarray(inp["L0_attn_a_b_V"], np.float32).reshape(16, 128).T),
        "bo_col": asc(np.asarray(inp["L0_attn_a_b_O"], np.float32).reshape(16, 128).T),
        "wqkv_b": asc(np.asarray(inp["L0_attn_b_w_qkv"], np.float32).T),
        "wo_b": asc(np.asarray(inp["L0_attn_b_w_o"], np.float32).T),
        "qng": asc(np.asarray(inp["L0_attn_b_qn_g"], np.float32).reshape(128, 1)),
        "qnb": asc(np.asarray(inp["L0_attn_b_qn_b"], np.float32).reshape(128, 1)),
        "kng": asc(np.asarray(inp["L0_attn_b_kn_g"], np.float32).reshape(128, 1)),
        "knb": asc(np.asarray(inp["L0_attn_b_kn_b"], np.float32).reshape(128, 1)),
        "w_upT": asc(np.asarray(inp["L0_ffn_W_up"], np.float32).T),
        "w_gateT": asc(np.asarray(inp["L0_ffn_W_gate"], np.float32).T),
        "su_ug": asc(np.asarray(inp["L0_ffn_SU_ug"], np.float32).reshape(16, 128).T),
        "su_dp_re": asc(np.asarray(inp["L0_ffn_SU_dp"], np.float32).reshape(64, 128)),
        "wsu": asc(np.asarray(inp["L0_ffn_Wscale_ug"], np.float32)[:8192].reshape(64, 128).T),
        "wsg": asc(np.asarray(inp["L0_ffn_Wscale_ug"], np.float32)[8192:].reshape(64, 128).T),
        "wsdp": asc(np.asarray(inp["L0_ffn_Wscale_dp"], np.float32).reshape(16, 128).T),
    }
    w_downT = np.asarray(inp["L0_ffn_W_down"], np.float32).T
    maps = []
    for c in range(8):
        g, r = divmod(c, 4)
        blocks = [4 * m + r for m in range(8)]
        pos = np.concatenate([np.arange(128) + 128 * b for b in blocks])
        xc = x[g].reshape(32, 128, 2048)[blocks].reshape(1024, 2048)
        cos_a, sin_a = _rope_tables(pos, 10000.0)
        cos_b, sin_b = _rope_tables(pos, 1000000.0)
        m = dict(shared)
        m["xT"] = asc(xc.T)
        m["w_downT_sl"] = asc(w_downT[:, c * 256:(c + 1) * 256])
        m["cos_a"] = asc(cos_a); m["sin_a"] = asc(sin_a)
        m["cos_b"] = asc(cos_b); m["sin_b"] = asc(sin_b)
        m["maskT"] = _mask_tile(r)
        maps.append(m)
    return maps


def _assemble(results):
    out = np.empty((2, 4096, 2048), np.float32)
    for c in range(8):
        g, r = divmod(c, 4)
        xc = np.asarray(results[c]["out_x"]).T.reshape(8, 128, 2048)
        for mm in range(8):
            b = 4 * mm + r
            out[g, b * 128:(b + 1) * 128, :] = xc[mm]
    return out



def build_program():
    nc = bass.Bass("TRN2", target_bir_lowering=False, debug=False)
    p = Prog()

    ap = {}
    def ein(name, shape, dt=F32):
        ap[name] = nc.dram_tensor(name, list(shape), dt, kind="ExternalInput").ap()
    ein("xT", (2048, 1024))
    for n in ("wq_a", "wk_a", "wv_a", "wo_a"):
        ein(n, (2048, 2048))
    ein("bq_a", (128, 16)); ein("bk_a", (128, 16))
    ein("bv_col", (128, 16)); ein("bo_col", (128, 16))
    ein("wqkv_b", (2048, 3072)); ein("wo_b", (2048, 2048))
    for n in ("qng", "qnb", "kng", "knb"):
        ein(n, (128, 1))
    ein("w_upT", (2048, 8192)); ein("w_gateT", (2048, 8192))
    ein("w_downT_sl", (8192, 256))
    ein("su_ug", (128, 16)); ein("su_dp_re", (64, 128))
    ein("wsu", (128, 64)); ein("wsg", (128, 64)); ein("wsdp", (128, 16))
    for n in ("cos_a", "sin_a", "cos_b", "sin_b"):
        ein(n, (128, 1024), BF16)
    ein("maskT", (128, 512), BF16)

    out_x = nc.dram_tensor("out_x", [2048, 1024], F32, kind="ExternalOutput").ap()

    kvA_own = nc.dram_tensor("kvA_own", [4224, 1024], BF16, kind="Internal")
    kA_all = nc.dram_tensor("kA_all", [4 * 2176, 1024], BF16, kind="Internal")
    vA_all = nc.dram_tensor("vA_all", [4 * 2048, 1024], BF16, kind="Internal")
    kvB_own = nc.dram_tensor("kvB_own", [1152, 1024], BF16, kind="Internal")
    kvB_all = nc.dram_tensor("kvB_all", [4 * 1152, 1024], BF16, kind="Internal")
    vd = nc.dram_tensor("vd", [8192, 256], BF16, kind="Internal")
    wpp_own = nc.dram_tensor("wpp_own", [8192, 256], BF16, kind="Internal")
    wpp_all = nc.dram_tensor("wpp_all", [8 * 8192, 256], BF16, kind="Internal")

    ic = lambda a, n: nc.inline_tensor(np.ascontiguousarray(a), name=n).ap()
    ones_col_f = ic(np.ones((128, 1), np.float32), "c_ocf")
    ones_col_b = ic(np.ones((128, 1), NPBF), "c_ocb")
    ones_row_f = ic(np.ones((1, 128), np.float32), "c_orf")
    ident_b = ic(np.eye(128, dtype=NPBF), "c_id")
    rotT_np = np.zeros((128, 128), np.float32)
    for i in range(64):
        rotT_np[i, 64 + i] = 1.0
        rotT_np[64 + i, i] = -1.0
    rotT_c = ic(rotT_np.astype(NPBF), "c_rotT")

    def _hmat(n):
        h = np.array([[1.0]])
        while h.shape[0] < n:
            h = np.block([[h, h], [h, -h]])
        return h
    H16 = _hmat(16)
    Hp_c = ic(_hmat(128).astype(NPBF), "c_hp")
    Hm_c = ic((-_hmat(128)).astype(NPBF), "c_hm")
    H64_c = ic(_hmat(64).astype(NPBF), "c_h64")

    dbg = {}
    es = ExitStack()
    ec = es.enter_context

    buf_R = ec(nc.sbuf_tensor("buf_R", [128, 16384], F32))
    buf_L = ec(nc.sbuf_tensor("buf_L", [128, 16384], BF16))
    buf_A = ec(nc.sbuf_tensor("buf_A", [128, 16384], BF16))
    buf_B = ec(nc.sbuf_tensor("buf_B", [128, 16384], BF16))
    buf_W = ec(nc.sbuf_tensor("buf_W", [128, 8192], BF16))
    buf_S = ec(nc.sbuf_tensor("buf_S", [128, 3072], BF16))
    buf_K = ec(nc.sbuf_tensor("buf_K", [128, 2048], BF16))
    scr = ec(nc.sbuf_tensor("scr", [128, 1024], F32))
    rows = ec(nc.sbuf_tensor("rows", [1, 2048], F32))
    tabs = ec(nc.sbuf_tensor("tabs", [128, 2048], BF16))
    mask_sb = ec(nc.sbuf_tensor("mask_sb", [128, 512], BF16))
    bo_sb = ec(nc.sbuf_tensor("bo_sb", [128, 16], F32))
    bv_sb = ec(nc.sbuf_tensor("bv_sb", [128, 16], F32))
    vec16 = ec(nc.sbuf_tensor("vec16", [1, 128], F32))
    bq_sb = ec(nc.sbuf_tensor("bq_sb", [128, 16], F32))
    bk_sb = ec(nc.sbuf_tensor("bk_sb", [128, 16], F32))
    qn_sb = ec(nc.sbuf_tensor("qn_sb", [128, 4], F32))
    su_sb = ec(nc.sbuf_tensor("su_sb", [128, 16], F32))
    sudp_sb = ec(nc.sbuf_tensor("sudp_sb", [64, 128], F32))
    wsu_sb = ec(nc.sbuf_tensor("wsu_sb", [128, 64], F32))
    wsg_sb = ec(nc.sbuf_tensor("wsg_sb", [128, 64], F32))
    wsdp_sb = ec(nc.sbuf_tensor("wsdp_sb", [128, 16], F32))
    negB_sb = ec(nc.sbuf_tensor("negB_sb", [128, 16], F32))
    eps_sb = ec(nc.sbuf_tensor("eps_sb", [1, 1], F32))
    oc_f = ec(nc.sbuf_tensor("oc_f", [128, 1], F32))
    oc_b = ec(nc.sbuf_tensor("oc_b", [128, 1], BF16))
    or_f = ec(nc.sbuf_tensor("or_f", [1, 128], F32))
    id_sb = ec(nc.sbuf_tensor("id_sb", [128, 128], BF16))
    rot_sb = ec(nc.sbuf_tensor("rot_sb", [128, 128], BF16))
    hp_sb = ec(nc.sbuf_tensor("hp_sb", [128, 128], BF16))
    hm_sb = ec(nc.sbuf_tensor("hm_sb", [128, 128], BF16))
    h64_sb = ec(nc.sbuf_tensor("h64_sb", [64, 64], BF16))

    ps = ec(nc.psum_tensor("ps", [128, 4096], F32))
    sems = {s: ec(nc.semaphore("sem_" + s)) for s in SEMS}

    def Rk(k): return buf_R[:, k * 1024:(k + 1) * 1024]
    def Lk(k): return buf_L[:, k * 1024:(k + 1) * 1024]
    def Ak(k): return buf_A[:, k * 1024:(k + 1) * 1024]
    def Bk(k): return buf_B[:, k * 1024:(k + 1) * 1024]
    def Wslot(i): return buf_W[:, i * 2048:(i + 1) * 2048]
    def Sslot(i): return buf_S[:, i * 1024:(i + 1) * 1024]
    def Kslot(j): return buf_K[:, j * 1024:j * 1024 + 512]
    def Vslot(j): return buf_K[:, j * 1024 + 512:(j + 1) * 1024]

    resid_ev = [None] * NCH
    ln_ev = [None] * NCH

    def d(eng, out, in_, waits=()):
        sem = {"gpsimd": "dg", "sync": "ds", "scalar": "dc"}[eng]
        w = list(waits) + [p.guards.get("coll_last")]
        return p.op(eng, lambda e, o=out, i=in_: e.dma_start(out=o, in_=i),
                    waits=w, inc=sem)

    def coll(kind, op, groups, in_t, out_t, waits, in_ap=None, out_ap=None):
        ia = in_ap if in_ap is not None else in_t.ap()
        oa = out_ap if out_ap is not None else out_t.ap()
        if DBG_NO_COLL:
            n = oa.shape[0] // ia.shape[0]
            rows_ = ia.shape[0]
            ev = None
            for i in range(n):
                ev = d("gpsimd", oa[i * rows_:(i + 1) * rows_, :], ia,
                       waits=list(waits))
            return p.op("gpsimd", lambda g: g.engine_nop(), waits=[ev], inc="cc")
        w = list(waits) + [(s, p.plan[s]) for s in DMA_SEMS if p.plan[s] > 0]
        ev = p.op("gpsimd", lambda g, k=kind, o=op, gr=groups, ia=ia, oa=oa:
                  g.collective_compute(k, o, replica_groups=gr,
                                       ins=[ia], outs=[oa]),
                  waits=w, inc="cc")
        p.gs("coll_last", ev)
        return ev

    def tap(tag, src_ap, shape, waits, dt=F32):
        if tag not in _TAPS or tag in dbg:
            return
        t = nc.dram_tensor("dbg_" + tag, list(shape), dt, kind="ExternalOutput").ap()
        dbg[tag] = True
        d("sync", t, src_ap, waits=list(waits))

    ev_init = []
    for dst, src in ((oc_f[:, :], ones_col_f), (oc_b[:, :], ones_col_b),
                     (or_f[:, :], ones_row_f),
                     (id_sb[:, :], ident_b), (rot_sb[:, :], rotT_c),
                     (hp_sb[:, :], Hp_c), (hm_sb[:, :], Hm_c),
                     (h64_sb[:, :], H64_c),
                     (bq_sb[:, :], ap["bq_a"]), (bk_sb[:, :], ap["bk_a"]),
                     (qn_sb[:, 0:1], ap["qng"]), (qn_sb[:, 1:2], ap["qnb"]),
                     (qn_sb[:, 2:3], ap["kng"]), (qn_sb[:, 3:4], ap["knb"]),
                     (su_sb[:, :], ap["su_ug"]), (sudp_sb[:, :], ap["su_dp_re"]),
                     (wsu_sb[:, :], ap["wsu"]), (wsg_sb[:, :], ap["wsg"]),
                     (wsdp_sb[:, :], ap["wsdp"]), (mask_sb[:, :], ap["maskT"]),
                     (bo_sb[:, :], ap["bo_col"]), (bv_sb[:, :], ap["bv_col"])):
        ev_init.append(d("scalar", dst, src))
    ev_memset = p.op("vector", lambda v: v.memset(eps_sb[:, :], EPS), inc="ve")
    ev_init.append(ev_memset)
    EVI = list(ev_init)

    for k in range(NCH):
        resid_ev[k] = d("sync", Rk(k), ap["xT"][k * 128:(k + 1) * 128, :])

    def w2_precompute():
        ev_st1 = []
        for bi in range(8):
            half = bi % 2
            src = ap["w_downT_sl"].rearrange("(b a pp) r -> b pp a r", b=8, pp=128)[bi]
            dstA = buf_A[:, half * 2048:(half + 1) * 2048]
            evw = d("gpsimd", dstA.rearrange("pp (a r) -> pp a r", r=256), src,
                    waits=[p.gw(("w2in", half))] + EVI)
            evac_evs = []
            for a8 in range(8):
                lo = (a8 % 4) * 256 + (a8 // 4) * 1024
                evm = p.op("tensor", lambda t, h=half, a=a8, lo=lo: t.matmul(
                    ps[:, lo:lo + 256], hp_sb[:, :],
                    buf_A[:, h * 2048 + a * 256:h * 2048 + (a + 1) * 256],
                    start=True, stop=True),
                    waits=[evw] + p.ps_w(lo, lo + 256), inc="pe")
                eve = p.op("vector", lambda v, h=half, a=a8, lo=lo: v.tensor_scalar(
                    buf_A[:, 4096 + h * 2048 + a * 256:4096 + h * 2048 + (a + 1) * 256],
                    ps[:, lo:lo + 256], HAD_N2, None, op0=ALU.mult),
                    waits=[evm, p.gw(("w2out", half))], inc="ve")
                p.ps_r(lo, lo + 256, eve)
                evac_evs.append(eve)
            dstv = vd.ap().rearrange("(b a pp) r -> b pp a r", b=8, pp=128)[bi]
            srcv = buf_A[:, 4096 + half * 2048:4096 + (half + 1) * 2048]
            evd = d("sync", dstv, srcv.rearrange("pp (a r) -> pp a r", r=256),
                    waits=evac_evs)
            p.gs(("w2in", half), evd)
            p.gs(("w2out", half), evd)
            ev_st1.append(evd)
        ev_out = []
        for pb in range(32):
            half = pb % 2
            src = vd.ap().rearrange("(a pp) r -> pp a r", pp=128)[pb * 4:(pb + 1) * 4]
            src = src.rearrange("q a r -> a q r")
            dst = buf_A[0:64, 8192 + half * 1024:8192 + (half + 1) * 1024]
            evw = d("sync", dst.rearrange("a (q r) -> a q r", r=256), src,
                    waits=[p.gw(("w2in2", half))] + ev_st1)
            lo = 2048 + half * 1024
            mevs = []
            for hh in range(2):
                mevs.append(p.op("tensor", lambda t, h=half, hh=hh, lo=lo: t.matmul(
                    ps[0:64, lo + hh * 512:lo + (hh + 1) * 512], h64_sb[:, :],
                    buf_A[0:64, 8192 + h * 1024 + hh * 512:8192 + h * 1024 + (hh + 1) * 512],
                    start=True, stop=True),
                    waits=[evw] + p.ps_w(lo, lo + 1024) + EVI, inc="pe"))
            evs = []
            for q in range(4):
                P = pb * 4 + q
                eve = p.op("vector", lambda v, h=half, q=q, P=P, lo=lo: v.tensor_scalar(
                    buf_A[0:64, 12288 + h * 1024 + q * 256:12288 + h * 1024 + (q + 1) * 256],
                    ps[0:64, lo + q * 256:lo + (q + 1) * 256],
                    sudp_sb[:, P:P + 1], None, op0=ALU.mult),
                    waits=mevs + [p.gw(("w2out2", half))] + EVI, inc="ve")
                evs.append(eve)
            p.ps_r(lo, lo + 1024, evs[-1])
            for q in range(4):
                P = pb * 4 + q
                dstw = wpp_own.ap().rearrange("(a pp) r -> pp a r", pp=128)[P]
                evd = d("sync", dstw,
                        buf_A[0:64, 12288 + half * 1024 + q * 256:12288 + half * 1024 + (q + 1) * 256].rearrange(
                            "a (o r) -> a o r", o=1)[:, 0, :] if False else
                        buf_A[0:64, 12288 + half * 1024 + q * 256:12288 + half * 1024 + (q + 1) * 256],
                        waits=evs)
                ev_out.append(evd)
            p.gs(("w2in2", half), ev_out[-1])
            p.gs(("w2out2", half), ev_out[-1])
        return ev_out

    w2_out_evs = [] if DBG_NO_W2 else w2_precompute()
    W2 = {"ev": None}
    if not DBG_NO_W2:
        W2["ev"] = coll("AllGather", ALU.bypass, [list(range(8))],
                        wpp_own, wpp_all, w2_out_evs)

    def emit_layernorm(su_fold):
        mm_last = None
        for k in range(NCH):
            sl = k % 3
            evq = p.op("scalar", lambda s, k=k, sl=sl: s.square(Sslot(sl), Rk(k)),
                       waits=[resid_ev[k], p.gw(("sslot", sl))], inc="se")
            for hh in range(2):
                w1 = [resid_ev[k]] + (p.ps_w(2048, 3072) if k == 0 else [])
                p.op("tensor", lambda t, k=k, hh=hh: t.matmul(
                    ps[0:1, 2048 + hh * 512:2048 + (hh + 1) * 512], oc_f[:, :],
                    Rk(k)[:, hh * 512:(hh + 1) * 512],
                    start=(k == 0), stop=(k == NCH - 1)), waits=w1, inc="pe")
                w2 = [evq] + (p.ps_w(3072, 4096) if k == 0 else [])
                mm_last = p.op("tensor", lambda t, k=k, hh=hh, sl=sl: t.matmul(
                    ps[0:1, 3072 + hh * 512:3072 + (hh + 1) * 512], oc_b[:, :],
                    Sslot(sl)[:, hh * 512:(hh + 1) * 512],
                    start=(k == 0), stop=(k == NCH - 1)), waits=w2, inc="pe")
            p.gs(("sslot", sl), mm_last)
        ev_mu = p.op("vector", lambda v: v.tensor_scalar(
            rows[0:1, 0:1024], ps[0:1, 2048:3072], 1.0 / 2048.0, None, op0=ALU.mult),
            waits=[mm_last, p.gw("rows")], inc="ve")
        ev_var = p.op("vector", lambda v: v.tensor_scalar(
            rows[0:1, 1024:2048], ps[0:1, 3072:4096], 1.0 / 2048.0, None, op0=ALU.mult),
            waits=[mm_last], inc="ve")
        p.ps_r(2048, 4096, ev_var)
        ev_t = p.op("vector", lambda v: v.tensor_mul(
            scr[0:1, 0:1024], rows[0:1, 0:1024], rows[0:1, 0:1024]),
            waits=[ev_mu, p.gw("scr")], inc="ve")
        ev_var2 = p.op("vector", lambda v: v.tensor_sub(
            rows[0:1, 1024:2048], rows[0:1, 1024:2048], scr[0:1, 0:1024]),
            waits=[ev_var, ev_t], inc="ve")
        ev_sd = p.op("scalar", lambda s: s.activation(
            out=scr[0:1, 0:1024], in_=rows[0:1, 1024:2048], func=AF.Sqrt,
            bias=eps_sb[0:1, 0:1]),
            waits=[ev_var2, ev_memset], inc="se")
        ev_rstd = p.op("vector", lambda v: v.reciprocal(
            rows[0:1, 1024:2048], scr[0:1, 0:1024]), waits=[ev_sd], inc="ve")
        bevs = []
        for hh in range(2):
            bevs.append(p.op("tensor", lambda t, hh=hh: t.matmul(
                ps[:, hh * 512:(hh + 1) * 512], or_f[0:1, :],
                rows[0:1, hh * 512:(hh + 1) * 512], start=True, stop=True),
                waits=[ev_mu] + p.ps_w(hh * 512, (hh + 1) * 512), inc="pe"))
            bevs.append(p.op("tensor", lambda t, hh=hh: t.matmul(
                ps[:, 1024 + hh * 512:1024 + (hh + 1) * 512], or_f[0:1, :],
                rows[0:1, 1024 + hh * 512:1024 + (hh + 1) * 512], start=True, stop=True),
                waits=[ev_rstd] + p.ps_w(1024 + hh * 512, 1024 + (hh + 1) * 512),
                inc="pe"))
        p.gs("rows", bevs[-1])
        for k in range(NCH):
            ev1 = p.op("vector", lambda v, k=k: v.tensor_sub(
                scr[:, :], Rk(k), ps[:, 0:1024]),
                waits=bevs + [resid_ev[k], p.gw("scr")], inc="ve")
            if su_fold:
                ev2 = p.op("vector", lambda v, k=k: v.scalar_tensor_tensor(
                    Lk(k), scr[:, :], su_sb[:, k:k + 1], ps[:, 1024:2048],
                    op0=ALU.mult, op1=ALU.mult),
                    waits=[ev1] + EVI, inc="ve")
            else:
                ev2 = p.op("vector", lambda v, k=k: v.tensor_tensor(
                    Lk(k), scr[:, :], ps[:, 1024:2048], op=ALU.mult),
                    waits=[ev1], inc="ve")
            p.gs("scr", ev2)
            ln_ev[k] = ev2
        p.ps_r(0, 2048, ln_ev[NCH - 1])

    def rope_chunk(dst_ap, src_ev, tab_ev):
        if DBG_NO_ROPE:
            return src_ev
        lo = 2048
        rot_evs = []
        for hh in range(2):
            rot_evs.append(p.op("tensor", lambda t, hh=hh: t.matmul(
                ps[:, lo + hh * 512:lo + (hh + 1) * 512], rot_sb[:, :],
                dst_ap[:, hh * 512:(hh + 1) * 512], start=True, stop=True),
                waits=[src_ev] + p.ps_w(lo, lo + 1024) + EVI, inc="pe"))
        ev1 = p.op("vector", lambda v: v.tensor_tensor(
            Sslot(2), ps[:, lo:lo + 1024], tabs[:, 1024:2048], op=ALU.mult),
            waits=rot_evs + [p.gw(("sslot", 2))] + tab_ev, inc="ve")
        p.ps_r(lo, lo + 1024, ev1)
        ev2 = p.op("vector", lambda v: v.tensor_tensor(
            dst_ap, dst_ap, tabs[:, 0:1024], op=ALU.mult),
            waits=rot_evs + tab_ev, inc="ve")
        ev3 = p.op("vector", lambda v: v.tensor_add(dst_ap, dst_ap, Sslot(2)),
                   waits=[ev1, ev2], inc="ve")
        p.gs(("sslot", 2), ev3)
        return ev3

    def norm2max_chunk(src_ap, src_ev, vec_off):
        evq = p.op("scalar", lambda s: s.square(Sslot(2), src_ap),
                   waits=[src_ev, p.gw(("sslot", 2))], inc="se")
        mevs = []
        for hh in range(2):
            mevs.append(p.op("tensor", lambda t, hh=hh: t.matmul(
                ps[0:1, 3072 + hh * 512:3072 + (hh + 1) * 512], oc_b[:, :],
                Sslot(2)[:, hh * 512:(hh + 1) * 512], start=True, stop=True),
                waits=[evq] + p.ps_w(3072, 4096), inc="pe"))
        ev = p.op("vector", lambda v, vo=vec_off: v.reduce_max(
            vec16[0:1, vo:vo + 1], ps[0:1, 3072:4096], axis=AX.X),
            waits=mevs, inc="ve")
        p.ps_r(3072, 4096, ev)
        p.gs(("sslot", 2), ev)
        return ev

    def attn_sublayer(var):
        is_a = (var == "a")
        n_kch = 16 if is_a else 4
        fw = 512 if is_a else 128
        NR = 2048 if is_a else 512
        VOFF = NR + 128
        VROWS = 2048 if is_a else 512
        kv_own = kvA_own if is_a else kvB_own

        tev1 = d("sync", tabs[:, 0:1024], ap["cos_" + var], waits=[p.gw("tabs")])
        tev2 = d("sync", tabs[:, 1024:2048], ap["sin_" + var], waits=[p.gw("tabs")])
        tab_ev = [tev1, tev2]

        emit_layernorm(False)
        tap("ln_" + var, buf_L[:, :], (128, 16384), list(ln_ev), BF16)
        p.barrier()

        if is_a:
            w_k_src = lambda m: ap["wk_a"].rearrange(
                "(k pp) (mm f) -> pp k mm f", pp=128, f=128)[:, :, m, :]
            w_q_src = lambda m: ap["wq_a"].rearrange(
                "(k pp) (mm f) -> pp k mm f", pp=128, f=128)[:, :, m, :]
        else:
            w_k_src = lambda m: ap["wqkv_b"].rearrange(
                "(k pp) (mm f) -> pp k mm f", pp=128, f=128)[:, :, 16 + m, :]
            w_q_src = lambda m: ap["wqkv_b"].rearrange(
                "(k pp) (mm f) -> pp k mm f", pp=128, f=128)[:, :, m, :]

        def proj_mms(lhs_fn, evw, po):
            evs = []
            ln_last = ln_ev[NCH - 1]
            for k in range(NCH):
                for hh in range(2):
                    w = [evw, ln_last]
                    if k == 0:
                        w += p.ps_w(po + hh * 512, po + (hh + 1) * 512)
                    evs.append(p.op("tensor", lambda t, k=k, hh=hh, po=po: t.matmul(
                        ps[:, po + hh * 512:po + (hh + 1) * 512],
                        lhs_fn(k),
                        Lk(k)[:, hh * 512:(hh + 1) * 512],
                        start=(k == 0), stop=(k == NCH - 1)), waits=w, inc="pe"))
            return evs

        def qk_ln_ip(buf, g_col, b_col, ev_in):
            evq = p.op("scalar", lambda s: s.square(Sslot(2), buf),
                       waits=[ev_in, p.gw(("sslot", 2))], inc="se")
            mevs = []
            for hh in range(2):
                mevs.append(p.op("tensor", lambda t, hh=hh: t.matmul(
                    ps[0:1, 2048 + hh * 512:2048 + (hh + 1) * 512], oc_b[:, :],
                    buf[:, hh * 512:(hh + 1) * 512], start=True, stop=True),
                    waits=[ev_in] + p.ps_w(2048, 3072), inc="pe"))
                mevs.append(p.op("tensor", lambda t, hh=hh: t.matmul(
                    ps[0:1, 3072 + hh * 512:3072 + (hh + 1) * 512], oc_b[:, :],
                    Sslot(2)[:, hh * 512:(hh + 1) * 512], start=True, stop=True),
                    waits=[evq] + p.ps_w(3072, 4096), inc="pe"))
            ev_mu = p.op("vector", lambda v: v.tensor_scalar(
                rows[0:1, 0:1024], ps[0:1, 2048:3072], 1.0 / 128.0, None,
                op0=ALU.mult), waits=mevs + [p.gw("rows")], inc="ve")
            ev_msq = p.op("vector", lambda v: v.tensor_scalar(
                rows[0:1, 1024:2048], ps[0:1, 3072:4096], 1.0 / 128.0, None,
                op0=ALU.mult), waits=mevs, inc="ve")
            p.gs(("sslot", 2), ev_msq)
            ev_t = p.op("vector", lambda v: v.tensor_mul(
                scr[0:1, 0:1024], rows[0:1, 0:1024], rows[0:1, 0:1024]),
                waits=[ev_mu, p.gw("scr")], inc="ve")
            ev_var = p.op("vector", lambda v: v.tensor_sub(
                rows[0:1, 1024:2048], rows[0:1, 1024:2048], scr[0:1, 0:1024]),
                waits=[ev_msq, ev_t], inc="ve")
            ev_sd = p.op("scalar", lambda s: s.activation(
                out=scr[0:1, 0:1024], in_=rows[0:1, 1024:2048], func=AF.Sqrt,
                bias=eps_sb[0:1, 0:1]), waits=[ev_var, ev_memset], inc="se")
            ev_rstd = p.op("vector", lambda v: v.reciprocal(
                rows[0:1, 1024:2048], scr[0:1, 0:1024]), waits=[ev_sd], inc="ve")
            bevs = []
            for hh in range(2):
                bevs.append(p.op("tensor", lambda t, hh=hh: t.matmul(
                    ps[:, 2048 + hh * 512:2048 + (hh + 1) * 512], or_f[0:1, :],
                    rows[0:1, hh * 512:(hh + 1) * 512], start=True, stop=True),
                    waits=[ev_mu] + p.ps_w(2048, 3072), inc="pe"))
                bevs.append(p.op("tensor", lambda t, hh=hh: t.matmul(
                    ps[:, 3072 + hh * 512:3072 + (hh + 1) * 512], or_f[0:1, :],
                    rows[0:1, 1024 + hh * 512:1024 + (hh + 1) * 512],
                    start=True, stop=True),
                    waits=[ev_rstd] + p.ps_w(3072, 4096), inc="pe"))
            p.gs("rows", bevs[-1])
            ev1 = p.op("vector", lambda v: v.tensor_sub(
                scr[:, :], buf, ps[:, 2048:3072]), waits=bevs, inc="ve")
            ev2 = p.op("vector", lambda v: v.tensor_tensor(
                scr[:, :], scr[:, :], ps[:, 3072:4096], op=ALU.mult),
                waits=[ev1], inc="ve")
            p.ps_r(2048, 4096, ev2)
            ev3 = p.op("vector", lambda v, g=g_col, b=b_col: v.tensor_scalar(
                buf, scr[:, :], qn_sb[:, g:g + 1], qn_sb[:, b:b + 1],
                op0=ALU.mult, op1=ALU.add),
                waits=[ev2] + EVI, inc="ve")
            p.gs("scr", ev3)
            return ev3

        def post_chunk(buf, m, is_q):
            if is_a:
                ev = p.gw(("evac", is_q, m))
            else:
                ev = qk_ln_ip(buf, 0 if is_q else 2, 1 if is_q else 3,
                              p.gw(("evac", is_q, m)))
            ev_rope = rope_chunk(buf, ev, tab_ev)
            ev_nrm = norm2max_chunk(buf, ev_rope, (m if is_q else 16 + m))
            return ev_rope, ev_nrm

        ship_evs = []
        nrm_evs = []
        for m in range(n_kch + 1):
            if m < n_kch:
                wsl = m % 4
                evw = d("gpsimd", Wslot(wsl).rearrange("pp (k f) -> pp k f", f=128),
                        w_k_src(m), waits=[p.gw(("wslot", wsl))])
                po = (m % 2) * 1024
                mm_evs = proj_mms(
                    lambda k, wsl=wsl: Wslot(wsl)[:, k * 128:(k + 1) * 128], evw, po)
                p.gs(("wslot", wsl), mm_evs[-1])
                raw = m % 2
                if is_a:
                    ev_evac = p.op("vector", lambda v, m=m, po=po, raw=raw: v.tensor_scalar(
                        Sslot(raw), ps[:, po:po + 1024], bk_sb[:, m:m + 1], None,
                        op0=ALU.add),
                        waits=mm_evs + [p.gw(("sslot", raw))] + EVI, inc="ve")
                else:
                    ev_evac = p.op("vector", lambda v, po=po, raw=raw: v.tensor_copy(
                        Sslot(raw), ps[:, po:po + 1024]),
                        waits=mm_evs + [p.gw(("sslot", raw))], inc="ve")
                p.ps_r(po, po + 1024, ev_evac)
                p.gs(("evac", False, m), ev_evac)
            if m >= 1:
                mp = m - 1
                raw = mp % 2
                ev_rope, ev_nrm = post_chunk(Sslot(raw), mp, False)
                nrm_evs.append(ev_nrm)
                ev_ship = d("sync", kv_own.ap()[mp * 128:(mp + 1) * 128, :],
                            Sslot(raw), waits=[ev_rope, ev_nrm])
                p.gs(("sslot", raw), ev_ship)
                ship_evs.append(ev_ship)
        ev_nrm_ship = d("gpsimd", kv_own.ap()[NR:NR + 1, 0:n_kch],
                        vec16[0:1, 16:16 + n_kch], waits=nrm_evs)

        def v_stage(sl, extra=()):
            reg = sl % 2
            if is_a:
                vsrc = ap["wv_a"].rearrange(
                    "(k pp) (s f) -> pp k s f", pp=128, f=512)[:, :, sl, :]
            else:
                vsrc = ap["wqkv_b"].rearrange(
                    "(k pp) (s f) -> pp k s f", pp=128, f=128)[:, :, 20 + sl, :]
            vstg = buf_B[:, reg * 8192:reg * 8192 + 16 * fw]
            return vstg, d(
                "gpsimd", vstg.rearrange("pp (k f) -> pp k f", f=fw), vsrc,
                waits=[p.gw(("bufB", reg)), p.gw(("bufBv", reg))] + list(extra))
        vstg_evs = {0: v_stage(0), 1: v_stage(1)}
        preW_evs = []
        for k in range(4):
            preW_evs.append(d("gpsimd",
                              Wslot(k).rearrange("pp (kk f) -> pp kk f", f=128),
                              w_q_src(k), waits=[p.gw(("wslot", k))]))
        if is_a:
            ev_agK = coll("AllGather", ALU.bypass, [[0, 1, 2, 3], [4, 5, 6, 7]],
                          None, None, ship_evs + [ev_nrm_ship],
                          in_ap=kv_own.ap()[0:VOFF, :], out_ap=kA_all.ap())
        v_tile_evs = []
        for sl in range(4):
            reg = sl % 2
            vstg, evw = vstg_evs[sl]
            for tt in range(NQT):
                lo = 2048 + reg * 512
                mm = []
                for k in range(NCH):
                    w = [evw, ln_ev[NCH - 1]]
                    if k == 0:
                        w += p.ps_w(lo, lo + fw)
                    mm.append(p.op("tensor", lambda t, tt=tt, k=k, lo=lo, vstg=vstg: t.matmul(
                        ps[:, lo:lo + fw], Lk(k)[:, tt * 128:(tt + 1) * 128],
                        vstg[:, k * fw:(k + 1) * fw],
                        start=(k == 0), stop=(k == NCH - 1)), waits=w, inc="pe"))
                ac = tt * 2048 + sl * fw
                ev_evac = p.op("vector", lambda v, lo=lo, ac=ac: v.tensor_copy(
                    buf_A[:, ac:ac + fw], ps[:, lo:lo + fw]),
                    waits=[mm[-1], p.gw(("abuf", tt * 2))], inc="ve")
                p.ps_r(lo, lo + fw, ev_evac)
                v_tile_evs.append((tt, sl, ev_evac))
            p.gs(("bufBv", reg), mm[-1])
            if sl + 2 < 4:
                vstg_evs[sl + 2] = v_stage(sl + 2, extra=[mm[-1]])
        if is_a:
            v_ship_view = kv_own.ap()[VOFF:VOFF + VROWS, :].rearrange(
                "(t h) c -> t (h c)", h=2)
        else:
            v_ship_view = kv_own.ap()[VOFF:VOFF + VROWS, :].rearrange(
                "t (h c) -> (t h) c", h=2)
        vship_evs = []
        for tt, sl, ev in v_tile_evs:
            ac = tt * 2048 + sl * fw
            ev_ship = d("sync",
                        v_ship_view[tt * 128:(tt + 1) * 128, sl * fw:(sl + 1) * fw],
                        buf_A[:, ac:ac + fw], waits=[ev])
            vship_evs.append(ev_ship)
            p.gs(("abuf", tt * 2), ev_ship)
            p.gs(("abuf", tt * 2 + 1), ev_ship)

        pre_evs = {}
        for k in range(4, 8):
            dst = buf_B[:, k * 2048:(k + 1) * 2048]
            pre_evs[k] = d("gpsimd", dst.rearrange("pp (kk f) -> pp kk f", f=128),
                           w_q_src(k), waits=[p.gw(("bufBv", 1))])

        if is_a:
            ev_agV = coll("AllGather", ALU.bypass, [[0, 1, 2, 3], [4, 5, 6, 7]],
                          None, None, vship_evs,
                          in_ap=kv_own.ap()[VOFF:VOFF + VROWS, :], out_ap=vA_all.ap())
            ev_ag = ev_agV
        else:
            ev_ag = coll("AllGather", ALU.bypass, [[0, 1, 2, 3], [4, 5, 6, 7]],
                         kv_own, kvB_all, ship_evs + vship_evs + [ev_nrm_ship])
            ev_agK = ev_ag
            ev_agV = ev_ag

        rope_evs = []
        nq_evs = []
        LAG = 1 if is_a else 2
        for m in range(16 + LAG):
            if m < 16:
                po = (m % 2) * 1024
                if m < 4:
                    mm_evs = proj_mms(
                        lambda k, m=m: Wslot(m)[:, k * 128:(k + 1) * 128],
                        preW_evs[m], po)
                    p.gs(("wslot", m), mm_evs[-1])
                elif m < 8:
                    lhs_fn = lambda k, m=m: buf_B[:, m * 2048 + k * 128:m * 2048 + (k + 1) * 128]
                    mm_evs = proj_mms(lhs_fn, pre_evs[m], po)
                else:
                    wsl = m % 4
                    evw = d("gpsimd", Wslot(wsl).rearrange("pp (k f) -> pp k f", f=128),
                            w_q_src(m), waits=[p.gw(("wslot", wsl))])
                    mm_evs = proj_mms(
                        lambda k, wsl=wsl: Wslot(wsl)[:, k * 128:(k + 1) * 128], evw, po)
                    p.gs(("wslot", wsl), mm_evs[-1])
                if is_a:
                    ev_evac = p.op("vector", lambda v, m=m, po=po: v.tensor_scalar(
                        Ak(m), ps[:, po:po + 1024], bq_sb[:, m:m + 1], None,
                        op0=ALU.add),
                        waits=mm_evs + [p.gw(("abuf", m))] + EVI, inc="ve")
                else:
                    ev_evac = p.op("vector", lambda v, m=m, po=po: v.tensor_copy(
                        Ak(m), ps[:, po:po + 1024]),
                        waits=mm_evs + [p.gw(("abuf", m))], inc="ve")
                p.ps_r(po, po + 1024, ev_evac)
                p.gs(("evac", True, m), ev_evac)
            if m >= LAG:
                mp = m - LAG
                ev_rope, ev_nq = post_chunk(Ak(mp), mp, True)
                rope_evs.append(ev_rope)
                nq_evs.append(ev_nq)
        p.gs("tabs", rope_evs[-1])
        tap("q_" + var, buf_A[:, :], (128, 16384), rope_evs, BF16)

        SEGK = VOFF
        k_all_t = kA_all if is_a else kvB_all
        SEGF = VOFF if is_a else 1152
        ev_mkload = d("gpsimd", vec16[0:1, 64:64 + 4 * n_kch],
                      k_all_t.ap().rearrange("(r s) q -> s r q", s=SEGF)[
                          NR:NR + 1, :, 0:n_kch],
                      waits=[ev_agK])
        ev_mk = p.op("vector", lambda v: v.reduce_max(
            vec16[0:1, 32:33], vec16[0:1, 64:64 + 4 * n_kch], axis=AX.X),
            waits=[ev_mkload], inc="ve")
        ev_b1 = p.op("vector", lambda v: v.tensor_scalar(
            vec16[0:1, 48:64], vec16[0:1, 0:16], vec16[0:1, 32:33], None,
            op0=ALU.mult), waits=[ev_mk] + nq_evs, inc="ve")
        ev_b2 = p.op("scalar", lambda s: s.activation(
            out=vec16[0:1, 0:16], in_=vec16[0:1, 48:64], func=AF.Sqrt,
            bias=eps_sb[0:1, 0:1]), waits=[ev_b1, ev_memset], inc="se")
        ev_b3 = p.op("vector", lambda v: v.tensor_scalar(
            vec16[0:1, 48:64], vec16[0:1, 0:16], -B_SAFETY * ISQ_D, None,
            op0=ALU.mult), waits=[ev_b2], inc="ve")
        ev_b4 = p.op("tensor", lambda t: t.matmul(
            ps[:, 3072:3088], or_f[0:1, :], vec16[0:1, 48:64],
            start=True, stop=True),
            waits=[ev_b3] + p.ps_w(3072, 3584), inc="pe")
        ev_negB = p.op("vector", lambda v: v.tensor_copy(
            negB_sb[:, :], ps[:, 3072:3088]), waits=[ev_b4], inc="ve")
        p.ps_r(3072, 3584, ev_negB)
        tap("negb_" + var, negB_sb[:, :], (128, 16), [ev_negB])
        p.barrier()

        z_evs = []
        k_view = k_all_t.ap().rearrange("(r s) q -> s r q", s=SEGF)
        if is_a:
            v_view = vA_all.ap().rearrange("(r s) q -> r s q", s=2048).rearrange(
                "r (t h) c -> t r (h c)", h=2)
        else:
            v_view = kvB_all.ap().rearrange("(r s) q -> r s q", s=1152)[
                :, VOFF:VOFF + VROWS, :].rearrange("r t (h c) -> (t h) r c", h=2)
        for h in range(16):
            kh = h if is_a else h // 4
            kv_evs = [None] * 8

            def emit_kv_load(kb4, kh=kh, kv_evs=kv_evs):
                j = kb4 % 2
                ksrc = k_view[kh * 128:(kh + 1) * 128, :, kb4 * 128:(kb4 + 1) * 128]
                vsrc = v_view[kb4 * 128:(kb4 + 1) * 128, :, kh * 128:(kh + 1) * 128]
                gwk = p.gw(("kpair", j))
                evk = d("sync", Kslot(j).rearrange("pp (r q) -> pp r q", q=128),
                        ksrc, waits=[gwk, ev_agK])
                evv = d("sync", Vslot(j).rearrange("pp (r f) -> pp r f", f=128),
                        vsrc, waits=[gwk, ev_agV])
                kv_evs[kb4] = (evk, evv)

            emit_kv_load(0)
            emit_kv_load(1)
            last_sums = None
            last_pv = None
            for kb in range(NKB):
                m0 = kb // 4
                dd = kb % 4
                j = m0 % 2
                so = 0 if kb % 2 == 0 else 2048
                live0 = m0 * 128
                regions = [(live0, 512), (512, 1024)] if live0 < 512 else [(live0, 1024)]
                evk, evv = kv_evs[m0]
                sc_evs = []
                for (r0, r1) in regions:
                    w = [evk] + p.ps_w(so + r0, so + r1)
                    sc_evs.append(p.op("tensor", lambda t, so=so, r0=r0, r1=r1, j=j, dd=dd, h=h: t.matmul(
                        ps[:, so + r0:so + r1], Kslot(j)[:, dd * 128:(dd + 1) * 128],
                        Ak(h)[:, r0:r1], start=True, stop=False),
                        waits=w, inc="pe"))
                ev_mask = p.op("tensor", lambda t, so=so, live0=live0, dd=dd: t.matmul(
                    ps[:, so + live0:so + live0 + 128], id_sb[:, :],
                    mask_sb[:, dd * 128:(dd + 1) * 128], start=False, stop=True),
                    waits=sc_evs + EVI, inc="pe")
                pslot = kb % 3
                exp_evs = []
                for (r0, r1) in regions:
                    exp_evs.append(p.op("scalar", lambda s, so=so, r0=r0, r1=r1, pslot=pslot, h=h: s.activation(
                        out=Sslot(pslot)[:, r0:r1], in_=ps[:, so + r0:so + r1],
                        func=AF.Exp, bias=negB_sb[:, h:h + 1], scale=ISQ_D),
                        waits=[ev_mask, ev_negB, p.gw(("sslot", pslot))], inc="se"))
                for (r0, r1) in regions:
                    p.ps_r(so + r0, so + r1, exp_evs[-1])
                for (r0, r1) in regions:
                    w = exp_evs + (p.ps_w(3072 + r0, 3072 + r1) if kb == 0 else [])
                    last_sums = p.op("tensor", lambda t, r0=r0, r1=r1, pslot=pslot, kb=kb: t.matmul(
                        ps[0:1, 3072 + r0:3072 + r1], oc_b[:, :],
                        Sslot(pslot)[:, r0:r1],
                        start=(kb == 0), stop=(kb == NKB - 1)), waits=w, inc="pe")
                for (r0, r1) in regions:
                    w = exp_evs + [evv] + (p.ps_w(1024 + r0, 1024 + r1) if kb == 0 else [])
                    last_pv = p.op("tensor", lambda t, r0=r0, r1=r1, pslot=pslot, j=j, dd=dd, kb=kb: t.matmul(
                        ps[:, 1024 + r0:1024 + r1], Vslot(j)[:, dd * 128:(dd + 1) * 128],
                        Sslot(pslot)[:, r0:r1],
                        start=(kb == 0), stop=(kb == NKB - 1)), waits=w, inc="pe")
                p.gs(("sslot", pslot), last_pv)
                if dd == 3:
                    p.gs(("kpair", j), last_pv)
                    if m0 + 2 < 8:
                        emit_kv_load(m0 + 2)
            ev_sc = p.op("scalar", lambda s: s.copy(rows[0:1, 0:1024],
                                                    ps[0:1, 3072:4096]),
                         waits=[last_sums, p.gw("rows")], inc="se")
            rb_evs = []
            for hh in range(2):
                rb_evs.append(p.op("tensor", lambda t, hh=hh: t.matmul(
                    ps[:, 2048 + hh * 512:2048 + (hh + 1) * 512], or_f[0:1, :],
                    rows[0:1, hh * 512:(hh + 1) * 512], start=True, stop=True),
                    waits=[ev_sc] + p.ps_w(2048, 3072), inc="pe"))
            p.gs("rows", rb_evs[-1])
            ev_ln = p.op("scalar", lambda s: s.activation(
                out=ps[:, 3072:4096], in_=ps[:, 2048:3072], func=AF.Ln),
                waits=rb_evs, inc="se")
            ev_rcp = p.op("scalar", lambda s: s.activation(
                out=ps[:, 2048:3072], in_=ps[:, 3072:4096], func=AF.Exp, scale=-1.0),
                waits=[ev_ln], inc="se")
            p.ps_r(3072, 4096, ev_rcp)
            ev_zc = p.op("vector", lambda v: v.tensor_copy(scr[:, :], ps[:, 1024:2048]),
                         waits=[last_pv, p.gw("scr")], inc="ve")
            p.ps_r(1024, 2048, ev_zc)
            if is_a:
                ev_zm = p.op("vector", lambda v: v.tensor_tensor(
                    scr[:, :], scr[:, :], ps[:, 2048:3072], op=ALU.mult),
                    waits=[ev_zc, ev_rcp], inc="ve")
                ev_zn = p.op("vector", lambda v, h=h: v.tensor_scalar(
                    Bk(h), scr[:, :], bv_sb[:, h:h + 1], None, op0=ALU.add),
                    waits=[ev_zm] + EVI, inc="ve")
            else:
                ev_zn = p.op("vector", lambda v, h=h: v.tensor_tensor(
                    Bk(h), scr[:, :], ps[:, 2048:3072], op=ALU.mult),
                    waits=[ev_zc, ev_rcp], inc="ve")
            p.gs("scr", ev_zn)
            p.ps_r(2048, 3072, ev_zn)
            z_evs.append(ev_zn)
        tap("z_" + var, buf_B[:, :], (128, 16384), z_evs, BF16)
        p.barrier()

        wo_ap = ap["wo_a"] if is_a else ap["wo_b"]
        for m in range(16):
            wsl = m % 4
            wsrc = wo_ap.rearrange("(k pp) (mm f) -> pp k mm f", pp=128, f=128)[:, :, m, :]
            evw = d("gpsimd", Wslot(wsl).rearrange("pp (k f) -> pp k f", f=128),
                    wsrc, waits=[p.gw(("wslot", wsl))])
            po = (m % 2) * 1024
            mm_evs = []
            for k in range(16):
                for hh in range(2):
                    w = [evw, z_evs[15]]
                    if k == 0:
                        w += p.ps_w(po + hh * 512, po + (hh + 1) * 512)
                    mm_evs.append(p.op("tensor", lambda t, k=k, hh=hh, po=po, wsl=wsl: t.matmul(
                        ps[:, po + hh * 512:po + (hh + 1) * 512],
                        Wslot(wsl)[:, k * 128:(k + 1) * 128],
                        Bk(k)[:, hh * 512:(hh + 1) * 512],
                        start=(k == 0), stop=(k == 15)), waits=w, inc="pe"))
            p.gs(("wslot", wsl), mm_evs[-1])
            if is_a:
                ev_add = p.op("vector", lambda v, m=m, po=po: v.scalar_tensor_tensor(
                    Rk(m), ps[:, po:po + 1024], bo_sb[:, m:m + 1], Rk(m),
                    op0=ALU.add, op1=ALU.add),
                    waits=mm_evs + [resid_ev[m]] + EVI, inc="ve")
            else:
                ev_add = p.op("vector", lambda v, m=m, po=po: v.tensor_tensor(
                    Rk(m), Rk(m), ps[:, po:po + 1024], op=ALU.add),
                    waits=mm_evs + [resid_ev[m]], inc="ve")
            p.ps_r(po, po + 1024, ev_add)
            resid_ev[m] = ev_add
        p.barrier()
        for reg in range(2):
            p.gs(("bufB", reg), resid_ev[15])
            p.gs(("bufBv", reg), resid_ev[15])
        for m in range(16):
            p.gs(("abuf", m), resid_ev[15])

    def ffn_sublayer():
        emit_layernorm(True)
        p.barrier()
        xp_evs = []
        for a2 in range(16):
            po = (a2 % 2) * 1024
            mm_evs = []
            for a in range(16):
                hsb = hp_sb if H16[a2, a] > 0 else hm_sb
                for hh in range(2):
                    w = [ln_ev[NCH - 1]] + EVI
                    if a == 0:
                        w += p.ps_w(po + hh * 512, po + (hh + 1) * 512)
                    mm_evs.append(p.op("tensor", lambda t, a=a, hh=hh, po=po, hsb=hsb: t.matmul(
                        ps[:, po + hh * 512:po + (hh + 1) * 512], hsb[:, :],
                        Lk(a)[:, hh * 512:(hh + 1) * 512],
                        start=(a == 0), stop=(a == 15)), waits=w, inc="pe"))
            ev_e = p.op("vector", lambda v, a2=a2, po=po: v.tensor_scalar(
                Bk(a2), ps[:, po:po + 1024], HAD_N1, None, op0=ALU.mult),
                waits=mm_evs + [p.gw(("bbuf", a2))], inc="ve")
            p.ps_r(po, po + 1024, ev_e)
            xp_evs.append(ev_e)
        tap("xp", buf_B[:, :], (128, 16384), xp_evs, BF16)
        p.barrier()

        for q in range(4):
            h_evs = []
            for icl in range(16):
                I = q * 16 + icl
                upsrc = ap["w_upT"].rearrange(
                    "(k pp) (i f) -> pp k i f", pp=128, f=128)[:, :, I, :]
                gtsrc = ap["w_gateT"].rearrange(
                    "(k pp) (i f) -> pp k i f", pp=128, f=128)[:, :, I, :]
                su_ = icl % 2
                sg_ = 2 + icl % 2
                evwu = d("gpsimd", Wslot(su_).rearrange("pp (k f) -> pp k f", f=128),
                         upsrc, waits=[p.gw(("wslot", su_))])
                evwg = d("gpsimd", Wslot(sg_).rearrange("pp (k f) -> pp k f", f=128),
                         gtsrc, waits=[p.gw(("wslot", sg_))])
                po_u = (icl % 2) * 1024
                po_g = 2048 + (icl % 2) * 1024
                mmu, mmg = [], []
                for k in range(16):
                    for hh in range(2):
                        wu = [evwu, xp_evs[15]]
                        wg = [evwg, xp_evs[15]]
                        if k == 0:
                            wu += p.ps_w(po_u + hh * 512, po_u + (hh + 1) * 512)
                            wg += p.ps_w(po_g + hh * 512, po_g + (hh + 1) * 512)
                        mmu.append(p.op("tensor", lambda t, k=k, hh=hh, po=po_u, su_=su_: t.matmul(
                            ps[:, po + hh * 512:po + (hh + 1) * 512],
                            Wslot(su_)[:, k * 128:(k + 1) * 128],
                            Bk(k)[:, hh * 512:(hh + 1) * 512],
                            start=(k == 0), stop=(k == 15)), waits=wu, inc="pe"))
                        mmg.append(p.op("tensor", lambda t, k=k, hh=hh, po=po_g, sg_=sg_: t.matmul(
                            ps[:, po + hh * 512:po + (hh + 1) * 512],
                            Wslot(sg_)[:, k * 128:(k + 1) * 128],
                            Bk(k)[:, hh * 512:(hh + 1) * 512],
                            start=(k == 0), stop=(k == 15)), waits=wg, inc="pe"))
                p.gs(("wslot", su_), mmu[-1])
                p.gs(("wslot", sg_), mmg[-1])
                ssl = icl % 3
                ev_silu = p.op("scalar", lambda s, po=po_g, I=I, ssl=ssl: s.activation(
                    out=Sslot(ssl), in_=ps[:, po:po + 1024], func=AF.Silu,
                    scale=wsg_sb[:, I:I + 1]),
                    waits=mmg + [p.gw(("sslot", ssl))] + EVI, inc="se")
                ev_h = p.op("vector", lambda v, po=po_u, I=I, icl=icl, ssl=ssl: v.scalar_tensor_tensor(
                    Ak(icl), ps[:, po:po + 1024], wsu_sb[:, I:I + 1], Sslot(ssl),
                    op0=ALU.mult, op1=ALU.mult),
                    waits=[ev_silu, p.gw(("abuf", icl))] + mmu + EVI, inc="ve")
                p.ps_r(po_u, po_u + 1024, ev_h)
                p.ps_r(po_g, po_g + 1024, ev_h)
                p.gs(("sslot", ssl), ev_h)
                h_evs.append(ev_h)
            for m in range(16):
                wsl = m % 4
                s_rank = m // 2
                wsrc = wpp_all.ap()[s_rank * 8192 + q * 2048:s_rank * 8192 + (q + 1) * 2048,
                                    (m % 2) * 128:(m % 2) * 128 + 128]
                wsrc = wsrc.rearrange("(i pp) f -> pp i f", pp=128)
                evw = d("sync", Wslot(wsl).rearrange("pp (i f) -> pp i f", f=128),
                        wsrc, waits=[p.gw(("wslot", wsl)), W2["ev"]])
                po = (m % 2) * 1024
                mm_evs = []
                for icl in range(16):
                    for hh in range(2):
                        w = [evw, h_evs[15]]
                        if icl == 0:
                            w += p.ps_w(po + hh * 512, po + (hh + 1) * 512)
                        mm_evs.append(p.op("tensor", lambda t, icl=icl, hh=hh, po=po, wsl=wsl: t.matmul(
                            ps[:, po + hh * 512:po + (hh + 1) * 512],
                            Wslot(wsl)[:, icl * 128:(icl + 1) * 128],
                            Ak(icl)[:, hh * 512:(hh + 1) * 512],
                            start=(icl == 0), stop=(icl == 15)), waits=w, inc="pe"))
                p.gs(("wslot", wsl), mm_evs[-1])
                ev_add = p.op("vector", lambda v, m=m, po=po: v.scalar_tensor_tensor(
                    Rk(m), ps[:, po:po + 1024], wsdp_sb[:, m:m + 1], Rk(m),
                    op0=ALU.mult, op1=ALU.add),
                    waits=mm_evs + [resid_ev[m]] + EVI, inc="ve")
                p.ps_r(po, po + 1024, ev_add)
                resid_ev[m] = ev_add
            for icl in range(16):
                p.gs(("abuf", icl), resid_ev[15])
        for reg in range(2):
            p.gs(("bufB", reg), resid_ev[15])
        p.barrier()

    attn_sublayer("a")
    tap("r_a", buf_R[:, :], (128, 16384), list(resid_ev))
    ffn_sublayer()
    tap("r_f1", buf_R[:, :], (128, 16384), list(resid_ev))
    attn_sublayer("b")
    tap("r_b", buf_R[:, :], (128, 16384), list(resid_ev))
    ffn_sublayer()

    st_evs = [d("sync", out_x[k * 128:(k + 1) * 128, :], Rk(k),
                waits=[resid_ev[k]]) for k in range(NCH)]
    p.op("gpsimd", lambda g: g.engine_nop(), waits=st_evs)

    with nc.Block() as block:
        OWN = {} if DBG_KEEP_SELF else {"tensor": "pe"}

        def emitter(name):
            own = OWN.get(name)
            def run(eng):
                seen = {}
                for fn, ws, inc, amt in p.ops[name]:
                    for s, v in sorted(ws.items()):
                        if s == own:
                            continue
                        if seen.get(s, 0) < v:
                            eng.wait_ge(sems[s], v)
                            seen[s] = v
                    inst = fn(eng)
                    if inc is not None:
                        inst.then_inc(sems[inc], amt)
            return run
        block.tensor(emitter("tensor"))
        block.vector(emitter("vector"))
        block.scalar(emitter("scalar"))
        block.gpsimd(emitter("gpsimd"))
        block.sync(emitter("sync"))

    es.close()
    return nc, dbg


_cached = {}


def kernel(**inputs):
    maps = _prep_inputs(inputs)
    key = tuple(sorted(_TAPS))
    if key not in _cached:
        _cached[key] = build_program()
    nc, dbg = _cached[key]
    res = run_bass_kernel_spmd(nc, maps, core_ids=list(range(8)))
    out = _assemble(res.results)
    if dbg:
        kernel._dbg = [{t: np.asarray(r["dbg_" + t]) for t in dbg} for r in res.results]
    return out


kernel._dbg = None


# version 11
# speedup vs baseline: 1.0436x; 1.0258x over previous
import numpy as np
import ml_dtypes
from contextlib import ExitStack

import concourse.bass as bass
import concourse.mybir as mybir
from concourse.bass_utils import run_bass_kernel_spmd

F32 = mybir.dt.float32
BF16 = mybir.dt.bfloat16
AF = mybir.ActivationFunctionType
ALU = mybir.AluOpType
AX = mybir.AxisListType
NPBF = ml_dtypes.bfloat16

NCH = 16
NQT = 8
NKB = 32
ISQ_D = float(1.0 / np.sqrt(128.0))
HAD_N1 = float(1.0 / np.sqrt(2048.0))
HAD_N2 = float(1.0 / np.sqrt(8192.0))
EPS = 1e-5
B_SAFETY = 1.05
NEG_BIG = -30000.0

_TAPS = set()
import os as _os
DBG_NO_W2 = bool(int(_os.environ.get("K_DBG_NO_W2", "0")))
DBG_NO_ROPE = bool(int(_os.environ.get("K_DBG_NO_ROPE", "0")))
DBG_NO_COLL = bool(int(_os.environ.get("K_DBG_NO_COLL", "0")))
DBG_KEEP_SELF = bool(int(_os.environ.get("K_DBG_KEEP_SELF", "0")))

ENGS = ("tensor", "vector", "scalar", "gpsimd", "sync")
SEMS = ("dg", "ds", "dc", "pe", "ve", "se", "cc")
DMA_SEMS = ("dg", "ds", "dc")


class Prog:
    def __init__(self):
        self.ops = {e: [] for e in ENGS}
        self.plan = {s: 0 for s in SEMS}
        self.pending = {e: {} for e in ENGS}
        self.guards = {}

    def op(self, eng, fn, waits=(), inc=None, amt=None):
        ws = dict(self.pending[eng])
        self.pending[eng] = {}
        for w in waits:
            if w is None:
                continue
            s, v = w
            ws[s] = max(ws.get(s, 0), v)
        ev = None
        if inc is not None:
            if amt is None:
                amt = 16 if inc in DMA_SEMS else 1
            self.plan[inc] += amt
            ev = (inc, self.plan[inc])
        self.ops[eng].append((fn, ws, inc, amt))
        return ev

    def barrier(self):
        evs = [(s, self.plan[s]) for s in SEMS if self.plan[s] > 0]
        for e in ("tensor", "vector", "scalar"):
            dd = self.pending[e]
            for s, v in evs:
                dd[s] = max(dd.get(s, 0), v)

    def gw(self, key):
        return self.guards.get(key)

    def gs(self, key, ev):
        self.guards[key] = ev

    def ps_w(self, lo, hi):
        return [self.gw(("psb", b)) for b in range(lo // 512, (hi + 511) // 512)]

    def ps_r(self, lo, hi, ev):
        for b in range(lo // 512, (hi + 511) // 512):
            self.gs(("psb", b), ev)



def _rope_tables(positions, base):
    inv = 1.0 / base ** (np.arange(0, 128, 2, dtype=np.float64) / 128.0)
    ang = np.outer(inv, positions.astype(np.float64))
    cos = np.concatenate([np.cos(ang), np.cos(ang)], axis=0)
    sin = np.concatenate([np.sin(ang), np.sin(ang)], axis=0)
    return cos.astype(NPBF), sin.astype(NPBF)


def _mask_tile(r):
    kk = np.arange(128)[:, None]
    qi = np.arange(128)[None, :]
    tiles = []
    for dd in range(4):
        allowed = (128 * dd + kk) <= (128 * r + qi)
        tiles.append(np.where(allowed, 0.0, NEG_BIG))
    return np.concatenate(tiles, axis=1).astype(NPBF)


def _prep_inputs(inp):
    asc = np.ascontiguousarray
    x = np.asarray(inp["L0_attn_a_normalized_resid_pre"], dtype=np.float32)
    perm = np.concatenate([np.arange(0, 128, 2), np.arange(1, 128, 2)])

    wq = np.asarray(inp["L0_attn_a_W_Q"], np.float32)[:, :, perm]
    wk = np.asarray(inp["L0_attn_a_W_K"], np.float32)[:, :, perm]
    wv = np.asarray(inp["L0_attn_a_W_V"], np.float32)
    shared = {
        "wq_a": asc(wq.transpose(1, 0, 2).reshape(2048, 2048)),
        "wk_a": asc(wk.transpose(1, 0, 2).reshape(2048, 2048)),
        "wv_a": asc(wv.transpose(1, 0, 2).reshape(2048, 2048)),
        "wo_a": asc(np.asarray(inp["L0_attn_a_W_O"], np.float32).reshape(2048, 2048)),
        "bq_a": asc(np.asarray(inp["L0_attn_a_b_Q"], np.float32)[:, perm].T),
        "bk_a": asc(np.asarray(inp["L0_attn_a_b_K"], np.float32)[:, perm].T),
        "bv_col": asc(np.asarray(inp["L0_attn_a_b_V"], np.float32).reshape(16, 128).T),
        "bo_col": asc(np.asarray(inp["L0_attn_a_b_O"], np.float32).reshape(16, 128).T),
        "wqkv_b": asc(np.asarray(inp["L0_attn_b_w_qkv"], np.float32).T),
        "wo_b": asc(np.asarray(inp["L0_attn_b_w_o"], np.float32).T),
        "qng": asc(np.asarray(inp["L0_attn_b_qn_g"], np.float32).reshape(128, 1)),
        "qnb": asc(np.asarray(inp["L0_attn_b_qn_b"], np.float32).reshape(128, 1)),
        "kng": asc(np.asarray(inp["L0_attn_b_kn_g"], np.float32).reshape(128, 1)),
        "knb": asc(np.asarray(inp["L0_attn_b_kn_b"], np.float32).reshape(128, 1)),
        "w_upT": asc(np.asarray(inp["L0_ffn_W_up"], np.float32).T),
        "w_gateT": asc(np.asarray(inp["L0_ffn_W_gate"], np.float32).T),
        "su_ug": asc(np.asarray(inp["L0_ffn_SU_ug"], np.float32).reshape(16, 128).T),
        "su_dp_re": asc(np.asarray(inp["L0_ffn_SU_dp"], np.float32).reshape(64, 128)),
        "wsu": asc(np.asarray(inp["L0_ffn_Wscale_ug"], np.float32)[:8192].reshape(64, 128).T),
        "wsg": asc(np.asarray(inp["L0_ffn_Wscale_ug"], np.float32)[8192:].reshape(64, 128).T),
        "wsdp": asc(np.asarray(inp["L0_ffn_Wscale_dp"], np.float32).reshape(16, 128).T),
    }
    w_downT = np.asarray(inp["L0_ffn_W_down"], np.float32).T
    w_downT_c = asc(w_downT)
    maps = []
    for c in range(8):
        g, r = divmod(c, 4)
        blocks = [4 * m + r for m in range(8)]
        pos = np.concatenate([np.arange(128) + 128 * b for b in blocks])
        xc = x[g].reshape(32, 128, 2048)[blocks].reshape(1024, 2048)
        cos_a, sin_a = _rope_tables(pos, 10000.0)
        cos_b, sin_b = _rope_tables(pos, 1000000.0)
        m = dict(shared)
        m["xT"] = asc(xc.T)
        m["w_downT_sl"] = w_downT_c
        m["cos_a"] = asc(cos_a); m["sin_a"] = asc(sin_a)
        m["cos_b"] = asc(cos_b); m["sin_b"] = asc(sin_b)
        m["maskT"] = _mask_tile(r)
        maps.append(m)
    return maps


def _assemble(results):
    out = np.empty((2, 4096, 2048), np.float32)
    for c in range(8):
        g, r = divmod(c, 4)
        xc = np.asarray(results[c]["out_x"]).T.reshape(8, 128, 2048)
        for mm in range(8):
            b = 4 * mm + r
            out[g, b * 128:(b + 1) * 128, :] = xc[mm]
    return out



def build_program():
    nc = bass.Bass("TRN2", target_bir_lowering=False, debug=False)
    p = Prog()

    ap = {}
    def ein(name, shape, dt=F32):
        ap[name] = nc.dram_tensor(name, list(shape), dt, kind="ExternalInput").ap()
    ein("xT", (2048, 1024))
    for n in ("wq_a", "wk_a", "wv_a", "wo_a"):
        ein(n, (2048, 2048))
    ein("bq_a", (128, 16)); ein("bk_a", (128, 16))
    ein("bv_col", (128, 16)); ein("bo_col", (128, 16))
    ein("wqkv_b", (2048, 3072)); ein("wo_b", (2048, 2048))
    for n in ("qng", "qnb", "kng", "knb"):
        ein(n, (128, 1))
    ein("w_upT", (2048, 8192)); ein("w_gateT", (2048, 8192))
    ein("w_downT_sl", (8192, 2048))
    ein("su_ug", (128, 16)); ein("su_dp_re", (64, 128))
    ein("wsu", (128, 64)); ein("wsg", (128, 64)); ein("wsdp", (128, 16))
    for n in ("cos_a", "sin_a", "cos_b", "sin_b"):
        ein(n, (128, 1024), BF16)
    ein("maskT", (128, 512), BF16)

    out_x = nc.dram_tensor("out_x", [2048, 1024], F32, kind="ExternalOutput").ap()

    kvA_own = nc.dram_tensor("kvA_own", [4224, 1024], BF16, kind="Internal")
    kA_all = nc.dram_tensor("kA_all", [4 * 2176, 1024], BF16, kind="Internal")
    vA_all = nc.dram_tensor("vA_all", [4 * 2048, 1024], BF16, kind="Internal")
    kvB_own = nc.dram_tensor("kvB_own", [1152, 1024], BF16, kind="Internal")
    kvB_all = nc.dram_tensor("kvB_all", [4 * 1152, 1024], BF16, kind="Internal")
    vd = nc.dram_tensor("vd", [8192, 2048], BF16, kind="Internal")
    wpp = nc.dram_tensor("wpp", [8192, 2048], BF16, kind="Internal")

    ic = lambda a, n: nc.inline_tensor(np.ascontiguousarray(a), name=n).ap()
    ones_col_f = ic(np.ones((128, 1), np.float32), "c_ocf")
    ones_col_b = ic(np.ones((128, 1), NPBF), "c_ocb")
    ones_row_f = ic(np.ones((1, 128), np.float32), "c_orf")
    ident_b = ic(np.eye(128, dtype=NPBF), "c_id")
    rotT_np = np.zeros((128, 128), np.float32)
    for i in range(64):
        rotT_np[i, 64 + i] = 1.0
        rotT_np[64 + i, i] = -1.0
    rotT_c = ic(rotT_np.astype(NPBF), "c_rotT")

    def _hmat(n):
        h = np.array([[1.0]])
        while h.shape[0] < n:
            h = np.block([[h, h], [h, -h]])
        return h
    H16 = _hmat(16)
    Hp_c = ic(_hmat(128).astype(NPBF), "c_hp")
    Hm_c = ic((-_hmat(128)).astype(NPBF), "c_hm")
    H64_c = ic(_hmat(64).astype(NPBF), "c_h64")

    dbg = {}
    es = ExitStack()
    ec = es.enter_context

    buf_R = ec(nc.sbuf_tensor("buf_R", [128, 16384], F32))
    buf_L = ec(nc.sbuf_tensor("buf_L", [128, 16384], BF16))
    buf_A = ec(nc.sbuf_tensor("buf_A", [128, 16384], BF16))
    buf_B = ec(nc.sbuf_tensor("buf_B", [128, 16384], BF16))
    buf_W = ec(nc.sbuf_tensor("buf_W", [128, 8192], BF16))
    buf_S = ec(nc.sbuf_tensor("buf_S", [128, 3072], BF16))
    buf_K = ec(nc.sbuf_tensor("buf_K", [128, 2048], BF16))
    scr = ec(nc.sbuf_tensor("scr", [128, 1024], F32))
    rows = ec(nc.sbuf_tensor("rows", [1, 2048], F32))
    tabs = ec(nc.sbuf_tensor("tabs", [128, 2048], BF16))
    mask_sb = ec(nc.sbuf_tensor("mask_sb", [128, 512], BF16))
    bo_sb = ec(nc.sbuf_tensor("bo_sb", [128, 16], F32))
    bv_sb = ec(nc.sbuf_tensor("bv_sb", [128, 16], F32))
    vec16 = ec(nc.sbuf_tensor("vec16", [1, 128], F32))
    bq_sb = ec(nc.sbuf_tensor("bq_sb", [128, 16], F32))
    bk_sb = ec(nc.sbuf_tensor("bk_sb", [128, 16], F32))
    qn_sb = ec(nc.sbuf_tensor("qn_sb", [128, 4], F32))
    su_sb = ec(nc.sbuf_tensor("su_sb", [128, 16], F32))
    sudp_sb = ec(nc.sbuf_tensor("sudp_sb", [64, 128], F32))
    wsu_sb = ec(nc.sbuf_tensor("wsu_sb", [128, 64], F32))
    wsg_sb = ec(nc.sbuf_tensor("wsg_sb", [128, 64], F32))
    wsdp_sb = ec(nc.sbuf_tensor("wsdp_sb", [128, 16], F32))
    negB_sb = ec(nc.sbuf_tensor("negB_sb", [128, 16], F32))
    eps_sb = ec(nc.sbuf_tensor("eps_sb", [1, 1], F32))
    oc_f = ec(nc.sbuf_tensor("oc_f", [128, 1], F32))
    oc_b = ec(nc.sbuf_tensor("oc_b", [128, 1], BF16))
    or_f = ec(nc.sbuf_tensor("or_f", [1, 128], F32))
    id_sb = ec(nc.sbuf_tensor("id_sb", [128, 128], BF16))
    rot_sb = ec(nc.sbuf_tensor("rot_sb", [128, 128], BF16))
    hp_sb = ec(nc.sbuf_tensor("hp_sb", [128, 128], BF16))
    hm_sb = ec(nc.sbuf_tensor("hm_sb", [128, 128], BF16))
    h64_sb = ec(nc.sbuf_tensor("h64_sb", [64, 64], BF16))

    ps = ec(nc.psum_tensor("ps", [128, 4096], F32))
    sems = {s: ec(nc.semaphore("sem_" + s)) for s in SEMS}

    def Rk(k): return buf_R[:, k * 1024:(k + 1) * 1024]
    def Lk(k): return buf_L[:, k * 1024:(k + 1) * 1024]
    def Ak(k): return buf_A[:, k * 1024:(k + 1) * 1024]
    def Bk(k): return buf_B[:, k * 1024:(k + 1) * 1024]
    def Wslot(i): return buf_W[:, i * 2048:(i + 1) * 2048]
    def Sslot(i): return buf_S[:, i * 1024:(i + 1) * 1024]
    def Kslot(j): return buf_K[:, j * 1024:j * 1024 + 512]
    def Vslot(j): return buf_K[:, j * 1024 + 512:(j + 1) * 1024]

    resid_ev = [None] * NCH
    ln_ev = [None] * NCH

    def d(eng, out, in_, waits=()):
        sem = {"gpsimd": "dg", "sync": "ds", "scalar": "dc"}[eng]
        w = list(waits) + [p.guards.get("coll_last")]
        return p.op(eng, lambda e, o=out, i=in_: e.dma_start(out=o, in_=i),
                    waits=w, inc=sem)

    def coll(kind, op, groups, in_t, out_t, waits, in_ap=None, out_ap=None):
        ia = in_ap if in_ap is not None else in_t.ap()
        oa = out_ap if out_ap is not None else out_t.ap()
        if DBG_NO_COLL:
            n = oa.shape[0] // ia.shape[0]
            rows_ = ia.shape[0]
            ev = None
            for i in range(n):
                ev = d("gpsimd", oa[i * rows_:(i + 1) * rows_, :], ia,
                       waits=list(waits))
            return p.op("gpsimd", lambda g: g.engine_nop(), waits=[ev], inc="cc")
        w = list(waits) + [(s, p.plan[s]) for s in DMA_SEMS if p.plan[s] > 0]
        ev = p.op("gpsimd", lambda g, k=kind, o=op, gr=groups, ia=ia, oa=oa:
                  g.collective_compute(k, o, replica_groups=gr,
                                       ins=[ia], outs=[oa]),
                  waits=w, inc="cc")
        p.gs("coll_last", ev)
        return ev

    def tap(tag, src_ap, shape, waits, dt=F32):
        if tag not in _TAPS or tag in dbg:
            return
        t = nc.dram_tensor("dbg_" + tag, list(shape), dt, kind="ExternalOutput").ap()
        dbg[tag] = True
        d("sync", t, src_ap, waits=list(waits))

    ev_init = []
    for dst, src in ((oc_f[:, :], ones_col_f), (oc_b[:, :], ones_col_b),
                     (or_f[:, :], ones_row_f),
                     (id_sb[:, :], ident_b), (rot_sb[:, :], rotT_c),
                     (hp_sb[:, :], Hp_c), (hm_sb[:, :], Hm_c),
                     (h64_sb[:, :], H64_c),
                     (bq_sb[:, :], ap["bq_a"]), (bk_sb[:, :], ap["bk_a"]),
                     (qn_sb[:, 0:1], ap["qng"]), (qn_sb[:, 1:2], ap["qnb"]),
                     (qn_sb[:, 2:3], ap["kng"]), (qn_sb[:, 3:4], ap["knb"]),
                     (su_sb[:, :], ap["su_ug"]), (sudp_sb[:, :], ap["su_dp_re"]),
                     (wsu_sb[:, :], ap["wsu"]), (wsg_sb[:, :], ap["wsg"]),
                     (wsdp_sb[:, :], ap["wsdp"]), (mask_sb[:, :], ap["maskT"]),
                     (bo_sb[:, :], ap["bo_col"]), (bv_sb[:, :], ap["bv_col"])):
        ev_init.append(d("scalar", dst, src))
    ev_memset = p.op("vector", lambda v: v.memset(eps_sb[:, :], EPS), inc="ve")
    ev_init.append(ev_memset)
    EVI = list(ev_init)

    for k in range(NCH):
        resid_ev[k] = d("sync", Rk(k), ap["xT"][k * 128:(k + 1) * 128, :])

    def w2_precompute():
        ev_st1 = []
        for a in range(64):
            half = a % 2
            src_ = ap["w_downT_sl"][a * 128:(a + 1) * 128, :]
            dstA = buf_A[:, half * 2048:(half + 1) * 2048]
            evw = d("gpsimd", dstA, src_, waits=[p.gw(("w2in", half))] + EVI)
            mevs = []
            for hh in range(4):
                lo = half * 2048 + hh * 512
                mevs.append(p.op("tensor", lambda t, h=half, hh=hh, lo=lo: t.matmul(
                    ps[:, lo:lo + 512], hp_sb[:, :],
                    buf_A[:, h * 2048 + hh * 512:h * 2048 + (hh + 1) * 512],
                    start=True, stop=True),
                    waits=[evw] + p.ps_w(lo, lo + 512), inc="pe"))
            eng = "scalar" if a % 2 == 0 else "vector"
            if eng == "scalar":
                eve = p.op("scalar", lambda s, h=half: s.mul(
                    buf_A[:, 4096 + h * 2048:4096 + (h + 1) * 2048],
                    ps[:, h * 2048:(h + 1) * 2048], HAD_N2),
                    waits=mevs + [p.gw(("w2out", half))], inc="se")
            else:
                eve = p.op("vector", lambda v, h=half: v.tensor_scalar(
                    buf_A[:, 4096 + h * 2048:4096 + (h + 1) * 2048],
                    ps[:, h * 2048:(h + 1) * 2048], HAD_N2, None, op0=ALU.mult),
                    waits=mevs + [p.gw(("w2out", half))], inc="ve")
            p.ps_r(half * 2048, (half + 1) * 2048, eve)
            evd = d("sync", vd.ap()[a * 128:(a + 1) * 128, :],
                    buf_A[:, 4096 + half * 2048:4096 + (half + 1) * 2048],
                    waits=[eve])
            p.gs(("w2in", half), evd)
            p.gs(("w2out", half), evd)
            ev_st1.append(evd)
        ev_out = []
        for P in range(128):
            half = P % 4
            src_ = vd.ap().rearrange("(a pp) r -> pp a r", pp=128)[P]
            dst = buf_A[0:64, 8192 + half * 2048:8192 + (half + 1) * 2048]
            evw = d("sync", dst, src_, waits=[p.gw(("w2in2", half))] + ev_st1)
            lo = half % 2 * 2048
            mevs = []
            for hh in range(4):
                mevs.append(p.op("tensor", lambda t, h=half, hh=hh, lo=lo: t.matmul(
                    ps[0:64, lo + hh * 512:lo + (hh + 1) * 512], h64_sb[:, :],
                    buf_A[0:64, 8192 + h * 2048 + hh * 512:8192 + h * 2048 + (hh + 1) * 512],
                    start=True, stop=True),
                    waits=[evw] + p.ps_w(lo + hh * 512, lo + (hh + 1) * 512), inc="pe"))
            eve = p.op("vector", lambda v, h=half, P=P, lo=lo: v.tensor_scalar(
                buf_A[0:64, 8192 + h * 2048:8192 + (h + 1) * 2048],
                ps[0:64, lo:lo + 2048], sudp_sb[:, P:P + 1], None, op0=ALU.mult),
                waits=mevs + EVI, inc="ve")
            p.ps_r(lo, lo + 2048, eve)
            evd = d("sync", wpp.ap().rearrange("(a pp) r -> pp a r", pp=128)[P],
                    buf_A[0:64, 8192 + half * 2048:8192 + (half + 1) * 2048],
                    waits=[eve])
            p.gs(("w2in2", half), evd)
            ev_out.append(evd)
        return ev_out

    w2_out_evs = [] if DBG_NO_W2 else w2_precompute()
    W2 = {"ev": w2_out_evs[-1] if w2_out_evs else None}

    def emit_layernorm(su_fold):
        mm_last = None
        for k in range(NCH):
            sl = k % 3
            evq = p.op("scalar", lambda s, k=k, sl=sl: s.square(Sslot(sl), Rk(k)),
                       waits=[resid_ev[k], p.gw(("sslot", sl))], inc="se")
            for hh in range(2):
                w1 = [resid_ev[k]] + (p.ps_w(2048, 3072) if k == 0 else [])
                p.op("tensor", lambda t, k=k, hh=hh: t.matmul(
                    ps[0:1, 2048 + hh * 512:2048 + (hh + 1) * 512], oc_f[:, :],
                    Rk(k)[:, hh * 512:(hh + 1) * 512],
                    start=(k == 0), stop=(k == NCH - 1)), waits=w1, inc="pe")
                w2 = [evq] + (p.ps_w(3072, 4096) if k == 0 else [])
                mm_last = p.op("tensor", lambda t, k=k, hh=hh, sl=sl: t.matmul(
                    ps[0:1, 3072 + hh * 512:3072 + (hh + 1) * 512], oc_b[:, :],
                    Sslot(sl)[:, hh * 512:(hh + 1) * 512],
                    start=(k == 0), stop=(k == NCH - 1)), waits=w2, inc="pe")
            p.gs(("sslot", sl), mm_last)
        ev_mu = p.op("vector", lambda v: v.tensor_scalar(
            rows[0:1, 0:1024], ps[0:1, 2048:3072], 1.0 / 2048.0, None, op0=ALU.mult),
            waits=[mm_last, p.gw("rows")], inc="ve")
        ev_var = p.op("vector", lambda v: v.tensor_scalar(
            rows[0:1, 1024:2048], ps[0:1, 3072:4096], 1.0 / 2048.0, None, op0=ALU.mult),
            waits=[mm_last], inc="ve")
        p.ps_r(2048, 4096, ev_var)
        ev_t = p.op("vector", lambda v: v.tensor_mul(
            scr[0:1, 0:1024], rows[0:1, 0:1024], rows[0:1, 0:1024]),
            waits=[ev_mu, p.gw("scr")], inc="ve")
        ev_var2 = p.op("vector", lambda v: v.tensor_sub(
            rows[0:1, 1024:2048], rows[0:1, 1024:2048], scr[0:1, 0:1024]),
            waits=[ev_var, ev_t], inc="ve")
        ev_sd = p.op("scalar", lambda s: s.activation(
            out=scr[0:1, 0:1024], in_=rows[0:1, 1024:2048], func=AF.Sqrt,
            bias=eps_sb[0:1, 0:1]),
            waits=[ev_var2, ev_memset], inc="se")
        ev_rstd = p.op("vector", lambda v: v.reciprocal(
            rows[0:1, 1024:2048], scr[0:1, 0:1024]), waits=[ev_sd], inc="ve")
        bevs = []
        for hh in range(2):
            bevs.append(p.op("tensor", lambda t, hh=hh: t.matmul(
                ps[:, hh * 512:(hh + 1) * 512], or_f[0:1, :],
                rows[0:1, hh * 512:(hh + 1) * 512], start=True, stop=True),
                waits=[ev_mu] + p.ps_w(hh * 512, (hh + 1) * 512), inc="pe"))
            bevs.append(p.op("tensor", lambda t, hh=hh: t.matmul(
                ps[:, 1024 + hh * 512:1024 + (hh + 1) * 512], or_f[0:1, :],
                rows[0:1, 1024 + hh * 512:1024 + (hh + 1) * 512], start=True, stop=True),
                waits=[ev_rstd] + p.ps_w(1024 + hh * 512, 1024 + (hh + 1) * 512),
                inc="pe"))
        p.gs("rows", bevs[-1])
        for k in range(NCH):
            ev1 = p.op("vector", lambda v, k=k: v.tensor_sub(
                scr[:, :], Rk(k), ps[:, 0:1024]),
                waits=bevs + [resid_ev[k], p.gw("scr")], inc="ve")
            if su_fold:
                ev2 = p.op("vector", lambda v, k=k: v.scalar_tensor_tensor(
                    Lk(k), scr[:, :], su_sb[:, k:k + 1], ps[:, 1024:2048],
                    op0=ALU.mult, op1=ALU.mult),
                    waits=[ev1] + EVI, inc="ve")
            else:
                ev2 = p.op("vector", lambda v, k=k: v.tensor_tensor(
                    Lk(k), scr[:, :], ps[:, 1024:2048], op=ALU.mult),
                    waits=[ev1], inc="ve")
            p.gs("scr", ev2)
            ln_ev[k] = ev2
        p.ps_r(0, 2048, ln_ev[NCH - 1])

    def rope_chunk(dst_ap, src_ev, tab_ev):
        if DBG_NO_ROPE:
            return src_ev
        lo = 2048
        rot_evs = []
        for hh in range(2):
            rot_evs.append(p.op("tensor", lambda t, hh=hh: t.matmul(
                ps[:, lo + hh * 512:lo + (hh + 1) * 512], rot_sb[:, :],
                dst_ap[:, hh * 512:(hh + 1) * 512], start=True, stop=True),
                waits=[src_ev] + p.ps_w(lo, lo + 1024) + EVI, inc="pe"))
        ev1 = p.op("vector", lambda v: v.tensor_tensor(
            Sslot(2), ps[:, lo:lo + 1024], tabs[:, 1024:2048], op=ALU.mult),
            waits=rot_evs + [p.gw(("sslot", 2))] + tab_ev, inc="ve")
        p.ps_r(lo, lo + 1024, ev1)
        ev2 = p.op("vector", lambda v: v.tensor_tensor(
            dst_ap, dst_ap, tabs[:, 0:1024], op=ALU.mult),
            waits=rot_evs + tab_ev, inc="ve")
        ev3 = p.op("vector", lambda v: v.tensor_add(dst_ap, dst_ap, Sslot(2)),
                   waits=[ev1, ev2], inc="ve")
        p.gs(("sslot", 2), ev3)
        return ev3

    def norm2max_chunk(src_ap, src_ev, vec_off):
        evq = p.op("scalar", lambda s: s.square(Sslot(2), src_ap),
                   waits=[src_ev, p.gw(("sslot", 2))], inc="se")
        mevs = []
        for hh in range(2):
            mevs.append(p.op("tensor", lambda t, hh=hh: t.matmul(
                ps[0:1, 3072 + hh * 512:3072 + (hh + 1) * 512], oc_b[:, :],
                Sslot(2)[:, hh * 512:(hh + 1) * 512], start=True, stop=True),
                waits=[evq] + p.ps_w(3072, 4096), inc="pe"))
        ev = p.op("vector", lambda v, vo=vec_off: v.reduce_max(
            vec16[0:1, vo:vo + 1], ps[0:1, 3072:4096], axis=AX.X),
            waits=mevs, inc="ve")
        p.ps_r(3072, 4096, ev)
        p.gs(("sslot", 2), ev)
        return ev

    def attn_sublayer(var):
        is_a = (var == "a")
        n_kch = 16 if is_a else 4
        fw = 512 if is_a else 128
        NR = 2048 if is_a else 512
        VOFF = NR + 128
        VROWS = 2048 if is_a else 512
        kv_own = kvA_own if is_a else kvB_own

        tev1 = d("sync", tabs[:, 0:1024], ap["cos_" + var], waits=[p.gw("tabs")])
        tev2 = d("sync", tabs[:, 1024:2048], ap["sin_" + var], waits=[p.gw("tabs")])
        tab_ev = [tev1, tev2]

        emit_layernorm(False)
        tap("ln_" + var, buf_L[:, :], (128, 16384), list(ln_ev), BF16)
        p.barrier()

        if is_a:
            w_k_src = lambda m: ap["wk_a"].rearrange(
                "(k pp) (mm f) -> pp k mm f", pp=128, f=128)[:, :, m, :]
            w_q_src = lambda m: ap["wq_a"].rearrange(
                "(k pp) (mm f) -> pp k mm f", pp=128, f=128)[:, :, m, :]
        else:
            w_k_src = lambda m: ap["wqkv_b"].rearrange(
                "(k pp) (mm f) -> pp k mm f", pp=128, f=128)[:, :, 16 + m, :]
            w_q_src = lambda m: ap["wqkv_b"].rearrange(
                "(k pp) (mm f) -> pp k mm f", pp=128, f=128)[:, :, m, :]

        def proj_mms(lhs_fn, evw, po):
            evs = []
            ln_last = ln_ev[NCH - 1]
            for k in range(NCH):
                for hh in range(2):
                    w = [evw, ln_last]
                    if k == 0:
                        w += p.ps_w(po + hh * 512, po + (hh + 1) * 512)
                    evs.append(p.op("tensor", lambda t, k=k, hh=hh, po=po: t.matmul(
                        ps[:, po + hh * 512:po + (hh + 1) * 512],
                        lhs_fn(k),
                        Lk(k)[:, hh * 512:(hh + 1) * 512],
                        start=(k == 0), stop=(k == NCH - 1)), waits=w, inc="pe"))
            return evs

        def qk_ln_ip(buf, g_col, b_col, ev_in):
            evq = p.op("scalar", lambda s: s.square(Sslot(2), buf),
                       waits=[ev_in, p.gw(("sslot", 2))], inc="se")
            mevs = []
            for hh in range(2):
                mevs.append(p.op("tensor", lambda t, hh=hh: t.matmul(
                    ps[0:1, 2048 + hh * 512:2048 + (hh + 1) * 512], oc_b[:, :],
                    buf[:, hh * 512:(hh + 1) * 512], start=True, stop=True),
                    waits=[ev_in] + p.ps_w(2048, 3072), inc="pe"))
                mevs.append(p.op("tensor", lambda t, hh=hh: t.matmul(
                    ps[0:1, 3072 + hh * 512:3072 + (hh + 1) * 512], oc_b[:, :],
                    Sslot(2)[:, hh * 512:(hh + 1) * 512], start=True, stop=True),
                    waits=[evq] + p.ps_w(3072, 4096), inc="pe"))
            ev_mu = p.op("vector", lambda v: v.tensor_scalar(
                rows[0:1, 0:1024], ps[0:1, 2048:3072], 1.0 / 128.0, None,
                op0=ALU.mult), waits=mevs + [p.gw("rows")], inc="ve")
            ev_msq = p.op("vector", lambda v: v.tensor_scalar(
                rows[0:1, 1024:2048], ps[0:1, 3072:4096], 1.0 / 128.0, None,
                op0=ALU.mult), waits=mevs, inc="ve")
            p.gs(("sslot", 2), ev_msq)
            ev_t = p.op("vector", lambda v: v.tensor_mul(
                scr[0:1, 0:1024], rows[0:1, 0:1024], rows[0:1, 0:1024]),
                waits=[ev_mu, p.gw("scr")], inc="ve")
            ev_var = p.op("vector", lambda v: v.tensor_sub(
                rows[0:1, 1024:2048], rows[0:1, 1024:2048], scr[0:1, 0:1024]),
                waits=[ev_msq, ev_t], inc="ve")
            ev_sd = p.op("scalar", lambda s: s.activation(
                out=scr[0:1, 0:1024], in_=rows[0:1, 1024:2048], func=AF.Sqrt,
                bias=eps_sb[0:1, 0:1]), waits=[ev_var, ev_memset], inc="se")
            ev_rstd = p.op("vector", lambda v: v.reciprocal(
                rows[0:1, 1024:2048], scr[0:1, 0:1024]), waits=[ev_sd], inc="ve")
            bevs = []
            for hh in range(2):
                bevs.append(p.op("tensor", lambda t, hh=hh: t.matmul(
                    ps[:, 2048 + hh * 512:2048 + (hh + 1) * 512], or_f[0:1, :],
                    rows[0:1, hh * 512:(hh + 1) * 512], start=True, stop=True),
                    waits=[ev_mu] + p.ps_w(2048, 3072), inc="pe"))
                bevs.append(p.op("tensor", lambda t, hh=hh: t.matmul(
                    ps[:, 3072 + hh * 512:3072 + (hh + 1) * 512], or_f[0:1, :],
                    rows[0:1, 1024 + hh * 512:1024 + (hh + 1) * 512],
                    start=True, stop=True),
                    waits=[ev_rstd] + p.ps_w(3072, 4096), inc="pe"))
            p.gs("rows", bevs[-1])
            ev1 = p.op("vector", lambda v: v.tensor_sub(
                scr[:, :], buf, ps[:, 2048:3072]), waits=bevs, inc="ve")
            ev2 = p.op("vector", lambda v: v.tensor_tensor(
                scr[:, :], scr[:, :], ps[:, 3072:4096], op=ALU.mult),
                waits=[ev1], inc="ve")
            p.ps_r(2048, 4096, ev2)
            ev3 = p.op("vector", lambda v, g=g_col, b=b_col: v.tensor_scalar(
                buf, scr[:, :], qn_sb[:, g:g + 1], qn_sb[:, b:b + 1],
                op0=ALU.mult, op1=ALU.add),
                waits=[ev2] + EVI, inc="ve")
            p.gs("scr", ev3)
            return ev3

        def post_chunk(buf, m, is_q):
            if is_a:
                ev = p.gw(("evac", is_q, m))
            else:
                ev = qk_ln_ip(buf, 0 if is_q else 2, 1 if is_q else 3,
                              p.gw(("evac", is_q, m)))
            ev_rope = rope_chunk(buf, ev, tab_ev)
            ev_nrm = norm2max_chunk(buf, ev_rope, (m if is_q else 16 + m))
            return ev_rope, ev_nrm

        ship_evs = []
        nrm_evs = []
        for m in range(n_kch + 1):
            if m < n_kch:
                wsl = m % 4
                evw = d("gpsimd", Wslot(wsl).rearrange("pp (k f) -> pp k f", f=128),
                        w_k_src(m), waits=[p.gw(("wslot", wsl))])
                po = (m % 2) * 1024
                mm_evs = proj_mms(
                    lambda k, wsl=wsl: Wslot(wsl)[:, k * 128:(k + 1) * 128], evw, po)
                p.gs(("wslot", wsl), mm_evs[-1])
                raw = m % 2
                if is_a:
                    ev_evac = p.op("vector", lambda v, m=m, po=po, raw=raw: v.tensor_scalar(
                        Sslot(raw), ps[:, po:po + 1024], bk_sb[:, m:m + 1], None,
                        op0=ALU.add),
                        waits=mm_evs + [p.gw(("sslot", raw))] + EVI, inc="ve")
                else:
                    ev_evac = p.op("vector", lambda v, po=po, raw=raw: v.tensor_copy(
                        Sslot(raw), ps[:, po:po + 1024]),
                        waits=mm_evs + [p.gw(("sslot", raw))], inc="ve")
                p.ps_r(po, po + 1024, ev_evac)
                p.gs(("evac", False, m), ev_evac)
            if m >= 1:
                mp = m - 1
                raw = mp % 2
                ev_rope, ev_nrm = post_chunk(Sslot(raw), mp, False)
                nrm_evs.append(ev_nrm)
                ev_ship = d("sync", kv_own.ap()[mp * 128:(mp + 1) * 128, :],
                            Sslot(raw), waits=[ev_rope, ev_nrm])
                p.gs(("sslot", raw), ev_ship)
                ship_evs.append(ev_ship)
        ev_nrm_ship = d("gpsimd", kv_own.ap()[NR:NR + 1, 0:n_kch],
                        vec16[0:1, 16:16 + n_kch], waits=nrm_evs)

        def v_stage(sl, extra=()):
            reg = sl % 2
            if is_a:
                vsrc = ap["wv_a"].rearrange(
                    "(k pp) (s f) -> pp k s f", pp=128, f=512)[:, :, sl, :]
            else:
                vsrc = ap["wqkv_b"].rearrange(
                    "(k pp) (s f) -> pp k s f", pp=128, f=128)[:, :, 20 + sl, :]
            vstg = buf_B[:, reg * 8192:reg * 8192 + 16 * fw]
            return vstg, d(
                "gpsimd", vstg.rearrange("pp (k f) -> pp k f", f=fw), vsrc,
                waits=[p.gw(("bufB", reg)), p.gw(("bufBv", reg))] + list(extra))
        vstg_evs = {0: v_stage(0), 1: v_stage(1)}
        preW_evs = []
        for k in range(4):
            preW_evs.append(d("gpsimd",
                              Wslot(k).rearrange("pp (kk f) -> pp kk f", f=128),
                              w_q_src(k), waits=[p.gw(("wslot", k))]))
        if is_a:
            ev_agK = coll("AllGather", ALU.bypass, [[0, 1, 2, 3], [4, 5, 6, 7]],
                          None, None, ship_evs + [ev_nrm_ship],
                          in_ap=kv_own.ap()[0:VOFF, :], out_ap=kA_all.ap())
        v_tile_evs = []
        for sl in range(4):
            reg = sl % 2
            vstg, evw = vstg_evs[sl]
            for tt in range(NQT):
                lo = 2048 + reg * 512
                mm = []
                for k in range(NCH):
                    w = [evw, ln_ev[NCH - 1]]
                    if k == 0:
                        w += p.ps_w(lo, lo + fw)
                    mm.append(p.op("tensor", lambda t, tt=tt, k=k, lo=lo, vstg=vstg: t.matmul(
                        ps[:, lo:lo + fw], Lk(k)[:, tt * 128:(tt + 1) * 128],
                        vstg[:, k * fw:(k + 1) * fw],
                        start=(k == 0), stop=(k == NCH - 1)), waits=w, inc="pe"))
                ac = tt * 2048 + sl * fw
                ev_evac = p.op("vector", lambda v, lo=lo, ac=ac: v.tensor_copy(
                    buf_A[:, ac:ac + fw], ps[:, lo:lo + fw]),
                    waits=[mm[-1], p.gw(("abuf", tt * 2))], inc="ve")
                p.ps_r(lo, lo + fw, ev_evac)
                v_tile_evs.append((tt, sl, ev_evac))
            p.gs(("bufBv", reg), mm[-1])
            if sl + 2 < 4:
                vstg_evs[sl + 2] = v_stage(sl + 2, extra=[mm[-1]])
        if is_a:
            v_ship_view = kv_own.ap()[VOFF:VOFF + VROWS, :].rearrange(
                "(t h) c -> t (h c)", h=2)
        else:
            v_ship_view = kv_own.ap()[VOFF:VOFF + VROWS, :].rearrange(
                "t (h c) -> (t h) c", h=2)
        vship_evs = []
        for tt, sl, ev in v_tile_evs:
            ac = tt * 2048 + sl * fw
            ev_ship = d("sync",
                        v_ship_view[tt * 128:(tt + 1) * 128, sl * fw:(sl + 1) * fw],
                        buf_A[:, ac:ac + fw], waits=[ev])
            vship_evs.append(ev_ship)
            p.gs(("abuf", tt * 2), ev_ship)
            p.gs(("abuf", tt * 2 + 1), ev_ship)

        pre_evs = {}
        for k in range(4, 8):
            dst = buf_B[:, k * 2048:(k + 1) * 2048]
            pre_evs[k] = d("gpsimd", dst.rearrange("pp (kk f) -> pp kk f", f=128),
                           w_q_src(k), waits=[p.gw(("bufBv", 1))])

        if is_a:
            ev_agV = coll("AllGather", ALU.bypass, [[0, 1, 2, 3], [4, 5, 6, 7]],
                          None, None, vship_evs,
                          in_ap=kv_own.ap()[VOFF:VOFF + VROWS, :], out_ap=vA_all.ap())
            ev_ag = ev_agV
        else:
            ev_ag = coll("AllGather", ALU.bypass, [[0, 1, 2, 3], [4, 5, 6, 7]],
                         kv_own, kvB_all, ship_evs + vship_evs + [ev_nrm_ship])
            ev_agK = ev_ag
            ev_agV = ev_ag

        rope_evs = []
        nq_evs = []
        LAG = 1 if is_a else 2
        for m in range(16 + LAG):
            if m < 16:
                po = (m % 2) * 1024
                if m < 4:
                    mm_evs = proj_mms(
                        lambda k, m=m: Wslot(m)[:, k * 128:(k + 1) * 128],
                        preW_evs[m], po)
                    p.gs(("wslot", m), mm_evs[-1])
                elif m < 8:
                    lhs_fn = lambda k, m=m: buf_B[:, m * 2048 + k * 128:m * 2048 + (k + 1) * 128]
                    mm_evs = proj_mms(lhs_fn, pre_evs[m], po)
                else:
                    wsl = m % 4
                    evw = d("gpsimd", Wslot(wsl).rearrange("pp (k f) -> pp k f", f=128),
                            w_q_src(m), waits=[p.gw(("wslot", wsl))])
                    mm_evs = proj_mms(
                        lambda k, wsl=wsl: Wslot(wsl)[:, k * 128:(k + 1) * 128], evw, po)
                    p.gs(("wslot", wsl), mm_evs[-1])
                if is_a:
                    ev_evac = p.op("vector", lambda v, m=m, po=po: v.tensor_scalar(
                        Ak(m), ps[:, po:po + 1024], bq_sb[:, m:m + 1], None,
                        op0=ALU.add),
                        waits=mm_evs + [p.gw(("abuf", m))] + EVI, inc="ve")
                else:
                    ev_evac = p.op("vector", lambda v, m=m, po=po: v.tensor_copy(
                        Ak(m), ps[:, po:po + 1024]),
                        waits=mm_evs + [p.gw(("abuf", m))], inc="ve")
                p.ps_r(po, po + 1024, ev_evac)
                p.gs(("evac", True, m), ev_evac)
            if m >= LAG:
                mp = m - LAG
                ev_rope, ev_nq = post_chunk(Ak(mp), mp, True)
                rope_evs.append(ev_rope)
                nq_evs.append(ev_nq)
        p.gs("tabs", rope_evs[-1])
        tap("q_" + var, buf_A[:, :], (128, 16384), rope_evs, BF16)

        SEGK = VOFF
        k_all_t = kA_all if is_a else kvB_all
        SEGF = VOFF if is_a else 1152
        ev_mkload = d("gpsimd", vec16[0:1, 64:64 + 4 * n_kch],
                      k_all_t.ap().rearrange("(r s) q -> s r q", s=SEGF)[
                          NR:NR + 1, :, 0:n_kch],
                      waits=[ev_agK])
        ev_mk = p.op("vector", lambda v: v.reduce_max(
            vec16[0:1, 32:33], vec16[0:1, 64:64 + 4 * n_kch], axis=AX.X),
            waits=[ev_mkload], inc="ve")
        ev_b1 = p.op("vector", lambda v: v.tensor_scalar(
            vec16[0:1, 48:64], vec16[0:1, 0:16], vec16[0:1, 32:33], None,
            op0=ALU.mult), waits=[ev_mk] + nq_evs, inc="ve")
        ev_b2 = p.op("scalar", lambda s: s.activation(
            out=vec16[0:1, 0:16], in_=vec16[0:1, 48:64], func=AF.Sqrt,
            bias=eps_sb[0:1, 0:1]), waits=[ev_b1, ev_memset], inc="se")
        ev_b3 = p.op("vector", lambda v: v.tensor_scalar(
            vec16[0:1, 48:64], vec16[0:1, 0:16], -B_SAFETY * ISQ_D, None,
            op0=ALU.mult), waits=[ev_b2], inc="ve")
        ev_b4 = p.op("tensor", lambda t: t.matmul(
            ps[:, 3072:3088], or_f[0:1, :], vec16[0:1, 48:64],
            start=True, stop=True),
            waits=[ev_b3] + p.ps_w(3072, 3584), inc="pe")
        ev_negB = p.op("vector", lambda v: v.tensor_copy(
            negB_sb[:, :], ps[:, 3072:3088]), waits=[ev_b4], inc="ve")
        p.ps_r(3072, 3584, ev_negB)
        tap("negb_" + var, negB_sb[:, :], (128, 16), [ev_negB])
        p.barrier()

        z_evs = []
        k_view = k_all_t.ap().rearrange("(r s) q -> s r q", s=SEGF)
        if is_a:
            v_view = vA_all.ap().rearrange("(r s) q -> r s q", s=2048).rearrange(
                "r (t h) c -> t r (h c)", h=2)
        else:
            v_view = kvB_all.ap().rearrange("(r s) q -> r s q", s=1152)[
                :, VOFF:VOFF + VROWS, :].rearrange("r t (h c) -> (t h) r c", h=2)
        for h in range(16):
            kh = h if is_a else h // 4
            kv_evs = [None] * 8

            def emit_kv_load(kb4, kh=kh, kv_evs=kv_evs):
                j = kb4 % 2
                ksrc = k_view[kh * 128:(kh + 1) * 128, :, kb4 * 128:(kb4 + 1) * 128]
                vsrc = v_view[kb4 * 128:(kb4 + 1) * 128, :, kh * 128:(kh + 1) * 128]
                gwk = p.gw(("kpair", j))
                evk = d("sync", Kslot(j).rearrange("pp (r q) -> pp r q", q=128),
                        ksrc, waits=[gwk, ev_agK])
                evv = d("sync", Vslot(j).rearrange("pp (r f) -> pp r f", f=128),
                        vsrc, waits=[gwk, ev_agV])
                kv_evs[kb4] = (evk, evv)

            emit_kv_load(0)
            emit_kv_load(1)
            last_sums = None
            last_pv = None
            pend = None

            def emit_sums_pv(kb, regions, exp_evs, evv, j, dd):
                nonlocal last_sums, last_pv
                for (r0, r1) in regions:
                    w = exp_evs + (p.ps_w(3072 + r0, 3072 + r1) if kb == 0 else [])
                    last_sums = p.op("tensor", lambda t, r0=r0, r1=r1, kb=kb: t.matmul(
                        ps[0:1, 3072 + r0:3072 + r1], oc_b[:, :],
                        Sslot(kb % 3)[:, r0:r1],
                        start=(kb == 0), stop=(kb == NKB - 1)), waits=w, inc="pe")
                for (r0, r1) in regions:
                    w = exp_evs + [evv] + (p.ps_w(1024 + r0, 1024 + r1) if kb == 0 else [])
                    last_pv = p.op("tensor", lambda t, r0=r0, r1=r1, j=j, dd=dd, kb=kb: t.matmul(
                        ps[:, 1024 + r0:1024 + r1], Vslot(j)[:, dd * 128:(dd + 1) * 128],
                        Sslot(kb % 3)[:, r0:r1],
                        start=(kb == 0), stop=(kb == NKB - 1)), waits=w, inc="pe")
                p.gs(("sslot", kb % 3), last_pv)
                if dd == 3:
                    p.gs(("kpair", j), last_pv)
                    m0_ = kb // 4
                    if m0_ + 2 < 8:
                        emit_kv_load(m0_ + 2)

            for kb in range(NKB):
                m0 = kb // 4
                dd = kb % 4
                j = m0 % 2
                so = 0 if kb % 2 == 0 else 2048
                live0 = m0 * 128
                regions = [(live0, 512), (512, 1024)] if live0 < 512 else [(live0, 1024)]
                evk, evv = kv_evs[m0]
                sc_evs = []
                for (r0, r1) in regions:
                    w = [evk] + p.ps_w(so + r0, so + r1)
                    sc_evs.append(p.op("tensor", lambda t, so=so, r0=r0, r1=r1, j=j, dd=dd, h=h: t.matmul(
                        ps[:, so + r0:so + r1], Kslot(j)[:, dd * 128:(dd + 1) * 128],
                        Ak(h)[:, r0:r1], start=True, stop=False),
                        waits=w, inc="pe"))
                ev_mask = p.op("tensor", lambda t, so=so, live0=live0, dd=dd: t.matmul(
                    ps[:, so + live0:so + live0 + 128], id_sb[:, :],
                    mask_sb[:, dd * 128:(dd + 1) * 128], start=False, stop=True),
                    waits=sc_evs + EVI, inc="pe")
                pslot = kb % 3
                exp_evs = []
                for (r0, r1) in regions:
                    exp_evs.append(p.op("scalar", lambda s, so=so, r0=r0, r1=r1, pslot=pslot, h=h: s.activation(
                        out=Sslot(pslot)[:, r0:r1], in_=ps[:, so + r0:so + r1],
                        func=AF.Exp, bias=negB_sb[:, h:h + 1], scale=ISQ_D),
                        waits=[ev_mask, ev_negB, p.gw(("sslot", pslot))], inc="se"))
                for (r0, r1) in regions:
                    p.ps_r(so + r0, so + r1, exp_evs[-1])
                if pend is not None:
                    emit_sums_pv(*pend)
                pend = (kb, regions, exp_evs, evv, j, dd)
            emit_sums_pv(*pend)
            ev_sc = p.op("scalar", lambda s: s.copy(rows[0:1, 0:1024],
                                                    ps[0:1, 3072:4096]),
                         waits=[last_sums, p.gw("rows")], inc="se")
            rb_evs = []
            for hh in range(2):
                rb_evs.append(p.op("tensor", lambda t, hh=hh: t.matmul(
                    ps[:, 2048 + hh * 512:2048 + (hh + 1) * 512], or_f[0:1, :],
                    rows[0:1, hh * 512:(hh + 1) * 512], start=True, stop=True),
                    waits=[ev_sc] + p.ps_w(2048, 3072), inc="pe"))
            p.gs("rows", rb_evs[-1])
            ev_ln = p.op("scalar", lambda s: s.activation(
                out=ps[:, 3072:4096], in_=ps[:, 2048:3072], func=AF.Ln),
                waits=rb_evs, inc="se")
            ev_rcp = p.op("scalar", lambda s: s.activation(
                out=ps[:, 2048:3072], in_=ps[:, 3072:4096], func=AF.Exp, scale=-1.0),
                waits=[ev_ln], inc="se")
            p.ps_r(3072, 4096, ev_rcp)
            ev_zc = p.op("vector", lambda v: v.tensor_copy(scr[:, :], ps[:, 1024:2048]),
                         waits=[last_pv, p.gw("scr")], inc="ve")
            p.ps_r(1024, 2048, ev_zc)
            if is_a:
                ev_zm = p.op("vector", lambda v: v.tensor_tensor(
                    scr[:, :], scr[:, :], ps[:, 2048:3072], op=ALU.mult),
                    waits=[ev_zc, ev_rcp], inc="ve")
                ev_zn = p.op("vector", lambda v, h=h: v.tensor_scalar(
                    Bk(h), scr[:, :], bv_sb[:, h:h + 1], None, op0=ALU.add),
                    waits=[ev_zm] + EVI, inc="ve")
            else:
                ev_zn = p.op("vector", lambda v, h=h: v.tensor_tensor(
                    Bk(h), scr[:, :], ps[:, 2048:3072], op=ALU.mult),
                    waits=[ev_zc, ev_rcp], inc="ve")
            p.gs("scr", ev_zn)
            p.ps_r(2048, 3072, ev_zn)
            z_evs.append(ev_zn)
        tap("z_" + var, buf_B[:, :], (128, 16384), z_evs, BF16)
        p.barrier()

        wo_ap = ap["wo_a"] if is_a else ap["wo_b"]
        for m in range(16):
            wsl = m % 4
            wsrc = wo_ap.rearrange("(k pp) (mm f) -> pp k mm f", pp=128, f=128)[:, :, m, :]
            evw = d("gpsimd", Wslot(wsl).rearrange("pp (k f) -> pp k f", f=128),
                    wsrc, waits=[p.gw(("wslot", wsl))])
            po = (m % 2) * 1024
            mm_evs = []
            for k in range(16):
                for hh in range(2):
                    w = [evw, z_evs[15]]
                    if k == 0:
                        w += p.ps_w(po + hh * 512, po + (hh + 1) * 512)
                    mm_evs.append(p.op("tensor", lambda t, k=k, hh=hh, po=po, wsl=wsl: t.matmul(
                        ps[:, po + hh * 512:po + (hh + 1) * 512],
                        Wslot(wsl)[:, k * 128:(k + 1) * 128],
                        Bk(k)[:, hh * 512:(hh + 1) * 512],
                        start=(k == 0), stop=(k == 15)), waits=w, inc="pe"))
            p.gs(("wslot", wsl), mm_evs[-1])
            if is_a:
                ev_add = p.op("vector", lambda v, m=m, po=po: v.scalar_tensor_tensor(
                    Rk(m), ps[:, po:po + 1024], bo_sb[:, m:m + 1], Rk(m),
                    op0=ALU.add, op1=ALU.add),
                    waits=mm_evs + [resid_ev[m]] + EVI, inc="ve")
            else:
                ev_add = p.op("vector", lambda v, m=m, po=po: v.tensor_tensor(
                    Rk(m), Rk(m), ps[:, po:po + 1024], op=ALU.add),
                    waits=mm_evs + [resid_ev[m]], inc="ve")
            p.ps_r(po, po + 1024, ev_add)
            resid_ev[m] = ev_add
        p.barrier()
        for reg in range(2):
            p.gs(("bufB", reg), resid_ev[15])
            p.gs(("bufBv", reg), resid_ev[15])
        for m in range(16):
            p.gs(("abuf", m), resid_ev[15])

    def ffn_sublayer():
        emit_layernorm(True)
        p.barrier()
        xp_evs = []
        for a2 in range(16):
            po = (a2 % 2) * 1024
            mm_evs = []
            for a in range(16):
                hsb = hp_sb if H16[a2, a] > 0 else hm_sb
                for hh in range(2):
                    w = [ln_ev[NCH - 1]] + EVI
                    if a == 0:
                        w += p.ps_w(po + hh * 512, po + (hh + 1) * 512)
                    mm_evs.append(p.op("tensor", lambda t, a=a, hh=hh, po=po, hsb=hsb: t.matmul(
                        ps[:, po + hh * 512:po + (hh + 1) * 512], hsb[:, :],
                        Lk(a)[:, hh * 512:(hh + 1) * 512],
                        start=(a == 0), stop=(a == 15)), waits=w, inc="pe"))
            ev_e = p.op("vector", lambda v, a2=a2, po=po: v.tensor_scalar(
                Bk(a2), ps[:, po:po + 1024], HAD_N1, None, op0=ALU.mult),
                waits=mm_evs + [p.gw(("bbuf", a2))], inc="ve")
            p.ps_r(po, po + 1024, ev_e)
            xp_evs.append(ev_e)
        tap("xp", buf_B[:, :], (128, 16384), xp_evs, BF16)
        p.barrier()

        for q in range(4):
            h_evs = []
            for icl in range(16):
                I = q * 16 + icl
                upsrc = ap["w_upT"].rearrange(
                    "(k pp) (i f) -> pp k i f", pp=128, f=128)[:, :, I, :]
                gtsrc = ap["w_gateT"].rearrange(
                    "(k pp) (i f) -> pp k i f", pp=128, f=128)[:, :, I, :]
                su_ = icl % 2
                sg_ = 2 + icl % 2
                evwu = d("gpsimd", Wslot(su_).rearrange("pp (k f) -> pp k f", f=128),
                         upsrc, waits=[p.gw(("wslot", su_))])
                evwg = d("gpsimd", Wslot(sg_).rearrange("pp (k f) -> pp k f", f=128),
                         gtsrc, waits=[p.gw(("wslot", sg_))])
                po_u = (icl % 2) * 1024
                po_g = 2048 + (icl % 2) * 1024
                mmu, mmg = [], []
                for k in range(16):
                    for hh in range(2):
                        wu = [evwu, xp_evs[15]]
                        wg = [evwg, xp_evs[15]]
                        if k == 0:
                            wu += p.ps_w(po_u + hh * 512, po_u + (hh + 1) * 512)
                            wg += p.ps_w(po_g + hh * 512, po_g + (hh + 1) * 512)
                        mmu.append(p.op("tensor", lambda t, k=k, hh=hh, po=po_u, su_=su_: t.matmul(
                            ps[:, po + hh * 512:po + (hh + 1) * 512],
                            Wslot(su_)[:, k * 128:(k + 1) * 128],
                            Bk(k)[:, hh * 512:(hh + 1) * 512],
                            start=(k == 0), stop=(k == 15)), waits=wu, inc="pe"))
                        mmg.append(p.op("tensor", lambda t, k=k, hh=hh, po=po_g, sg_=sg_: t.matmul(
                            ps[:, po + hh * 512:po + (hh + 1) * 512],
                            Wslot(sg_)[:, k * 128:(k + 1) * 128],
                            Bk(k)[:, hh * 512:(hh + 1) * 512],
                            start=(k == 0), stop=(k == 15)), waits=wg, inc="pe"))
                p.gs(("wslot", su_), mmu[-1])
                p.gs(("wslot", sg_), mmg[-1])
                ssl = icl % 3
                ev_silu = p.op("scalar", lambda s, po=po_g, I=I, ssl=ssl: s.activation(
                    out=Sslot(ssl), in_=ps[:, po:po + 1024], func=AF.Silu,
                    scale=wsg_sb[:, I:I + 1]),
                    waits=mmg + [p.gw(("sslot", ssl))] + EVI, inc="se")
                ev_h = p.op("vector", lambda v, po=po_u, I=I, icl=icl, ssl=ssl: v.scalar_tensor_tensor(
                    Ak(icl), ps[:, po:po + 1024], wsu_sb[:, I:I + 1], Sslot(ssl),
                    op0=ALU.mult, op1=ALU.mult),
                    waits=[ev_silu, p.gw(("abuf", icl))] + mmu + EVI, inc="ve")
                p.ps_r(po_u, po_u + 1024, ev_h)
                p.ps_r(po_g, po_g + 1024, ev_h)
                p.gs(("sslot", ssl), ev_h)
                h_evs.append(ev_h)
            for m in range(16):
                wsl = m % 4
                wsrc = wpp.ap()[q * 2048:(q + 1) * 2048, m * 128:(m + 1) * 128]
                wsrc = wsrc.rearrange("(i pp) f -> pp i f", pp=128)
                evw = d("sync", Wslot(wsl).rearrange("pp (i f) -> pp i f", f=128),
                        wsrc, waits=[p.gw(("wslot", wsl)), W2["ev"]])
                po = (m % 2) * 1024
                mm_evs = []
                for icl in range(16):
                    for hh in range(2):
                        w = [evw, h_evs[15]]
                        if icl == 0:
                            w += p.ps_w(po + hh * 512, po + (hh + 1) * 512)
                        mm_evs.append(p.op("tensor", lambda t, icl=icl, hh=hh, po=po, wsl=wsl: t.matmul(
                            ps[:, po + hh * 512:po + (hh + 1) * 512],
                            Wslot(wsl)[:, icl * 128:(icl + 1) * 128],
                            Ak(icl)[:, hh * 512:(hh + 1) * 512],
                            start=(icl == 0), stop=(icl == 15)), waits=w, inc="pe"))
                p.gs(("wslot", wsl), mm_evs[-1])
                ev_add = p.op("vector", lambda v, m=m, po=po: v.scalar_tensor_tensor(
                    Rk(m), ps[:, po:po + 1024], wsdp_sb[:, m:m + 1], Rk(m),
                    op0=ALU.mult, op1=ALU.add),
                    waits=mm_evs + [resid_ev[m]] + EVI, inc="ve")
                p.ps_r(po, po + 1024, ev_add)
                resid_ev[m] = ev_add
            for icl in range(16):
                p.gs(("abuf", icl), resid_ev[15])
        for reg in range(2):
            p.gs(("bufB", reg), resid_ev[15])
        p.barrier()

    attn_sublayer("a")
    tap("r_a", buf_R[:, :], (128, 16384), list(resid_ev))
    ffn_sublayer()
    tap("r_f1", buf_R[:, :], (128, 16384), list(resid_ev))
    attn_sublayer("b")
    tap("r_b", buf_R[:, :], (128, 16384), list(resid_ev))
    ffn_sublayer()

    st_evs = [d("sync", out_x[k * 128:(k + 1) * 128, :], Rk(k),
                waits=[resid_ev[k]]) for k in range(NCH)]
    p.op("gpsimd", lambda g: g.engine_nop(), waits=st_evs)

    with nc.Block() as block:
        OWN = {} if DBG_KEEP_SELF else {"tensor": "pe"}

        def emitter(name):
            own = OWN.get(name)
            def run(eng):
                seen = {}
                for fn, ws, inc, amt in p.ops[name]:
                    for s, v in sorted(ws.items()):
                        if s == own:
                            continue
                        if seen.get(s, 0) < v:
                            eng.wait_ge(sems[s], v)
                            seen[s] = v
                    inst = fn(eng)
                    if inc is not None:
                        inst.then_inc(sems[inc], amt)
            return run
        block.tensor(emitter("tensor"))
        block.vector(emitter("vector"))
        block.scalar(emitter("scalar"))
        block.gpsimd(emitter("gpsimd"))
        block.sync(emitter("sync"))

    es.close()
    return nc, dbg


_cached = {}


def kernel(**inputs):
    maps = _prep_inputs(inputs)
    key = tuple(sorted(_TAPS))
    if key not in _cached:
        _cached[key] = build_program()
    nc, dbg = _cached[key]
    res = run_bass_kernel_spmd(nc, maps, core_ids=list(range(8)))
    out = _assemble(res.results)
    if dbg:
        kernel._dbg = [{t: np.asarray(r["dbg_" + t]) for t in dbg} for r in res.results]
    return out


kernel._dbg = None


# version 12 (final)
# speedup vs baseline: 1.2229x; 1.1718x over previous
import numpy as np
import ml_dtypes
from contextlib import ExitStack

import concourse.bass as bass
import concourse.mybir as mybir
from concourse.bass_utils import run_bass_kernel_spmd

F32 = mybir.dt.float32
BF16 = mybir.dt.bfloat16
AF = mybir.ActivationFunctionType
ALU = mybir.AluOpType
AX = mybir.AxisListType
NPBF = ml_dtypes.bfloat16

NCH = 16
NQT = 8
NKB = 32
ISQ_D = float(1.0 / np.sqrt(128.0))
HAD_N1 = float(1.0 / np.sqrt(2048.0))
HAD_N2 = float(1.0 / np.sqrt(8192.0))
EPS = 1e-5
B_SAFETY = 1.05
NEG_BIG = -30000.0

_TAPS = set()
import os as _os
DBG_NO_W2 = bool(int(_os.environ.get("K_DBG_NO_W2", "0")))
DBG_NO_ROPE = bool(int(_os.environ.get("K_DBG_NO_ROPE", "0")))
DBG_NO_COLL = bool(int(_os.environ.get("K_DBG_NO_COLL", "0")))
DBG_KEEP_SELF = bool(int(_os.environ.get("K_DBG_KEEP_SELF", "0")))

ENGS = ("tensor", "vector", "scalar", "gpsimd", "sync")
SEMS = ("dg", "ds", "dc", "pe", "ve", "se", "cc")
DMA_SEMS = ("dg", "ds", "dc")


class Prog:
    def __init__(self):
        self.ops = {e: [] for e in ENGS}
        self.plan = {s: 0 for s in SEMS}
        self.pending = {e: {} for e in ENGS}
        self.guards = {}

    def op(self, eng, fn, waits=(), inc=None, amt=None):
        ws = dict(self.pending[eng])
        self.pending[eng] = {}
        for w in waits:
            if w is None:
                continue
            s, v = w
            ws[s] = max(ws.get(s, 0), v)
        ev = None
        if inc is not None:
            if amt is None:
                amt = 16 if inc in DMA_SEMS else 1
            self.plan[inc] += amt
            ev = (inc, self.plan[inc])
        self.ops[eng].append((fn, ws, inc, amt))
        return ev

    def barrier(self):
        evs = [(s, self.plan[s]) for s in SEMS if self.plan[s] > 0]
        for e in ("tensor", "vector", "scalar"):
            dd = self.pending[e]
            for s, v in evs:
                dd[s] = max(dd.get(s, 0), v)

    def gw(self, key):
        return self.guards.get(key)

    def gs(self, key, ev):
        self.guards[key] = ev

    def ps_w(self, lo, hi):
        return [self.gw(("psb", b)) for b in range(lo // 512, (hi + 511) // 512)]

    def ps_r(self, lo, hi, ev):
        for b in range(lo // 512, (hi + 511) // 512):
            self.gs(("psb", b), ev)



def _rope_tables(positions, base):
    inv = 1.0 / base ** (np.arange(0, 128, 2, dtype=np.float64) / 128.0)
    ang = np.outer(inv, positions.astype(np.float64))
    cos = np.concatenate([np.cos(ang), np.cos(ang)], axis=0)
    sin = np.concatenate([np.sin(ang), np.sin(ang)], axis=0)
    return cos.astype(NPBF), sin.astype(NPBF)


def _mask_tile(r):
    kk = np.arange(128)[:, None]
    qi = np.arange(128)[None, :]
    tiles = []
    for dd in range(4):
        allowed = (128 * dd + kk) <= (128 * r + qi)
        tiles.append(np.where(allowed, 0.0, NEG_BIG))
    return np.concatenate(tiles, axis=1).astype(NPBF)


def _prep_inputs(inp):
    asc = np.ascontiguousarray
    x = np.asarray(inp["L0_attn_a_normalized_resid_pre"], dtype=np.float32)
    perm = np.concatenate([np.arange(0, 128, 2), np.arange(1, 128, 2)])

    wq = np.asarray(inp["L0_attn_a_W_Q"], np.float32)[:, :, perm]
    wk = np.asarray(inp["L0_attn_a_W_K"], np.float32)[:, :, perm]
    wv = np.asarray(inp["L0_attn_a_W_V"], np.float32)
    shared = {
        "wq_a": asc(wq.transpose(1, 0, 2).reshape(2048, 2048)),
        "wk_a": asc(wk.transpose(1, 0, 2).reshape(2048, 2048)),
        "wv_a": asc(wv.transpose(1, 0, 2).reshape(2048, 2048)),
        "wo_a": asc(np.asarray(inp["L0_attn_a_W_O"], np.float32).reshape(2048, 2048)),
        "bq_a": asc(np.asarray(inp["L0_attn_a_b_Q"], np.float32)[:, perm].T),
        "bk_a": asc(np.asarray(inp["L0_attn_a_b_K"], np.float32)[:, perm].T),
        "bv_col": asc(np.asarray(inp["L0_attn_a_b_V"], np.float32).reshape(16, 128).T),
        "bo_col": asc(np.asarray(inp["L0_attn_a_b_O"], np.float32).reshape(16, 128).T),
        "wqkv_b": asc(np.asarray(inp["L0_attn_b_w_qkv"], np.float32).T),
        "wo_b": asc(np.asarray(inp["L0_attn_b_w_o"], np.float32).T),
        "qng": asc(np.asarray(inp["L0_attn_b_qn_g"], np.float32).reshape(128, 1)),
        "qnb": asc(np.asarray(inp["L0_attn_b_qn_b"], np.float32).reshape(128, 1)),
        "kng": asc(np.asarray(inp["L0_attn_b_kn_g"], np.float32).reshape(128, 1)),
        "knb": asc(np.asarray(inp["L0_attn_b_kn_b"], np.float32).reshape(128, 1)),
        "w_upT": asc(np.asarray(inp["L0_ffn_W_up"], np.float32).T),
        "w_gateT": asc(np.asarray(inp["L0_ffn_W_gate"], np.float32).T),
        "su_ug": asc(np.asarray(inp["L0_ffn_SU_ug"], np.float32).reshape(16, 128).T),
        "su_dp_re": asc(np.asarray(inp["L0_ffn_SU_dp"], np.float32).reshape(64, 128)),
        "wsu": asc(np.asarray(inp["L0_ffn_Wscale_ug"], np.float32)[:8192].reshape(64, 128).T),
        "wsg": asc(np.asarray(inp["L0_ffn_Wscale_ug"], np.float32)[8192:].reshape(64, 128).T),
        "wsdp": asc(np.asarray(inp["L0_ffn_Wscale_dp"], np.float32).reshape(16, 128).T),
    }
    w_downT = np.asarray(inp["L0_ffn_W_down"], np.float32).T
    maps = []
    for c in range(8):
        g, r = divmod(c, 4)
        blocks = [4 * m + r for m in range(8)]
        pos = np.concatenate([np.arange(128) + 128 * b for b in blocks])
        xc = x[g].reshape(32, 128, 2048)[blocks].reshape(1024, 2048)
        cos_a, sin_a = _rope_tables(pos, 10000.0)
        cos_b, sin_b = _rope_tables(pos, 1000000.0)
        m = dict(shared)
        m["xT"] = asc(xc.T)
        m["w_downT_sl"] = asc(w_downT[:, c * 256:(c + 1) * 256])
        m["cos_a"] = asc(cos_a); m["sin_a"] = asc(sin_a)
        m["cos_b"] = asc(cos_b); m["sin_b"] = asc(sin_b)
        m["maskT"] = _mask_tile(r)
        maps.append(m)
    return maps


def _assemble(results):
    out = np.empty((2, 4096, 2048), np.float32)
    for c in range(8):
        g, r = divmod(c, 4)
        xc = np.asarray(results[c]["out_x"]).T.reshape(8, 128, 2048)
        for mm in range(8):
            b = 4 * mm + r
            out[g, b * 128:(b + 1) * 128, :] = xc[mm]
    return out



def build_program():
    nc = bass.Bass("TRN2", target_bir_lowering=False, debug=False)
    p = Prog()

    ap = {}
    def ein(name, shape, dt=F32):
        ap[name] = nc.dram_tensor(name, list(shape), dt, kind="ExternalInput").ap()
    ein("xT", (2048, 1024))
    for n in ("wq_a", "wk_a", "wv_a", "wo_a"):
        ein(n, (2048, 2048))
    ein("bq_a", (128, 16)); ein("bk_a", (128, 16))
    ein("bv_col", (128, 16)); ein("bo_col", (128, 16))
    ein("wqkv_b", (2048, 3072)); ein("wo_b", (2048, 2048))
    for n in ("qng", "qnb", "kng", "knb"):
        ein(n, (128, 1))
    ein("w_upT", (2048, 8192)); ein("w_gateT", (2048, 8192))
    ein("w_downT_sl", (8192, 256))
    ein("su_ug", (128, 16)); ein("su_dp_re", (64, 128))
    ein("wsu", (128, 64)); ein("wsg", (128, 64)); ein("wsdp", (128, 16))
    for n in ("cos_a", "sin_a", "cos_b", "sin_b"):
        ein(n, (128, 1024), BF16)
    ein("maskT", (128, 512), BF16)

    out_x = nc.dram_tensor("out_x", [2048, 1024], F32, kind="ExternalOutput").ap()

    kvA_own = nc.dram_tensor("kvA_own", [4224, 1024], BF16, kind="Internal")
    kA_all = nc.dram_tensor("kA_all", [4 * 2176, 1024], BF16, kind="Internal")
    vA_all = nc.dram_tensor("vA_all", [4 * 2048, 1024], BF16, kind="Internal")
    kvB_own = nc.dram_tensor("kvB_own", [1152, 1024], BF16, kind="Internal")
    kvB_all = nc.dram_tensor("kvB_all", [4 * 1152, 1024], BF16, kind="Internal")
    vd = nc.dram_tensor("vd", [8192, 256], BF16, kind="Internal")
    wpp_own = nc.dram_tensor("wpp_own", [8192, 256], BF16, kind="Internal")
    wpp_all = nc.dram_tensor("wpp_all", [8 * 8192, 256], BF16, kind="Internal")

    ic = lambda a, n: nc.inline_tensor(np.ascontiguousarray(a), name=n).ap()
    ones_col_f = ic(np.ones((128, 1), np.float32), "c_ocf")
    ones_col_b = ic(np.ones((128, 1), NPBF), "c_ocb")
    ones_row_f = ic(np.ones((1, 128), np.float32), "c_orf")
    ident_b = ic(np.eye(128, dtype=NPBF), "c_id")
    rotT_np = np.zeros((128, 128), np.float32)
    for i in range(64):
        rotT_np[i, 64 + i] = 1.0
        rotT_np[64 + i, i] = -1.0
    rotT_c = ic(rotT_np.astype(NPBF), "c_rotT")

    def _hmat(n):
        h = np.array([[1.0]])
        while h.shape[0] < n:
            h = np.block([[h, h], [h, -h]])
        return h
    H16 = _hmat(16)
    Hp_c = ic(_hmat(128).astype(NPBF), "c_hp")
    Hm_c = ic((-_hmat(128)).astype(NPBF), "c_hm")
    H64_c = ic(_hmat(64).astype(NPBF), "c_h64")

    dbg = {}
    es = ExitStack()
    ec = es.enter_context

    buf_R = ec(nc.sbuf_tensor("buf_R", [128, 16384], F32))
    buf_L = ec(nc.sbuf_tensor("buf_L", [128, 16384], BF16))
    buf_A = ec(nc.sbuf_tensor("buf_A", [128, 16384], BF16))
    buf_B = ec(nc.sbuf_tensor("buf_B", [128, 16384], BF16))
    buf_W = ec(nc.sbuf_tensor("buf_W", [128, 8192], BF16))
    buf_S = ec(nc.sbuf_tensor("buf_S", [128, 3072], BF16))
    buf_K = ec(nc.sbuf_tensor("buf_K", [128, 2048], BF16))
    scr = ec(nc.sbuf_tensor("scr", [128, 1024], F32))
    rows = ec(nc.sbuf_tensor("rows", [1, 2048], F32))
    tabs = ec(nc.sbuf_tensor("tabs", [128, 2048], BF16))
    mask_sb = ec(nc.sbuf_tensor("mask_sb", [128, 512], BF16))
    bo_sb = ec(nc.sbuf_tensor("bo_sb", [128, 16], F32))
    bv_sb = ec(nc.sbuf_tensor("bv_sb", [128, 16], F32))
    vec16 = ec(nc.sbuf_tensor("vec16", [1, 128], F32))
    bq_sb = ec(nc.sbuf_tensor("bq_sb", [128, 16], F32))
    bk_sb = ec(nc.sbuf_tensor("bk_sb", [128, 16], F32))
    qn_sb = ec(nc.sbuf_tensor("qn_sb", [128, 4], F32))
    su_sb = ec(nc.sbuf_tensor("su_sb", [128, 16], F32))
    sudp_sb = ec(nc.sbuf_tensor("sudp_sb", [64, 128], F32))
    wsu_sb = ec(nc.sbuf_tensor("wsu_sb", [128, 64], F32))
    wsg_sb = ec(nc.sbuf_tensor("wsg_sb", [128, 64], F32))
    wsdp_sb = ec(nc.sbuf_tensor("wsdp_sb", [128, 16], F32))
    negB_sb = ec(nc.sbuf_tensor("negB_sb", [128, 16], F32))
    eps_sb = ec(nc.sbuf_tensor("eps_sb", [1, 1], F32))
    oc_f = ec(nc.sbuf_tensor("oc_f", [128, 1], F32))
    oc_b = ec(nc.sbuf_tensor("oc_b", [128, 1], BF16))
    or_f = ec(nc.sbuf_tensor("or_f", [1, 128], F32))
    id_sb = ec(nc.sbuf_tensor("id_sb", [128, 128], BF16))
    rot_sb = ec(nc.sbuf_tensor("rot_sb", [128, 128], BF16))
    hp_sb = ec(nc.sbuf_tensor("hp_sb", [128, 128], BF16))
    hm_sb = ec(nc.sbuf_tensor("hm_sb", [128, 128], BF16))
    h64_sb = ec(nc.sbuf_tensor("h64_sb", [64, 64], BF16))

    ps = ec(nc.psum_tensor("ps", [128, 4096], F32))
    sems = {s: ec(nc.semaphore("sem_" + s)) for s in SEMS}

    def Rk(k): return buf_R[:, k * 1024:(k + 1) * 1024]
    def Lk(k): return buf_L[:, k * 1024:(k + 1) * 1024]
    def Ak(k): return buf_A[:, k * 1024:(k + 1) * 1024]
    def Bk(k): return buf_B[:, k * 1024:(k + 1) * 1024]
    def Wslot(i): return buf_W[:, i * 2048:(i + 1) * 2048]
    def Sslot(i): return buf_S[:, i * 1024:(i + 1) * 1024]
    def Kslot(j): return buf_K[:, j * 1024:j * 1024 + 512]
    def Vslot(j): return buf_K[:, j * 1024 + 512:(j + 1) * 1024]

    resid_ev = [None] * NCH
    ln_ev = [None] * NCH

    def d(eng, out, in_, waits=()):
        sem = {"gpsimd": "dg", "sync": "ds", "scalar": "dc"}[eng]
        w = list(waits) + [p.guards.get("coll_last")]
        return p.op(eng, lambda e, o=out, i=in_: e.dma_start(out=o, in_=i),
                    waits=w, inc=sem)

    def coll(kind, op, groups, in_t, out_t, waits, in_ap=None, out_ap=None):
        ia = in_ap if in_ap is not None else in_t.ap()
        oa = out_ap if out_ap is not None else out_t.ap()
        if DBG_NO_COLL:
            n = oa.shape[0] // ia.shape[0]
            rows_ = ia.shape[0]
            ev = None
            for i in range(n):
                ev = d("gpsimd", oa[i * rows_:(i + 1) * rows_, :], ia,
                       waits=list(waits))
            return p.op("gpsimd", lambda g: g.engine_nop(), waits=[ev], inc="cc")
        w = list(waits) + [(s, p.plan[s]) for s in DMA_SEMS if p.plan[s] > 0]
        ev = p.op("gpsimd", lambda g, k=kind, o=op, gr=groups, ia=ia, oa=oa:
                  g.collective_compute(k, o, replica_groups=gr,
                                       ins=[ia], outs=[oa]),
                  waits=w, inc="cc")
        p.gs("coll_last", ev)
        return ev

    def tap(tag, src_ap, shape, waits, dt=F32):
        if tag not in _TAPS or tag in dbg:
            return
        t = nc.dram_tensor("dbg_" + tag, list(shape), dt, kind="ExternalOutput").ap()
        dbg[tag] = True
        d("sync", t, src_ap, waits=list(waits))

    ev_init = []
    for dst, src in ((oc_f[:, :], ones_col_f), (oc_b[:, :], ones_col_b),
                     (or_f[:, :], ones_row_f),
                     (id_sb[:, :], ident_b), (rot_sb[:, :], rotT_c),
                     (hp_sb[:, :], Hp_c), (hm_sb[:, :], Hm_c),
                     (h64_sb[:, :], H64_c),
                     (bq_sb[:, :], ap["bq_a"]), (bk_sb[:, :], ap["bk_a"]),
                     (qn_sb[:, 0:1], ap["qng"]), (qn_sb[:, 1:2], ap["qnb"]),
                     (qn_sb[:, 2:3], ap["kng"]), (qn_sb[:, 3:4], ap["knb"]),
                     (su_sb[:, :], ap["su_ug"]), (sudp_sb[:, :], ap["su_dp_re"]),
                     (wsu_sb[:, :], ap["wsu"]), (wsg_sb[:, :], ap["wsg"]),
                     (wsdp_sb[:, :], ap["wsdp"]), (mask_sb[:, :], ap["maskT"]),
                     (bo_sb[:, :], ap["bo_col"]), (bv_sb[:, :], ap["bv_col"])):
        ev_init.append(d("scalar", dst, src))
    ev_memset = p.op("vector", lambda v: v.memset(eps_sb[:, :], EPS), inc="ve")
    ev_init.append(ev_memset)
    EVI = list(ev_init)

    for k in range(NCH):
        resid_ev[k] = d("sync", Rk(k), ap["xT"][k * 128:(k + 1) * 128, :])

    def w2_precompute():
        ev_st1 = []
        for bi in range(8):
            half = bi % 2
            src = ap["w_downT_sl"].rearrange("(b a pp) r -> b pp a r", b=8, pp=128)[bi]
            dstA = buf_A[:, half * 2048:(half + 1) * 2048]
            evw = d("gpsimd", dstA.rearrange("pp (a r) -> pp a r", r=256), src,
                    waits=[p.gw(("w2in", half))] + EVI)
            evac_evs = []
            for a8 in range(8):
                lo = (a8 % 4) * 256 + (a8 // 4) * 1024
                evm = p.op("tensor", lambda t, h=half, a=a8, lo=lo: t.matmul(
                    ps[:, lo:lo + 256], hp_sb[:, :],
                    buf_A[:, h * 2048 + a * 256:h * 2048 + (a + 1) * 256],
                    start=True, stop=True),
                    waits=[evw] + p.ps_w(lo, lo + 256), inc="pe")
                eve = p.op("vector", lambda v, h=half, a=a8, lo=lo: v.tensor_scalar(
                    buf_A[:, 4096 + h * 2048 + a * 256:4096 + h * 2048 + (a + 1) * 256],
                    ps[:, lo:lo + 256], HAD_N2, None, op0=ALU.mult),
                    waits=[evm, p.gw(("w2out", half))], inc="ve")
                p.ps_r(lo, lo + 256, eve)
                evac_evs.append(eve)
            dstv = vd.ap().rearrange("(b a pp) r -> b pp a r", b=8, pp=128)[bi]
            srcv = buf_A[:, 4096 + half * 2048:4096 + (half + 1) * 2048]
            evd = d("sync", dstv, srcv.rearrange("pp (a r) -> pp a r", r=256),
                    waits=evac_evs)
            p.gs(("w2in", half), evd)
            p.gs(("w2out", half), evd)
            ev_st1.append(evd)
        ev_out = []
        for pb in range(32):
            half = pb % 2
            src = vd.ap().rearrange("(a pp) r -> pp a r", pp=128)[pb * 4:(pb + 1) * 4]
            src = src.rearrange("q a r -> a q r")
            dst = buf_A[0:64, 8192 + half * 1024:8192 + (half + 1) * 1024]
            evw = d("sync", dst.rearrange("a (q r) -> a q r", r=256), src,
                    waits=[p.gw(("w2in2", half))] + ev_st1)
            lo = 2048 + half * 1024
            mevs = []
            for hh in range(2):
                mevs.append(p.op("tensor", lambda t, h=half, hh=hh, lo=lo: t.matmul(
                    ps[0:64, lo + hh * 512:lo + (hh + 1) * 512], h64_sb[:, :],
                    buf_A[0:64, 8192 + h * 1024 + hh * 512:8192 + h * 1024 + (hh + 1) * 512],
                    start=True, stop=True),
                    waits=[evw] + p.ps_w(lo, lo + 1024) + EVI, inc="pe"))
            evs = []
            for q in range(4):
                P = pb * 4 + q
                eve = p.op("vector", lambda v, h=half, q=q, P=P, lo=lo: v.tensor_scalar(
                    buf_A[0:64, 12288 + h * 1024 + q * 256:12288 + h * 1024 + (q + 1) * 256],
                    ps[0:64, lo + q * 256:lo + (q + 1) * 256],
                    sudp_sb[:, P:P + 1], None, op0=ALU.mult),
                    waits=mevs + [p.gw(("w2out2", half))] + EVI, inc="ve")
                evs.append(eve)
            p.ps_r(lo, lo + 1024, evs[-1])
            for q in range(4):
                P = pb * 4 + q
                dstw = wpp_own.ap().rearrange("(a pp) r -> pp a r", pp=128)[P]
                evd = d("sync", dstw,
                        buf_A[0:64, 12288 + half * 1024 + q * 256:12288 + half * 1024 + (q + 1) * 256].rearrange(
                            "a (o r) -> a o r", o=1)[:, 0, :] if False else
                        buf_A[0:64, 12288 + half * 1024 + q * 256:12288 + half * 1024 + (q + 1) * 256],
                        waits=evs)
                ev_out.append(evd)
            p.gs(("w2in2", half), ev_out[-1])
            p.gs(("w2out2", half), ev_out[-1])
        return ev_out

    w2_out_evs = [] if DBG_NO_W2 else w2_precompute()
    W2 = {"ev": None}

    def emit_layernorm(su_fold):
        mm_last = None
        for k in range(NCH):
            sl = k % 3
            evq = p.op("scalar", lambda s, k=k, sl=sl: s.square(Sslot(sl), Rk(k)),
                       waits=[resid_ev[k], p.gw(("sslot", sl))], inc="se")
            for hh in range(2):
                w1 = [resid_ev[k]] + (p.ps_w(2048, 3072) if k == 0 else [])
                p.op("tensor", lambda t, k=k, hh=hh: t.matmul(
                    ps[0:1, 2048 + hh * 512:2048 + (hh + 1) * 512], oc_f[:, :],
                    Rk(k)[:, hh * 512:(hh + 1) * 512],
                    start=(k == 0), stop=(k == NCH - 1)), waits=w1, inc="pe")
                w2 = [evq] + (p.ps_w(3072, 4096) if k == 0 else [])
                mm_last = p.op("tensor", lambda t, k=k, hh=hh, sl=sl: t.matmul(
                    ps[0:1, 3072 + hh * 512:3072 + (hh + 1) * 512], oc_b[:, :],
                    Sslot(sl)[:, hh * 512:(hh + 1) * 512],
                    start=(k == 0), stop=(k == NCH - 1)), waits=w2, inc="pe")
            p.gs(("sslot", sl), mm_last)
        ev_mu = p.op("vector", lambda v: v.tensor_scalar(
            rows[0:1, 0:1024], ps[0:1, 2048:3072], 1.0 / 2048.0, None, op0=ALU.mult),
            waits=[mm_last, p.gw("rows")], inc="ve")
        ev_var = p.op("vector", lambda v: v.tensor_scalar(
            rows[0:1, 1024:2048], ps[0:1, 3072:4096], 1.0 / 2048.0, None, op0=ALU.mult),
            waits=[mm_last], inc="ve")
        p.ps_r(2048, 4096, ev_var)
        ev_t = p.op("vector", lambda v: v.tensor_mul(
            scr[0:1, 0:1024], rows[0:1, 0:1024], rows[0:1, 0:1024]),
            waits=[ev_mu, p.gw("scr")], inc="ve")
        ev_var2 = p.op("vector", lambda v: v.tensor_sub(
            rows[0:1, 1024:2048], rows[0:1, 1024:2048], scr[0:1, 0:1024]),
            waits=[ev_var, ev_t], inc="ve")
        ev_sd = p.op("scalar", lambda s: s.activation(
            out=scr[0:1, 0:1024], in_=rows[0:1, 1024:2048], func=AF.Sqrt,
            bias=eps_sb[0:1, 0:1]),
            waits=[ev_var2, ev_memset], inc="se")
        ev_rstd = p.op("vector", lambda v: v.reciprocal(
            rows[0:1, 1024:2048], scr[0:1, 0:1024]), waits=[ev_sd], inc="ve")
        bevs = []
        for hh in range(2):
            bevs.append(p.op("tensor", lambda t, hh=hh: t.matmul(
                ps[:, hh * 512:(hh + 1) * 512], or_f[0:1, :],
                rows[0:1, hh * 512:(hh + 1) * 512], start=True, stop=True),
                waits=[ev_mu] + p.ps_w(hh * 512, (hh + 1) * 512), inc="pe"))
            bevs.append(p.op("tensor", lambda t, hh=hh: t.matmul(
                ps[:, 1024 + hh * 512:1024 + (hh + 1) * 512], or_f[0:1, :],
                rows[0:1, 1024 + hh * 512:1024 + (hh + 1) * 512], start=True, stop=True),
                waits=[ev_rstd] + p.ps_w(1024 + hh * 512, 1024 + (hh + 1) * 512),
                inc="pe"))
        p.gs("rows", bevs[-1])
        for k in range(NCH):
            ev1 = p.op("vector", lambda v, k=k: v.tensor_sub(
                scr[:, :], Rk(k), ps[:, 0:1024]),
                waits=bevs + [resid_ev[k], p.gw("scr")], inc="ve")
            if su_fold:
                ev2 = p.op("vector", lambda v, k=k: v.scalar_tensor_tensor(
                    Lk(k), scr[:, :], su_sb[:, k:k + 1], ps[:, 1024:2048],
                    op0=ALU.mult, op1=ALU.mult),
                    waits=[ev1] + EVI, inc="ve")
            else:
                ev2 = p.op("vector", lambda v, k=k: v.tensor_tensor(
                    Lk(k), scr[:, :], ps[:, 1024:2048], op=ALU.mult),
                    waits=[ev1], inc="ve")
            p.gs("scr", ev2)
            ln_ev[k] = ev2
        p.ps_r(0, 2048, ln_ev[NCH - 1])

    def rope_chunk(dst_ap, src_ev, tab_ev):
        if DBG_NO_ROPE:
            return src_ev
        lo = 2048
        rot_evs = []
        for hh in range(2):
            rot_evs.append(p.op("tensor", lambda t, hh=hh: t.matmul(
                ps[:, lo + hh * 512:lo + (hh + 1) * 512], rot_sb[:, :],
                dst_ap[:, hh * 512:(hh + 1) * 512], start=True, stop=True),
                waits=[src_ev] + p.ps_w(lo, lo + 1024) + EVI, inc="pe"))
        ev1 = p.op("vector", lambda v: v.tensor_tensor(
            Sslot(2), ps[:, lo:lo + 1024], tabs[:, 1024:2048], op=ALU.mult),
            waits=rot_evs + [p.gw(("sslot", 2))] + tab_ev, inc="ve")
        p.ps_r(lo, lo + 1024, ev1)
        ev2 = p.op("vector", lambda v: v.tensor_tensor(
            dst_ap, dst_ap, tabs[:, 0:1024], op=ALU.mult),
            waits=rot_evs + tab_ev, inc="ve")
        ev3 = p.op("vector", lambda v: v.tensor_add(dst_ap, dst_ap, Sslot(2)),
                   waits=[ev1, ev2], inc="ve")
        p.gs(("sslot", 2), ev3)
        return ev3

    def norm2max_chunk(src_ap, src_ev, vec_off):
        evq = p.op("scalar", lambda s: s.square(Sslot(2), src_ap),
                   waits=[src_ev, p.gw(("sslot", 2))], inc="se")
        mevs = []
        for hh in range(2):
            mevs.append(p.op("tensor", lambda t, hh=hh: t.matmul(
                ps[0:1, 3072 + hh * 512:3072 + (hh + 1) * 512], oc_b[:, :],
                Sslot(2)[:, hh * 512:(hh + 1) * 512], start=True, stop=True),
                waits=[evq] + p.ps_w(3072, 4096), inc="pe"))
        ev = p.op("vector", lambda v, vo=vec_off: v.reduce_max(
            vec16[0:1, vo:vo + 1], ps[0:1, 3072:4096], axis=AX.X),
            waits=mevs, inc="ve")
        p.ps_r(3072, 4096, ev)
        p.gs(("sslot", 2), ev)
        return ev

    def attn_sublayer(var):
        is_a = (var == "a")
        n_kch = 16 if is_a else 4
        fw = 512 if is_a else 128
        NR = 2048 if is_a else 512
        VOFF = NR + 128
        VROWS = 2048 if is_a else 512
        kv_own = kvA_own if is_a else kvB_own

        tev1 = d("sync", tabs[:, 0:1024], ap["cos_" + var], waits=[p.gw("tabs")])
        tev2 = d("sync", tabs[:, 1024:2048], ap["sin_" + var], waits=[p.gw("tabs")])
        tab_ev = [tev1, tev2]

        emit_layernorm(False)
        tap("ln_" + var, buf_L[:, :], (128, 16384), list(ln_ev), BF16)
        p.barrier()

        if is_a:
            w_k_src = lambda m: ap["wk_a"].rearrange(
                "(k pp) (mm f) -> pp k mm f", pp=128, f=128)[:, :, m, :]
            w_q_src = lambda m: ap["wq_a"].rearrange(
                "(k pp) (mm f) -> pp k mm f", pp=128, f=128)[:, :, m, :]
        else:
            w_k_src = lambda m: ap["wqkv_b"].rearrange(
                "(k pp) (mm f) -> pp k mm f", pp=128, f=128)[:, :, 16 + m, :]
            w_q_src = lambda m: ap["wqkv_b"].rearrange(
                "(k pp) (mm f) -> pp k mm f", pp=128, f=128)[:, :, m, :]

        def proj_mms(lhs_fn, evw, po):
            evs = []
            ln_last = ln_ev[NCH - 1]
            for k in range(NCH):
                for hh in range(2):
                    w = [evw, ln_last]
                    if k == 0:
                        w += p.ps_w(po + hh * 512, po + (hh + 1) * 512)
                    evs.append(p.op("tensor", lambda t, k=k, hh=hh, po=po: t.matmul(
                        ps[:, po + hh * 512:po + (hh + 1) * 512],
                        lhs_fn(k),
                        Lk(k)[:, hh * 512:(hh + 1) * 512],
                        start=(k == 0), stop=(k == NCH - 1)), waits=w, inc="pe"))
            return evs

        def qk_ln_ip(buf, g_col, b_col, ev_in):
            evq = p.op("scalar", lambda s: s.square(Sslot(2), buf),
                       waits=[ev_in, p.gw(("sslot", 2))], inc="se")
            mevs = []
            for hh in range(2):
                mevs.append(p.op("tensor", lambda t, hh=hh: t.matmul(
                    ps[0:1, 2048 + hh * 512:2048 + (hh + 1) * 512], oc_b[:, :],
                    buf[:, hh * 512:(hh + 1) * 512], start=True, stop=True),
                    waits=[ev_in] + p.ps_w(2048, 3072), inc="pe"))
                mevs.append(p.op("tensor", lambda t, hh=hh: t.matmul(
                    ps[0:1, 3072 + hh * 512:3072 + (hh + 1) * 512], oc_b[:, :],
                    Sslot(2)[:, hh * 512:(hh + 1) * 512], start=True, stop=True),
                    waits=[evq] + p.ps_w(3072, 4096), inc="pe"))
            ev_mu = p.op("vector", lambda v: v.tensor_scalar(
                rows[0:1, 0:1024], ps[0:1, 2048:3072], 1.0 / 128.0, None,
                op0=ALU.mult), waits=mevs + [p.gw("rows")], inc="ve")
            ev_msq = p.op("vector", lambda v: v.tensor_scalar(
                rows[0:1, 1024:2048], ps[0:1, 3072:4096], 1.0 / 128.0, None,
                op0=ALU.mult), waits=mevs, inc="ve")
            p.gs(("sslot", 2), ev_msq)
            ev_t = p.op("vector", lambda v: v.tensor_mul(
                scr[0:1, 0:1024], rows[0:1, 0:1024], rows[0:1, 0:1024]),
                waits=[ev_mu, p.gw("scr")], inc="ve")
            ev_var = p.op("vector", lambda v: v.tensor_sub(
                rows[0:1, 1024:2048], rows[0:1, 1024:2048], scr[0:1, 0:1024]),
                waits=[ev_msq, ev_t], inc="ve")
            ev_sd = p.op("scalar", lambda s: s.activation(
                out=scr[0:1, 0:1024], in_=rows[0:1, 1024:2048], func=AF.Sqrt,
                bias=eps_sb[0:1, 0:1]), waits=[ev_var, ev_memset], inc="se")
            ev_rstd = p.op("vector", lambda v: v.reciprocal(
                rows[0:1, 1024:2048], scr[0:1, 0:1024]), waits=[ev_sd], inc="ve")
            bevs = []
            for hh in range(2):
                bevs.append(p.op("tensor", lambda t, hh=hh: t.matmul(
                    ps[:, 2048 + hh * 512:2048 + (hh + 1) * 512], or_f[0:1, :],
                    rows[0:1, hh * 512:(hh + 1) * 512], start=True, stop=True),
                    waits=[ev_mu] + p.ps_w(2048, 3072), inc="pe"))
                bevs.append(p.op("tensor", lambda t, hh=hh: t.matmul(
                    ps[:, 3072 + hh * 512:3072 + (hh + 1) * 512], or_f[0:1, :],
                    rows[0:1, 1024 + hh * 512:1024 + (hh + 1) * 512],
                    start=True, stop=True),
                    waits=[ev_rstd] + p.ps_w(3072, 4096), inc="pe"))
            p.gs("rows", bevs[-1])
            ev1 = p.op("vector", lambda v: v.tensor_sub(
                scr[:, :], buf, ps[:, 2048:3072]), waits=bevs, inc="ve")
            ev2 = p.op("vector", lambda v: v.tensor_tensor(
                scr[:, :], scr[:, :], ps[:, 3072:4096], op=ALU.mult),
                waits=[ev1], inc="ve")
            p.ps_r(2048, 4096, ev2)
            ev3 = p.op("vector", lambda v, g=g_col, b=b_col: v.tensor_scalar(
                buf, scr[:, :], qn_sb[:, g:g + 1], qn_sb[:, b:b + 1],
                op0=ALU.mult, op1=ALU.add),
                waits=[ev2] + EVI, inc="ve")
            p.gs("scr", ev3)
            return ev3

        def post_chunk(buf, m, is_q):
            if is_a:
                ev = p.gw(("evac", is_q, m))
            else:
                ev = qk_ln_ip(buf, 0 if is_q else 2, 1 if is_q else 3,
                              p.gw(("evac", is_q, m)))
            ev_rope = rope_chunk(buf, ev, tab_ev)
            ev_nrm = norm2max_chunk(buf, ev_rope, (m if is_q else 16 + m))
            return ev_rope, ev_nrm

        ship_evs = []
        nrm_evs = []
        for m in range(n_kch + 1):
            if m < n_kch:
                wsl = m % 4
                evw = d("gpsimd", Wslot(wsl).rearrange("pp (k f) -> pp k f", f=128),
                        w_k_src(m), waits=[p.gw(("wslot", wsl))])
                po = (m % 2) * 1024
                mm_evs = proj_mms(
                    lambda k, wsl=wsl: Wslot(wsl)[:, k * 128:(k + 1) * 128], evw, po)
                p.gs(("wslot", wsl), mm_evs[-1])
                raw = m % 2
                if is_a:
                    ev_evac = p.op("vector", lambda v, m=m, po=po, raw=raw: v.tensor_scalar(
                        Sslot(raw), ps[:, po:po + 1024], bk_sb[:, m:m + 1], None,
                        op0=ALU.add),
                        waits=mm_evs + [p.gw(("sslot", raw))] + EVI, inc="ve")
                else:
                    ev_evac = p.op("vector", lambda v, po=po, raw=raw: v.tensor_copy(
                        Sslot(raw), ps[:, po:po + 1024]),
                        waits=mm_evs + [p.gw(("sslot", raw))], inc="ve")
                p.ps_r(po, po + 1024, ev_evac)
                p.gs(("evac", False, m), ev_evac)
            if m >= 1:
                mp = m - 1
                raw = mp % 2
                ev_rope, ev_nrm = post_chunk(Sslot(raw), mp, False)
                nrm_evs.append(ev_nrm)
                ev_ship = d("sync", kv_own.ap()[mp * 128:(mp + 1) * 128, :],
                            Sslot(raw), waits=[ev_rope, ev_nrm])
                p.gs(("sslot", raw), ev_ship)
                ship_evs.append(ev_ship)
        ev_nrm_ship = d("gpsimd", kv_own.ap()[NR:NR + 1, 0:n_kch],
                        vec16[0:1, 16:16 + n_kch], waits=nrm_evs)

        def v_stage(sl, extra=()):
            reg = sl % 2
            if is_a:
                vsrc = ap["wv_a"].rearrange(
                    "(k pp) (s f) -> pp k s f", pp=128, f=512)[:, :, sl, :]
            else:
                vsrc = ap["wqkv_b"].rearrange(
                    "(k pp) (s f) -> pp k s f", pp=128, f=128)[:, :, 20 + sl, :]
            vstg = buf_B[:, reg * 8192:reg * 8192 + 16 * fw]
            return vstg, d(
                "gpsimd", vstg.rearrange("pp (k f) -> pp k f", f=fw), vsrc,
                waits=[p.gw(("bufB", reg)), p.gw(("bufBv", reg))] + list(extra))
        vstg_evs = {0: v_stage(0), 1: v_stage(1)}
        preW_evs = []
        for k in range(4):
            preW_evs.append(d("gpsimd",
                              Wslot(k).rearrange("pp (kk f) -> pp kk f", f=128),
                              w_q_src(k), waits=[p.gw(("wslot", k))]))
        if is_a:
            ev_agK = coll("AllGather", ALU.bypass, [[0, 1, 2, 3], [4, 5, 6, 7]],
                          None, None, ship_evs + [ev_nrm_ship],
                          in_ap=kv_own.ap()[0:VOFF, :], out_ap=kA_all.ap())
        v_tile_evs = []
        for sl in range(4):
            reg = sl % 2
            vstg, evw = vstg_evs[sl]
            for tt in range(NQT):
                lo = 2048 + reg * 512
                mm = []
                for k in range(NCH):
                    w = [evw, ln_ev[NCH - 1]]
                    if k == 0:
                        w += p.ps_w(lo, lo + fw)
                    mm.append(p.op("tensor", lambda t, tt=tt, k=k, lo=lo, vstg=vstg: t.matmul(
                        ps[:, lo:lo + fw], Lk(k)[:, tt * 128:(tt + 1) * 128],
                        vstg[:, k * fw:(k + 1) * fw],
                        start=(k == 0), stop=(k == NCH - 1)), waits=w, inc="pe"))
                ac = tt * 2048 + sl * fw
                ev_evac = p.op("vector", lambda v, lo=lo, ac=ac: v.tensor_copy(
                    buf_A[:, ac:ac + fw], ps[:, lo:lo + fw]),
                    waits=[mm[-1], p.gw(("abuf", tt * 2))], inc="ve")
                p.ps_r(lo, lo + fw, ev_evac)
                v_tile_evs.append((tt, sl, ev_evac))
            p.gs(("bufBv", reg), mm[-1])
            if sl + 2 < 4:
                vstg_evs[sl + 2] = v_stage(sl + 2, extra=[mm[-1]])
        if is_a:
            v_ship_view = kv_own.ap()[VOFF:VOFF + VROWS, :].rearrange(
                "(t h) c -> t (h c)", h=2)
        else:
            v_ship_view = kv_own.ap()[VOFF:VOFF + VROWS, :].rearrange(
                "t (h c) -> (t h) c", h=2)
        vship_evs = []
        for tt, sl, ev in v_tile_evs:
            ac = tt * 2048 + sl * fw
            ev_ship = d("sync",
                        v_ship_view[tt * 128:(tt + 1) * 128, sl * fw:(sl + 1) * fw],
                        buf_A[:, ac:ac + fw], waits=[ev])
            vship_evs.append(ev_ship)
            p.gs(("abuf", tt * 2), ev_ship)
            p.gs(("abuf", tt * 2 + 1), ev_ship)

        pre_evs = {}
        for k in range(4, 8):
            dst = buf_B[:, k * 2048:(k + 1) * 2048]
            pre_evs[k] = d("gpsimd", dst.rearrange("pp (kk f) -> pp kk f", f=128),
                           w_q_src(k), waits=[p.gw(("bufBv", 1))])
        for k in range(8, 12):
            dst = buf_B[:, (k - 8) * 2048:(k - 7) * 2048]
            pre_evs[k] = d("gpsimd", dst.rearrange("pp (kk f) -> pp kk f", f=128),
                           w_q_src(k), waits=[p.gw(("bufBv", 0))])

        if is_a:
            ev_agV = coll("AllGather", ALU.bypass, [[0, 1, 2, 3], [4, 5, 6, 7]],
                          None, None, vship_evs,
                          in_ap=kv_own.ap()[VOFF:VOFF + VROWS, :], out_ap=vA_all.ap())
            ev_ag = ev_agV
        else:
            ev_ag = coll("AllGather", ALU.bypass, [[0, 1, 2, 3], [4, 5, 6, 7]],
                         kv_own, kvB_all, ship_evs + vship_evs + [ev_nrm_ship])
            ev_agK = ev_ag
            ev_agV = ev_ag

        rope_evs = []
        nq_evs = []
        LAG = 1 if is_a else 2
        for m in range(16 + LAG):
            if m < 16:
                po = (m % 2) * 1024
                if m < 4:
                    mm_evs = proj_mms(
                        lambda k, m=m: Wslot(m)[:, k * 128:(k + 1) * 128],
                        preW_evs[m], po)
                    p.gs(("wslot", m), mm_evs[-1])
                elif m < 12:
                    col0 = (m if m < 8 else m - 8) * 2048
                    lhs_fn = lambda k, col0=col0: buf_B[:, col0 + k * 128:col0 + (k + 1) * 128]
                    mm_evs = proj_mms(lhs_fn, pre_evs[m], po)
                else:
                    wsl = m % 4
                    evw = d("gpsimd", Wslot(wsl).rearrange("pp (k f) -> pp k f", f=128),
                            w_q_src(m), waits=[p.gw(("wslot", wsl))])
                    mm_evs = proj_mms(
                        lambda k, wsl=wsl: Wslot(wsl)[:, k * 128:(k + 1) * 128], evw, po)
                    p.gs(("wslot", wsl), mm_evs[-1])
                if is_a:
                    ev_evac = p.op("vector", lambda v, m=m, po=po: v.tensor_scalar(
                        Ak(m), ps[:, po:po + 1024], bq_sb[:, m:m + 1], None,
                        op0=ALU.add),
                        waits=mm_evs + [p.gw(("abuf", m))] + EVI, inc="ve")
                else:
                    ev_evac = p.op("vector", lambda v, m=m, po=po: v.tensor_copy(
                        Ak(m), ps[:, po:po + 1024]),
                        waits=mm_evs + [p.gw(("abuf", m))], inc="ve")
                p.ps_r(po, po + 1024, ev_evac)
                p.gs(("evac", True, m), ev_evac)
            if m >= LAG:
                mp = m - LAG
                ev_rope, ev_nq = post_chunk(Ak(mp), mp, True)
                rope_evs.append(ev_rope)
                nq_evs.append(ev_nq)
        p.gs("tabs", rope_evs[-1])
        tap("q_" + var, buf_A[:, :], (128, 16384), rope_evs, BF16)

        SEGK = VOFF
        k_all_t = kA_all if is_a else kvB_all
        SEGF = VOFF if is_a else 1152
        ev_mkload = d("gpsimd", vec16[0:1, 64:64 + 4 * n_kch],
                      k_all_t.ap().rearrange("(r s) q -> s r q", s=SEGF)[
                          NR:NR + 1, :, 0:n_kch],
                      waits=[ev_agK])
        ev_mk = p.op("vector", lambda v: v.reduce_max(
            vec16[0:1, 32:33], vec16[0:1, 64:64 + 4 * n_kch], axis=AX.X),
            waits=[ev_mkload], inc="ve")
        ev_b1 = p.op("vector", lambda v: v.tensor_scalar(
            vec16[0:1, 48:64], vec16[0:1, 0:16], vec16[0:1, 32:33], None,
            op0=ALU.mult), waits=[ev_mk] + nq_evs, inc="ve")
        ev_b2 = p.op("scalar", lambda s: s.activation(
            out=vec16[0:1, 0:16], in_=vec16[0:1, 48:64], func=AF.Sqrt,
            bias=eps_sb[0:1, 0:1]), waits=[ev_b1, ev_memset], inc="se")
        ev_b3 = p.op("vector", lambda v: v.tensor_scalar(
            vec16[0:1, 48:64], vec16[0:1, 0:16], -B_SAFETY * ISQ_D, None,
            op0=ALU.mult), waits=[ev_b2], inc="ve")
        ev_b4 = p.op("tensor", lambda t: t.matmul(
            ps[:, 3072:3088], or_f[0:1, :], vec16[0:1, 48:64],
            start=True, stop=True),
            waits=[ev_b3] + p.ps_w(3072, 3584), inc="pe")
        ev_negB = p.op("vector", lambda v: v.tensor_copy(
            negB_sb[:, :], ps[:, 3072:3088]), waits=[ev_b4], inc="ve")
        p.ps_r(3072, 3584, ev_negB)
        tap("negb_" + var, negB_sb[:, :], (128, 16), [ev_negB])
        p.barrier()

        z_evs = []
        k_view = k_all_t.ap().rearrange("(r s) q -> s r q", s=SEGF)
        if is_a:
            v_view = vA_all.ap().rearrange("(r s) q -> r s q", s=2048).rearrange(
                "r (t h) c -> t r (h c)", h=2)
        else:
            v_view = kvB_all.ap().rearrange("(r s) q -> r s q", s=1152)[
                :, VOFF:VOFF + VROWS, :].rearrange("r t (h c) -> (t h) r c", h=2)
        for h in range(16):
            kh = h if is_a else h // 4
            kv_evs = [None] * 8

            def emit_kv_load(kb4, kh=kh, kv_evs=kv_evs):
                j = kb4 % 2
                ksrc = k_view[kh * 128:(kh + 1) * 128, :, kb4 * 128:(kb4 + 1) * 128]
                vsrc = v_view[kb4 * 128:(kb4 + 1) * 128, :, kh * 128:(kh + 1) * 128]
                gwk = p.gw(("kpair", j))
                evk = d("sync", Kslot(j).rearrange("pp (r q) -> pp r q", q=128),
                        ksrc, waits=[gwk, ev_agK])
                evv = d("sync", Vslot(j).rearrange("pp (r f) -> pp r f", f=128),
                        vsrc, waits=[gwk, ev_agV])
                kv_evs[kb4] = (evk, evv)

            emit_kv_load(0)
            emit_kv_load(1)
            last_sums = None
            last_pv = None
            pend = None

            def emit_sums_pv(kb, regions, exp_evs, evv, j, dd):
                nonlocal last_sums, last_pv
                for (r0, r1) in regions:
                    w = exp_evs + (p.ps_w(3072 + r0, 3072 + r1) if kb == 0 else [])
                    last_sums = p.op("tensor", lambda t, r0=r0, r1=r1, kb=kb: t.matmul(
                        ps[0:1, 3072 + r0:3072 + r1], oc_b[:, :],
                        Sslot(kb % 3)[:, r0:r1],
                        start=(kb == 0), stop=(kb == NKB - 1)), waits=w, inc="pe")
                for (r0, r1) in regions:
                    w = exp_evs + [evv] + (p.ps_w(1024 + r0, 1024 + r1) if kb == 0 else [])
                    last_pv = p.op("tensor", lambda t, r0=r0, r1=r1, j=j, dd=dd, kb=kb: t.matmul(
                        ps[:, 1024 + r0:1024 + r1], Vslot(j)[:, dd * 128:(dd + 1) * 128],
                        Sslot(kb % 3)[:, r0:r1],
                        start=(kb == 0), stop=(kb == NKB - 1)), waits=w, inc="pe")
                p.gs(("sslot", kb % 3), last_pv)
                if dd == 3:
                    p.gs(("kpair", j), last_pv)
                    m0_ = kb // 4
                    if m0_ + 2 < 8:
                        emit_kv_load(m0_ + 2)

            for kb in range(NKB):
                m0 = kb // 4
                dd = kb % 4
                j = m0 % 2
                so = 0 if kb % 2 == 0 else 2048
                live0 = m0 * 128
                regions = [(live0, 512), (512, 1024)] if live0 < 512 else [(live0, 1024)]
                evk, evv = kv_evs[m0]
                sc_evs = []
                for (r0, r1) in regions:
                    w = [evk] + p.ps_w(so + r0, so + r1)
                    sc_evs.append(p.op("tensor", lambda t, so=so, r0=r0, r1=r1, j=j, dd=dd, h=h: t.matmul(
                        ps[:, so + r0:so + r1], Kslot(j)[:, dd * 128:(dd + 1) * 128],
                        Ak(h)[:, r0:r1], start=True, stop=False),
                        waits=w, inc="pe"))
                ev_mask = p.op("tensor", lambda t, so=so, live0=live0, dd=dd: t.matmul(
                    ps[:, so + live0:so + live0 + 128], id_sb[:, :],
                    mask_sb[:, dd * 128:(dd + 1) * 128], start=False, stop=True),
                    waits=sc_evs + EVI, inc="pe")
                pslot = kb % 3
                exp_evs = []
                for (r0, r1) in regions:
                    exp_evs.append(p.op("scalar", lambda s, so=so, r0=r0, r1=r1, pslot=pslot, h=h: s.activation(
                        out=Sslot(pslot)[:, r0:r1], in_=ps[:, so + r0:so + r1],
                        func=AF.Exp, bias=negB_sb[:, h:h + 1], scale=ISQ_D),
                        waits=[ev_mask, ev_negB, p.gw(("sslot", pslot))], inc="se"))
                for (r0, r1) in regions:
                    p.ps_r(so + r0, so + r1, exp_evs[-1])
                if pend is not None:
                    emit_sums_pv(*pend)
                pend = (kb, regions, exp_evs, evv, j, dd)
            emit_sums_pv(*pend)
            ev_sc = p.op("scalar", lambda s: s.copy(rows[0:1, 0:1024],
                                                    ps[0:1, 3072:4096]),
                         waits=[last_sums, p.gw("rows")], inc="se")
            rb_evs = []
            for hh in range(2):
                rb_evs.append(p.op("tensor", lambda t, hh=hh: t.matmul(
                    ps[:, 2048 + hh * 512:2048 + (hh + 1) * 512], or_f[0:1, :],
                    rows[0:1, hh * 512:(hh + 1) * 512], start=True, stop=True),
                    waits=[ev_sc] + p.ps_w(2048, 3072), inc="pe"))
            p.gs("rows", rb_evs[-1])
            ev_ln = p.op("scalar", lambda s: s.activation(
                out=ps[:, 3072:4096], in_=ps[:, 2048:3072], func=AF.Ln),
                waits=rb_evs, inc="se")
            ev_rcp = p.op("scalar", lambda s: s.activation(
                out=ps[:, 2048:3072], in_=ps[:, 3072:4096], func=AF.Exp, scale=-1.0),
                waits=[ev_ln], inc="se")
            p.ps_r(3072, 4096, ev_rcp)
            ev_zc = p.op("vector", lambda v: v.tensor_copy(scr[:, :], ps[:, 1024:2048]),
                         waits=[last_pv, p.gw("scr")], inc="ve")
            p.ps_r(1024, 2048, ev_zc)
            if is_a:
                ev_zm = p.op("vector", lambda v: v.tensor_tensor(
                    scr[:, :], scr[:, :], ps[:, 2048:3072], op=ALU.mult),
                    waits=[ev_zc, ev_rcp], inc="ve")
                ev_zn = p.op("vector", lambda v, h=h: v.tensor_scalar(
                    Bk(h), scr[:, :], bv_sb[:, h:h + 1], None, op0=ALU.add),
                    waits=[ev_zm] + EVI, inc="ve")
            else:
                ev_zn = p.op("vector", lambda v, h=h: v.tensor_tensor(
                    Bk(h), scr[:, :], ps[:, 2048:3072], op=ALU.mult),
                    waits=[ev_zc, ev_rcp], inc="ve")
            p.gs("scr", ev_zn)
            p.ps_r(2048, 3072, ev_zn)
            z_evs.append(ev_zn)
        tap("z_" + var, buf_B[:, :], (128, 16384), z_evs, BF16)
        p.barrier()

        wo_ap = ap["wo_a"] if is_a else ap["wo_b"]
        for m in range(16):
            wsl = m % 4
            wsrc = wo_ap.rearrange("(k pp) (mm f) -> pp k mm f", pp=128, f=128)[:, :, m, :]
            evw = d("gpsimd", Wslot(wsl).rearrange("pp (k f) -> pp k f", f=128),
                    wsrc, waits=[p.gw(("wslot", wsl))])
            po = (m % 2) * 1024
            mm_evs = []
            for k in range(16):
                for hh in range(2):
                    w = [evw, z_evs[15]]
                    if k == 0:
                        w += p.ps_w(po + hh * 512, po + (hh + 1) * 512)
                    mm_evs.append(p.op("tensor", lambda t, k=k, hh=hh, po=po, wsl=wsl: t.matmul(
                        ps[:, po + hh * 512:po + (hh + 1) * 512],
                        Wslot(wsl)[:, k * 128:(k + 1) * 128],
                        Bk(k)[:, hh * 512:(hh + 1) * 512],
                        start=(k == 0), stop=(k == 15)), waits=w, inc="pe"))
            p.gs(("wslot", wsl), mm_evs[-1])
            if is_a:
                ev_add = p.op("vector", lambda v, m=m, po=po: v.scalar_tensor_tensor(
                    Rk(m), ps[:, po:po + 1024], bo_sb[:, m:m + 1], Rk(m),
                    op0=ALU.add, op1=ALU.add),
                    waits=mm_evs + [resid_ev[m]] + EVI, inc="ve")
            else:
                ev_add = p.op("vector", lambda v, m=m, po=po: v.tensor_tensor(
                    Rk(m), Rk(m), ps[:, po:po + 1024], op=ALU.add),
                    waits=mm_evs + [resid_ev[m]], inc="ve")
            p.ps_r(po, po + 1024, ev_add)
            resid_ev[m] = ev_add
        p.barrier()
        for reg in range(2):
            p.gs(("bufB", reg), resid_ev[15])
            p.gs(("bufBv", reg), resid_ev[15])
        for m in range(16):
            p.gs(("abuf", m), resid_ev[15])

    def ffn_sublayer():
        emit_layernorm(True)
        p.barrier()
        if W2["ev"] is None and not DBG_NO_W2:
            W2["ev"] = coll("AllGather", ALU.bypass, [list(range(8))],
                            wpp_own, wpp_all, w2_out_evs)
        xp_evs = []
        for a2 in range(16):
            po = (a2 % 2) * 1024
            mm_evs = []
            for a in range(16):
                hsb = hp_sb if H16[a2, a] > 0 else hm_sb
                for hh in range(2):
                    w = [ln_ev[NCH - 1]] + EVI
                    if a == 0:
                        w += p.ps_w(po + hh * 512, po + (hh + 1) * 512)
                    mm_evs.append(p.op("tensor", lambda t, a=a, hh=hh, po=po, hsb=hsb: t.matmul(
                        ps[:, po + hh * 512:po + (hh + 1) * 512], hsb[:, :],
                        Lk(a)[:, hh * 512:(hh + 1) * 512],
                        start=(a == 0), stop=(a == 15)), waits=w, inc="pe"))
            ev_e = p.op("vector", lambda v, a2=a2, po=po: v.tensor_scalar(
                Bk(a2), ps[:, po:po + 1024], HAD_N1, None, op0=ALU.mult),
                waits=mm_evs + [p.gw(("bbuf", a2))], inc="ve")
            p.ps_r(po, po + 1024, ev_e)
            xp_evs.append(ev_e)
        tap("xp", buf_B[:, :], (128, 16384), xp_evs, BF16)
        p.barrier()

        for q in range(4):
            h_evs = []
            for icl in range(16):
                I = q * 16 + icl
                upsrc = ap["w_upT"].rearrange(
                    "(k pp) (i f) -> pp k i f", pp=128, f=128)[:, :, I, :]
                gtsrc = ap["w_gateT"].rearrange(
                    "(k pp) (i f) -> pp k i f", pp=128, f=128)[:, :, I, :]
                su_ = icl % 2
                sg_ = 2 + icl % 2
                evwu = d("gpsimd", Wslot(su_).rearrange("pp (k f) -> pp k f", f=128),
                         upsrc, waits=[p.gw(("wslot", su_))])
                evwg = d("gpsimd", Wslot(sg_).rearrange("pp (k f) -> pp k f", f=128),
                         gtsrc, waits=[p.gw(("wslot", sg_))])
                po_u = (icl % 2) * 1024
                po_g = 2048 + (icl % 2) * 1024
                mmu, mmg = [], []
                for k in range(16):
                    for hh in range(2):
                        wu = [evwu, xp_evs[15]]
                        wg = [evwg, xp_evs[15]]
                        if k == 0:
                            wu += p.ps_w(po_u + hh * 512, po_u + (hh + 1) * 512)
                            wg += p.ps_w(po_g + hh * 512, po_g + (hh + 1) * 512)
                        mmu.append(p.op("tensor", lambda t, k=k, hh=hh, po=po_u, su_=su_: t.matmul(
                            ps[:, po + hh * 512:po + (hh + 1) * 512],
                            Wslot(su_)[:, k * 128:(k + 1) * 128],
                            Bk(k)[:, hh * 512:(hh + 1) * 512],
                            start=(k == 0), stop=(k == 15)), waits=wu, inc="pe"))
                        mmg.append(p.op("tensor", lambda t, k=k, hh=hh, po=po_g, sg_=sg_: t.matmul(
                            ps[:, po + hh * 512:po + (hh + 1) * 512],
                            Wslot(sg_)[:, k * 128:(k + 1) * 128],
                            Bk(k)[:, hh * 512:(hh + 1) * 512],
                            start=(k == 0), stop=(k == 15)), waits=wg, inc="pe"))
                p.gs(("wslot", su_), mmu[-1])
                p.gs(("wslot", sg_), mmg[-1])
                ssl = icl % 3
                ev_silu = p.op("scalar", lambda s, po=po_g, I=I, ssl=ssl: s.activation(
                    out=Sslot(ssl), in_=ps[:, po:po + 1024], func=AF.Silu,
                    scale=wsg_sb[:, I:I + 1]),
                    waits=mmg + [p.gw(("sslot", ssl))] + EVI, inc="se")
                ev_h = p.op("vector", lambda v, po=po_u, I=I, icl=icl, ssl=ssl: v.scalar_tensor_tensor(
                    Ak(icl), ps[:, po:po + 1024], wsu_sb[:, I:I + 1], Sslot(ssl),
                    op0=ALU.mult, op1=ALU.mult),
                    waits=[ev_silu, p.gw(("abuf", icl))] + mmu + EVI, inc="ve")
                p.ps_r(po_u, po_u + 1024, ev_h)
                p.ps_r(po_g, po_g + 1024, ev_h)
                p.gs(("sslot", ssl), ev_h)
                h_evs.append(ev_h)
            for m in range(16):
                wsl = m % 4
                s_rank = m // 2
                wsrc = wpp_all.ap()[s_rank * 8192 + q * 2048:s_rank * 8192 + (q + 1) * 2048,
                                    (m % 2) * 128:(m % 2) * 128 + 128]
                wsrc = wsrc.rearrange("(i pp) f -> pp i f", pp=128)
                evw = d("sync", Wslot(wsl).rearrange("pp (i f) -> pp i f", f=128),
                        wsrc, waits=[p.gw(("wslot", wsl)), W2["ev"]])
                po = (m % 2) * 1024
                mm_evs = []
                for icl in range(16):
                    for hh in range(2):
                        w = [evw, h_evs[15]]
                        if icl == 0:
                            w += p.ps_w(po + hh * 512, po + (hh + 1) * 512)
                        mm_evs.append(p.op("tensor", lambda t, icl=icl, hh=hh, po=po, wsl=wsl: t.matmul(
                            ps[:, po + hh * 512:po + (hh + 1) * 512],
                            Wslot(wsl)[:, icl * 128:(icl + 1) * 128],
                            Ak(icl)[:, hh * 512:(hh + 1) * 512],
                            start=(icl == 0), stop=(icl == 15)), waits=w, inc="pe"))
                p.gs(("wslot", wsl), mm_evs[-1])
                ev_add = p.op("vector", lambda v, m=m, po=po: v.scalar_tensor_tensor(
                    Rk(m), ps[:, po:po + 1024], wsdp_sb[:, m:m + 1], Rk(m),
                    op0=ALU.mult, op1=ALU.add),
                    waits=mm_evs + [resid_ev[m]] + EVI, inc="ve")
                p.ps_r(po, po + 1024, ev_add)
                resid_ev[m] = ev_add
            for icl in range(16):
                p.gs(("abuf", icl), resid_ev[15])
        for reg in range(2):
            p.gs(("bufB", reg), resid_ev[15])
        p.barrier()

    attn_sublayer("a")
    tap("r_a", buf_R[:, :], (128, 16384), list(resid_ev))
    ffn_sublayer()
    tap("r_f1", buf_R[:, :], (128, 16384), list(resid_ev))
    attn_sublayer("b")
    tap("r_b", buf_R[:, :], (128, 16384), list(resid_ev))
    ffn_sublayer()

    st_evs = [d("sync", out_x[k * 128:(k + 1) * 128, :], Rk(k),
                waits=[resid_ev[k]]) for k in range(NCH)]
    p.op("gpsimd", lambda g: g.engine_nop(), waits=st_evs)

    with nc.Block() as block:
        OWN = {} if DBG_KEEP_SELF else {"tensor": "pe"}

        def emitter(name):
            own = OWN.get(name)
            def run(eng):
                seen = {}
                for fn, ws, inc, amt in p.ops[name]:
                    for s, v in sorted(ws.items()):
                        if s == own:
                            continue
                        if seen.get(s, 0) < v:
                            eng.wait_ge(sems[s], v)
                            seen[s] = v
                    inst = fn(eng)
                    if inc is not None:
                        inst.then_inc(sems[inc], amt)
            return run
        block.tensor(emitter("tensor"))
        block.vector(emitter("vector"))
        block.scalar(emitter("scalar"))
        block.gpsimd(emitter("gpsimd"))
        block.sync(emitter("sync"))

    es.close()
    return nc, dbg


_cached = {}


def kernel(**inputs):
    maps = _prep_inputs(inputs)
    key = tuple(sorted(_TAPS))
    if key not in _cached:
        _cached[key] = build_program()
    nc, dbg = _cached[key]
    res = run_bass_kernel_spmd(nc, maps, core_ids=list(range(8)))
    out = _assemble(res.results)
    if dbg:
        kernel._dbg = [{t: np.asarray(r["dbg_" + t]) for t in dbg} for r in res.results]
    return out


kernel._dbg = None


# version 13
# speedup vs baseline: 1.2268x; 1.0032x over previous
import numpy as np
import ml_dtypes
from contextlib import ExitStack

import concourse.bass as bass
import concourse.mybir as mybir
import concourse.bass_utils as _BU
from concourse.bass_utils import run_bass_kernel_spmd

if not getattr(_BU, "_ldw_patched", False):
    _orig_run_command = _BU.run_command

    def _run_command_ldw(argv, **kwargs):
        try:
            if isinstance(argv, list) and "--enable-ldw-opt=false" in argv:
                argv = ["--enable-ldw-opt=true" if a == "--enable-ldw-opt=false"
                        else a for a in argv]
        except Exception:
            pass
        return _orig_run_command(argv, **kwargs)

    _BU.run_command = _run_command_ldw
    _BU._ldw_patched = True

F32 = mybir.dt.float32
BF16 = mybir.dt.bfloat16
AF = mybir.ActivationFunctionType
ALU = mybir.AluOpType
AX = mybir.AxisListType
NPBF = ml_dtypes.bfloat16

NCH = 16
NQT = 8
NKB = 32
ISQ_D = float(1.0 / np.sqrt(128.0))
HAD_N1 = float(1.0 / np.sqrt(2048.0))
HAD_N2 = float(1.0 / np.sqrt(8192.0))
EPS = 1e-5
B_SAFETY = 1.05
NEG_BIG = -30000.0

_TAPS = set()
import os as _os
DBG_NO_W2 = bool(int(_os.environ.get("K_DBG_NO_W2", "0")))
DBG_NO_ROPE = bool(int(_os.environ.get("K_DBG_NO_ROPE", "0")))
DBG_NO_COLL = bool(int(_os.environ.get("K_DBG_NO_COLL", "0")))
DBG_KEEP_SELF = bool(int(_os.environ.get("K_DBG_KEEP_SELF", "0")))

ENGS = ("tensor", "vector", "scalar", "gpsimd", "sync")
SEMS = ("dg", "ds", "dc", "pe", "ve", "se", "cc")
DMA_SEMS = ("dg", "ds", "dc")


class Prog:
    def __init__(self):
        self.ops = {e: [] for e in ENGS}
        self.plan = {s: 0 for s in SEMS}
        self.pending = {e: {} for e in ENGS}
        self.guards = {}

    def op(self, eng, fn, waits=(), inc=None, amt=None):
        ws = dict(self.pending[eng])
        self.pending[eng] = {}
        for w in waits:
            if w is None:
                continue
            s, v = w
            ws[s] = max(ws.get(s, 0), v)
        ev = None
        if inc is not None:
            if amt is None:
                amt = 16 if inc in DMA_SEMS else 1
            self.plan[inc] += amt
            ev = (inc, self.plan[inc])
        self.ops[eng].append((fn, ws, inc, amt))
        return ev

    def barrier(self):
        evs = [(s, self.plan[s]) for s in SEMS if self.plan[s] > 0]
        for e in ("tensor", "vector", "scalar"):
            dd = self.pending[e]
            for s, v in evs:
                dd[s] = max(dd.get(s, 0), v)

    def gw(self, key):
        return self.guards.get(key)

    def gs(self, key, ev):
        self.guards[key] = ev

    def ps_w(self, lo, hi):
        return [self.gw(("psb", b)) for b in range(lo // 512, (hi + 511) // 512)]

    def ps_r(self, lo, hi, ev):
        for b in range(lo // 512, (hi + 511) // 512):
            self.gs(("psb", b), ev)



def _rope_tables(positions, base):
    inv = 1.0 / base ** (np.arange(0, 128, 2, dtype=np.float64) / 128.0)
    ang = np.outer(inv, positions.astype(np.float64))
    cos = np.concatenate([np.cos(ang), np.cos(ang)], axis=0)
    sin = np.concatenate([np.sin(ang), np.sin(ang)], axis=0)
    return cos.astype(NPBF), sin.astype(NPBF)


def _mask_tile(r):
    kk = np.arange(128)[:, None]
    qi = np.arange(128)[None, :]
    tiles = []
    for dd in range(4):
        allowed = (128 * dd + kk) <= (128 * r + qi)
        tiles.append(np.where(allowed, 0.0, NEG_BIG))
    return np.concatenate(tiles, axis=1).astype(NPBF)


def _prep_inputs(inp):
    asc = np.ascontiguousarray
    x = np.asarray(inp["L0_attn_a_normalized_resid_pre"], dtype=np.float32)
    perm = np.concatenate([np.arange(0, 128, 2), np.arange(1, 128, 2)])

    wq = np.asarray(inp["L0_attn_a_W_Q"], np.float32)[:, :, perm]
    wk = np.asarray(inp["L0_attn_a_W_K"], np.float32)[:, :, perm]
    wv = np.asarray(inp["L0_attn_a_W_V"], np.float32)
    shared = {
        "wq_a": asc(wq.transpose(1, 0, 2).reshape(2048, 2048)),
        "wk_a": asc(wk.transpose(1, 0, 2).reshape(2048, 2048)),
        "wv_a": asc(wv.transpose(1, 0, 2).reshape(2048, 2048)),
        "wo_a": asc(np.asarray(inp["L0_attn_a_W_O"], np.float32).reshape(2048, 2048)),
        "bq_a": asc(np.asarray(inp["L0_attn_a_b_Q"], np.float32)[:, perm].T),
        "bk_a": asc(np.asarray(inp["L0_attn_a_b_K"], np.float32)[:, perm].T),
        "bv_col": asc(np.asarray(inp["L0_attn_a_b_V"], np.float32).reshape(16, 128).T),
        "bo_col": asc(np.asarray(inp["L0_attn_a_b_O"], np.float32).reshape(16, 128).T),
        "wqkv_b": asc(np.asarray(inp["L0_attn_b_w_qkv"], np.float32).T),
        "wo_b": asc(np.asarray(inp["L0_attn_b_w_o"], np.float32).T),
        "qng": asc(np.asarray(inp["L0_attn_b_qn_g"], np.float32).reshape(128, 1)),
        "qnb": asc(np.asarray(inp["L0_attn_b_qn_b"], np.float32).reshape(128, 1)),
        "kng": asc(np.asarray(inp["L0_attn_b_kn_g"], np.float32).reshape(128, 1)),
        "knb": asc(np.asarray(inp["L0_attn_b_kn_b"], np.float32).reshape(128, 1)),
        "w_upT": asc(np.asarray(inp["L0_ffn_W_up"], np.float32).T),
        "w_gateT": asc(np.asarray(inp["L0_ffn_W_gate"], np.float32).T),
        "su_ug": asc(np.asarray(inp["L0_ffn_SU_ug"], np.float32).reshape(16, 128).T),
        "su_dp_re": asc(np.asarray(inp["L0_ffn_SU_dp"], np.float32).reshape(64, 128)),
        "wsu": asc(np.asarray(inp["L0_ffn_Wscale_ug"], np.float32)[:8192].reshape(64, 128).T),
        "wsg": asc(np.asarray(inp["L0_ffn_Wscale_ug"], np.float32)[8192:].reshape(64, 128).T),
        "wsdp": asc(np.asarray(inp["L0_ffn_Wscale_dp"], np.float32).reshape(16, 128).T),
    }
    w_downT = np.asarray(inp["L0_ffn_W_down"], np.float32).T
    maps = []
    for c in range(8):
        g, r = divmod(c, 4)
        blocks = [4 * m + r for m in range(8)]
        pos = np.concatenate([np.arange(128) + 128 * b for b in blocks])
        xc = x[g].reshape(32, 128, 2048)[blocks].reshape(1024, 2048)
        cos_a, sin_a = _rope_tables(pos, 10000.0)
        cos_b, sin_b = _rope_tables(pos, 1000000.0)
        m = dict(shared)
        m["xT"] = asc(xc.T)
        m["w_downT_sl"] = asc(w_downT[:, c * 256:(c + 1) * 256])
        m["cos_a"] = asc(cos_a); m["sin_a"] = asc(sin_a)
        m["cos_b"] = asc(cos_b); m["sin_b"] = asc(sin_b)
        m["maskT"] = _mask_tile(r)
        maps.append(m)
    return maps


def _assemble(results):
    out = np.empty((2, 4096, 2048), np.float32)
    for c in range(8):
        g, r = divmod(c, 4)
        xc = np.asarray(results[c]["out_x"]).T.reshape(8, 128, 2048)
        for mm in range(8):
            b = 4 * mm + r
            out[g, b * 128:(b + 1) * 128, :] = xc[mm]
    return out



def build_program():
    nc = bass.Bass("TRN2", target_bir_lowering=False, debug=False)
    p = Prog()

    ap = {}
    def ein(name, shape, dt=F32):
        ap[name] = nc.dram_tensor(name, list(shape), dt, kind="ExternalInput").ap()
    ein("xT", (2048, 1024))
    for n in ("wq_a", "wk_a", "wv_a", "wo_a"):
        ein(n, (2048, 2048))
    ein("bq_a", (128, 16)); ein("bk_a", (128, 16))
    ein("bv_col", (128, 16)); ein("bo_col", (128, 16))
    ein("wqkv_b", (2048, 3072)); ein("wo_b", (2048, 2048))
    for n in ("qng", "qnb", "kng", "knb"):
        ein(n, (128, 1))
    ein("w_upT", (2048, 8192)); ein("w_gateT", (2048, 8192))
    ein("w_downT_sl", (8192, 256))
    ein("su_ug", (128, 16)); ein("su_dp_re", (64, 128))
    ein("wsu", (128, 64)); ein("wsg", (128, 64)); ein("wsdp", (128, 16))
    for n in ("cos_a", "sin_a", "cos_b", "sin_b"):
        ein(n, (128, 1024), BF16)
    ein("maskT", (128, 512), BF16)

    out_x = nc.dram_tensor("out_x", [2048, 1024], F32, kind="ExternalOutput").ap()

    kvA_own = nc.dram_tensor("kvA_own", [4224, 1024], BF16, kind="Internal")
    kA_all = nc.dram_tensor("kA_all", [4 * 2176, 1024], BF16, kind="Internal")
    vA_all = nc.dram_tensor("vA_all", [4 * 2048, 1024], BF16, kind="Internal")
    kvB_own = nc.dram_tensor("kvB_own", [1152, 1024], BF16, kind="Internal")
    kvB_all = nc.dram_tensor("kvB_all", [4 * 1152, 1024], BF16, kind="Internal")
    vd = nc.dram_tensor("vd", [8192, 256], BF16, kind="Internal")
    wpp_own = nc.dram_tensor("wpp_own", [8192, 256], BF16, kind="Internal")
    wpp_all = nc.dram_tensor("wpp_all", [8 * 8192, 256], BF16, kind="Internal")

    ic = lambda a, n: nc.inline_tensor(np.ascontiguousarray(a), name=n).ap()
    ones_col_f = ic(np.ones((128, 1), np.float32), "c_ocf")
    ones_col_b = ic(np.ones((128, 1), NPBF), "c_ocb")
    ones_row_f = ic(np.ones((1, 128), np.float32), "c_orf")
    ident_b = ic(np.eye(128, dtype=NPBF), "c_id")
    rotT_np = np.zeros((128, 128), np.float32)
    for i in range(64):
        rotT_np[i, 64 + i] = 1.0
        rotT_np[64 + i, i] = -1.0
    rotT_c = ic(rotT_np.astype(NPBF), "c_rotT")

    def _hmat(n):
        h = np.array([[1.0]])
        while h.shape[0] < n:
            h = np.block([[h, h], [h, -h]])
        return h
    H16 = _hmat(16)
    Hp_c = ic(_hmat(128).astype(NPBF), "c_hp")
    Hm_c = ic((-_hmat(128)).astype(NPBF), "c_hm")
    H64_c = ic(_hmat(64).astype(NPBF), "c_h64")

    dbg = {}
    es = ExitStack()
    ec = es.enter_context

    buf_R = ec(nc.sbuf_tensor("buf_R", [128, 16384], F32))
    buf_L = ec(nc.sbuf_tensor("buf_L", [128, 16384], BF16))
    buf_A = ec(nc.sbuf_tensor("buf_A", [128, 16384], BF16))
    buf_B = ec(nc.sbuf_tensor("buf_B", [128, 16384], BF16))
    buf_W = ec(nc.sbuf_tensor("buf_W", [128, 8192], BF16))
    buf_S = ec(nc.sbuf_tensor("buf_S", [128, 3072], BF16))
    buf_K = ec(nc.sbuf_tensor("buf_K", [128, 2048], BF16))
    scr = ec(nc.sbuf_tensor("scr", [128, 1024], F32))
    rows = ec(nc.sbuf_tensor("rows", [1, 2048], F32))
    tabs = ec(nc.sbuf_tensor("tabs", [128, 2048], BF16))
    mask_sb = ec(nc.sbuf_tensor("mask_sb", [128, 512], BF16))
    bo_sb = ec(nc.sbuf_tensor("bo_sb", [128, 16], F32))
    bv_sb = ec(nc.sbuf_tensor("bv_sb", [128, 16], F32))
    vec16 = ec(nc.sbuf_tensor("vec16", [1, 128], F32))
    bq_sb = ec(nc.sbuf_tensor("bq_sb", [128, 16], F32))
    bk_sb = ec(nc.sbuf_tensor("bk_sb", [128, 16], F32))
    qn_sb = ec(nc.sbuf_tensor("qn_sb", [128, 4], F32))
    su_sb = ec(nc.sbuf_tensor("su_sb", [128, 16], F32))
    sudp_sb = ec(nc.sbuf_tensor("sudp_sb", [64, 128], F32))
    wsu_sb = ec(nc.sbuf_tensor("wsu_sb", [128, 64], F32))
    wsg_sb = ec(nc.sbuf_tensor("wsg_sb", [128, 64], F32))
    wsdp_sb = ec(nc.sbuf_tensor("wsdp_sb", [128, 16], F32))
    negB_sb = ec(nc.sbuf_tensor("negB_sb", [128, 16], F32))
    eps_sb = ec(nc.sbuf_tensor("eps_sb", [1, 1], F32))
    oc_f = ec(nc.sbuf_tensor("oc_f", [128, 1], F32))
    oc_b = ec(nc.sbuf_tensor("oc_b", [128, 1], BF16))
    or_f = ec(nc.sbuf_tensor("or_f", [1, 128], F32))
    id_sb = ec(nc.sbuf_tensor("id_sb", [128, 128], BF16))
    rot_sb = ec(nc.sbuf_tensor("rot_sb", [128, 128], BF16))
    hp_sb = ec(nc.sbuf_tensor("hp_sb", [128, 128], BF16))
    hm_sb = ec(nc.sbuf_tensor("hm_sb", [128, 128], BF16))
    h64_sb = ec(nc.sbuf_tensor("h64_sb", [64, 64], BF16))

    ps = ec(nc.psum_tensor("ps", [128, 4096], F32))
    sems = {s: ec(nc.semaphore("sem_" + s)) for s in SEMS}

    def Rk(k): return buf_R[:, k * 1024:(k + 1) * 1024]
    def Lk(k): return buf_L[:, k * 1024:(k + 1) * 1024]
    def Ak(k): return buf_A[:, k * 1024:(k + 1) * 1024]
    def Bk(k): return buf_B[:, k * 1024:(k + 1) * 1024]
    def Wslot(i): return buf_W[:, i * 2048:(i + 1) * 2048]
    def Sslot(i): return buf_S[:, i * 1024:(i + 1) * 1024]
    def Kslot(j): return buf_K[:, j * 1024:j * 1024 + 512]
    def Vslot(j): return buf_K[:, j * 1024 + 512:(j + 1) * 1024]

    resid_ev = [None] * NCH
    ln_ev = [None] * NCH

    def d(eng, out, in_, waits=()):
        sem = {"gpsimd": "dg", "sync": "ds", "scalar": "dc"}[eng]
        w = list(waits) + [p.guards.get("coll_last")]
        return p.op(eng, lambda e, o=out, i=in_: e.dma_start(out=o, in_=i),
                    waits=w, inc=sem)

    def coll(kind, op, groups, in_t, out_t, waits, in_ap=None, out_ap=None):
        ia = in_ap if in_ap is not None else in_t.ap()
        oa = out_ap if out_ap is not None else out_t.ap()
        if DBG_NO_COLL:
            n = oa.shape[0] // ia.shape[0]
            rows_ = ia.shape[0]
            ev = None
            for i in range(n):
                ev = d("gpsimd", oa[i * rows_:(i + 1) * rows_, :], ia,
                       waits=list(waits))
            return p.op("gpsimd", lambda g: g.engine_nop(), waits=[ev], inc="cc")
        w = list(waits) + [(s, p.plan[s]) for s in DMA_SEMS if p.plan[s] > 0]
        ev = p.op("gpsimd", lambda g, k=kind, o=op, gr=groups, ia=ia, oa=oa:
                  g.collective_compute(k, o, replica_groups=gr,
                                       ins=[ia], outs=[oa]),
                  waits=w, inc="cc")
        p.gs("coll_last", ev)
        return ev

    def tap(tag, src_ap, shape, waits, dt=F32):
        if tag not in _TAPS or tag in dbg:
            return
        t = nc.dram_tensor("dbg_" + tag, list(shape), dt, kind="ExternalOutput").ap()
        dbg[tag] = True
        d("sync", t, src_ap, waits=list(waits))

    ev_init = []
    for dst, src in ((oc_f[:, :], ones_col_f), (oc_b[:, :], ones_col_b),
                     (or_f[:, :], ones_row_f),
                     (id_sb[:, :], ident_b), (rot_sb[:, :], rotT_c),
                     (hp_sb[:, :], Hp_c), (hm_sb[:, :], Hm_c),
                     (h64_sb[:, :], H64_c),
                     (bq_sb[:, :], ap["bq_a"]), (bk_sb[:, :], ap["bk_a"]),
                     (qn_sb[:, 0:1], ap["qng"]), (qn_sb[:, 1:2], ap["qnb"]),
                     (qn_sb[:, 2:3], ap["kng"]), (qn_sb[:, 3:4], ap["knb"]),
                     (su_sb[:, :], ap["su_ug"]), (sudp_sb[:, :], ap["su_dp_re"]),
                     (wsu_sb[:, :], ap["wsu"]), (wsg_sb[:, :], ap["wsg"]),
                     (wsdp_sb[:, :], ap["wsdp"]), (mask_sb[:, :], ap["maskT"]),
                     (bo_sb[:, :], ap["bo_col"]), (bv_sb[:, :], ap["bv_col"])):
        ev_init.append(d("scalar", dst, src))
    ev_memset = p.op("vector", lambda v: v.memset(eps_sb[:, :], EPS), inc="ve")
    ev_init.append(ev_memset)
    EVI = list(ev_init)

    for k in range(NCH):
        resid_ev[k] = d("sync", Rk(k), ap["xT"][k * 128:(k + 1) * 128, :])

    def w2_precompute():
        ev_st1 = []
        for bi in range(8):
            half = bi % 2
            src = ap["w_downT_sl"].rearrange("(b a pp) r -> b pp a r", b=8, pp=128)[bi]
            dstA = buf_A[:, half * 2048:(half + 1) * 2048]
            evw = d("gpsimd", dstA.rearrange("pp (a r) -> pp a r", r=256), src,
                    waits=[p.gw(("w2in", half))] + EVI)
            evac_evs = []
            for a8 in range(8):
                lo = (a8 % 4) * 256 + (a8 // 4) * 1024
                evm = p.op("tensor", lambda t, h=half, a=a8, lo=lo: t.matmul(
                    ps[:, lo:lo + 256], hp_sb[:, :],
                    buf_A[:, h * 2048 + a * 256:h * 2048 + (a + 1) * 256],
                    start=True, stop=True),
                    waits=[evw] + p.ps_w(lo, lo + 256), inc="pe")
                eve = p.op("vector", lambda v, h=half, a=a8, lo=lo: v.tensor_scalar(
                    buf_A[:, 4096 + h * 2048 + a * 256:4096 + h * 2048 + (a + 1) * 256],
                    ps[:, lo:lo + 256], HAD_N2, None, op0=ALU.mult),
                    waits=[evm, p.gw(("w2out", half))], inc="ve")
                p.ps_r(lo, lo + 256, eve)
                evac_evs.append(eve)
            dstv = vd.ap().rearrange("(b a pp) r -> b pp a r", b=8, pp=128)[bi]
            srcv = buf_A[:, 4096 + half * 2048:4096 + (half + 1) * 2048]
            evd = d("sync", dstv, srcv.rearrange("pp (a r) -> pp a r", r=256),
                    waits=evac_evs)
            p.gs(("w2in", half), evd)
            p.gs(("w2out", half), evd)
            ev_st1.append(evd)
        ev_out = []
        for pb in range(32):
            half = pb % 2
            src = vd.ap().rearrange("(a pp) r -> pp a r", pp=128)[pb * 4:(pb + 1) * 4]
            src = src.rearrange("q a r -> a q r")
            dst = buf_A[0:64, 8192 + half * 1024:8192 + (half + 1) * 1024]
            evw = d("sync", dst.rearrange("a (q r) -> a q r", r=256), src,
                    waits=[p.gw(("w2in2", half))] + ev_st1)
            lo = 2048 + half * 1024
            mevs = []
            for hh in range(2):
                mevs.append(p.op("tensor", lambda t, h=half, hh=hh, lo=lo: t.matmul(
                    ps[0:64, lo + hh * 512:lo + (hh + 1) * 512], h64_sb[:, :],
                    buf_A[0:64, 8192 + h * 1024 + hh * 512:8192 + h * 1024 + (hh + 1) * 512],
                    start=True, stop=True),
                    waits=[evw] + p.ps_w(lo, lo + 1024) + EVI, inc="pe"))
            evs = []
            for q in range(4):
                P = pb * 4 + q
                eve = p.op("vector", lambda v, h=half, q=q, P=P, lo=lo: v.tensor_scalar(
                    buf_A[0:64, 12288 + h * 1024 + q * 256:12288 + h * 1024 + (q + 1) * 256],
                    ps[0:64, lo + q * 256:lo + (q + 1) * 256],
                    sudp_sb[:, P:P + 1], None, op0=ALU.mult),
                    waits=mevs + [p.gw(("w2out2", half))] + EVI, inc="ve")
                evs.append(eve)
            p.ps_r(lo, lo + 1024, evs[-1])
            for q in range(4):
                P = pb * 4 + q
                dstw = wpp_own.ap().rearrange("(a pp) r -> pp a r", pp=128)[P]
                evd = d("sync", dstw,
                        buf_A[0:64, 12288 + half * 1024 + q * 256:12288 + half * 1024 + (q + 1) * 256].rearrange(
                            "a (o r) -> a o r", o=1)[:, 0, :] if False else
                        buf_A[0:64, 12288 + half * 1024 + q * 256:12288 + half * 1024 + (q + 1) * 256],
                        waits=evs)
                ev_out.append(evd)
            p.gs(("w2in2", half), ev_out[-1])
            p.gs(("w2out2", half), ev_out[-1])
        return ev_out

    w2_out_evs = [] if DBG_NO_W2 else w2_precompute()
    W2 = {"ev": None}

    def emit_layernorm(su_fold):
        mm_last = None
        for k in range(NCH):
            sl = k % 3
            evq = p.op("scalar", lambda s, k=k, sl=sl: s.square(Sslot(sl), Rk(k)),
                       waits=[resid_ev[k], p.gw(("sslot", sl))], inc="se")
            for hh in range(2):
                w1 = [resid_ev[k]] + (p.ps_w(2048, 3072) if k == 0 else [])
                p.op("tensor", lambda t, k=k, hh=hh: t.matmul(
                    ps[0:1, 2048 + hh * 512:2048 + (hh + 1) * 512], oc_f[:, :],
                    Rk(k)[:, hh * 512:(hh + 1) * 512],
                    start=(k == 0), stop=(k == NCH - 1)), waits=w1, inc="pe")
                w2 = [evq] + (p.ps_w(3072, 4096) if k == 0 else [])
                mm_last = p.op("tensor", lambda t, k=k, hh=hh, sl=sl: t.matmul(
                    ps[0:1, 3072 + hh * 512:3072 + (hh + 1) * 512], oc_b[:, :],
                    Sslot(sl)[:, hh * 512:(hh + 1) * 512],
                    start=(k == 0), stop=(k == NCH - 1)), waits=w2, inc="pe")
            p.gs(("sslot", sl), mm_last)
        ev_mu = p.op("vector", lambda v: v.tensor_scalar(
            rows[0:1, 0:1024], ps[0:1, 2048:3072], 1.0 / 2048.0, None, op0=ALU.mult),
            waits=[mm_last, p.gw("rows")], inc="ve")
        ev_var = p.op("vector", lambda v: v.tensor_scalar(
            rows[0:1, 1024:2048], ps[0:1, 3072:4096], 1.0 / 2048.0, None, op0=ALU.mult),
            waits=[mm_last], inc="ve")
        p.ps_r(2048, 4096, ev_var)
        ev_t = p.op("vector", lambda v: v.tensor_mul(
            scr[0:1, 0:1024], rows[0:1, 0:1024], rows[0:1, 0:1024]),
            waits=[ev_mu, p.gw("scr")], inc="ve")
        ev_var2 = p.op("vector", lambda v: v.tensor_sub(
            rows[0:1, 1024:2048], rows[0:1, 1024:2048], scr[0:1, 0:1024]),
            waits=[ev_var, ev_t], inc="ve")
        ev_sd = p.op("scalar", lambda s: s.activation(
            out=scr[0:1, 0:1024], in_=rows[0:1, 1024:2048], func=AF.Sqrt,
            bias=eps_sb[0:1, 0:1]),
            waits=[ev_var2, ev_memset], inc="se")
        ev_rstd = p.op("vector", lambda v: v.reciprocal(
            rows[0:1, 1024:2048], scr[0:1, 0:1024]), waits=[ev_sd], inc="ve")
        bevs = []
        for hh in range(2):
            bevs.append(p.op("tensor", lambda t, hh=hh: t.matmul(
                ps[:, hh * 512:(hh + 1) * 512], or_f[0:1, :],
                rows[0:1, hh * 512:(hh + 1) * 512], start=True, stop=True),
                waits=[ev_mu] + p.ps_w(hh * 512, (hh + 1) * 512), inc="pe"))
            bevs.append(p.op("tensor", lambda t, hh=hh: t.matmul(
                ps[:, 1024 + hh * 512:1024 + (hh + 1) * 512], or_f[0:1, :],
                rows[0:1, 1024 + hh * 512:1024 + (hh + 1) * 512], start=True, stop=True),
                waits=[ev_rstd] + p.ps_w(1024 + hh * 512, 1024 + (hh + 1) * 512),
                inc="pe"))
        p.gs("rows", bevs[-1])
        for k in range(NCH):
            ev1 = p.op("vector", lambda v, k=k: v.tensor_sub(
                scr[:, :], Rk(k), ps[:, 0:1024]),
                waits=bevs + [resid_ev[k], p.gw("scr")], inc="ve")
            if su_fold:
                ev2 = p.op("vector", lambda v, k=k: v.scalar_tensor_tensor(
                    Lk(k), scr[:, :], su_sb[:, k:k + 1], ps[:, 1024:2048],
                    op0=ALU.mult, op1=ALU.mult),
                    waits=[ev1] + EVI, inc="ve")
            else:
                ev2 = p.op("vector", lambda v, k=k: v.tensor_tensor(
                    Lk(k), scr[:, :], ps[:, 1024:2048], op=ALU.mult),
                    waits=[ev1], inc="ve")
            p.gs("scr", ev2)
            ln_ev[k] = ev2
        p.ps_r(0, 2048, ln_ev[NCH - 1])

    def rope_chunk(dst_ap, src_ev, tab_ev):
        if DBG_NO_ROPE:
            return src_ev
        lo = 2048
        rot_evs = []
        for hh in range(2):
            rot_evs.append(p.op("tensor", lambda t, hh=hh: t.matmul(
                ps[:, lo + hh * 512:lo + (hh + 1) * 512], rot_sb[:, :],
                dst_ap[:, hh * 512:(hh + 1) * 512], start=True, stop=True),
                waits=[src_ev] + p.ps_w(lo, lo + 1024) + EVI, inc="pe"))
        ev1 = p.op("vector", lambda v: v.tensor_tensor(
            Sslot(2), ps[:, lo:lo + 1024], tabs[:, 1024:2048], op=ALU.mult),
            waits=rot_evs + [p.gw(("sslot", 2))] + tab_ev, inc="ve")
        p.ps_r(lo, lo + 1024, ev1)
        ev2 = p.op("vector", lambda v: v.tensor_tensor(
            dst_ap, dst_ap, tabs[:, 0:1024], op=ALU.mult),
            waits=rot_evs + tab_ev, inc="ve")
        ev3 = p.op("vector", lambda v: v.tensor_add(dst_ap, dst_ap, Sslot(2)),
                   waits=[ev1, ev2], inc="ve")
        p.gs(("sslot", 2), ev3)
        return ev3

    def norm2max_chunk(src_ap, src_ev, vec_off):
        evq = p.op("scalar", lambda s: s.square(Sslot(2), src_ap),
                   waits=[src_ev, p.gw(("sslot", 2))], inc="se")
        mevs = []
        for hh in range(2):
            mevs.append(p.op("tensor", lambda t, hh=hh: t.matmul(
                ps[0:1, 3072 + hh * 512:3072 + (hh + 1) * 512], oc_b[:, :],
                Sslot(2)[:, hh * 512:(hh + 1) * 512], start=True, stop=True),
                waits=[evq] + p.ps_w(3072, 4096), inc="pe"))
        ev = p.op("vector", lambda v, vo=vec_off: v.reduce_max(
            vec16[0:1, vo:vo + 1], ps[0:1, 3072:4096], axis=AX.X),
            waits=mevs, inc="ve")
        p.ps_r(3072, 4096, ev)
        p.gs(("sslot", 2), ev)
        return ev

    def attn_sublayer(var):
        is_a = (var == "a")
        n_kch = 16 if is_a else 4
        fw = 512 if is_a else 128
        NR = 2048 if is_a else 512
        VOFF = NR + 128
        VROWS = 2048 if is_a else 512
        kv_own = kvA_own if is_a else kvB_own

        tev1 = d("sync", tabs[:, 0:1024], ap["cos_" + var], waits=[p.gw("tabs")])
        tev2 = d("sync", tabs[:, 1024:2048], ap["sin_" + var], waits=[p.gw("tabs")])
        tab_ev = [tev1, tev2]

        emit_layernorm(False)
        tap("ln_" + var, buf_L[:, :], (128, 16384), list(ln_ev), BF16)
        p.barrier()

        if is_a:
            w_k_src = lambda m: ap["wk_a"].rearrange(
                "(k pp) (mm f) -> pp k mm f", pp=128, f=128)[:, :, m, :]
            w_q_src = lambda m: ap["wq_a"].rearrange(
                "(k pp) (mm f) -> pp k mm f", pp=128, f=128)[:, :, m, :]
        else:
            w_k_src = lambda m: ap["wqkv_b"].rearrange(
                "(k pp) (mm f) -> pp k mm f", pp=128, f=128)[:, :, 16 + m, :]
            w_q_src = lambda m: ap["wqkv_b"].rearrange(
                "(k pp) (mm f) -> pp k mm f", pp=128, f=128)[:, :, m, :]

        def proj_mms(lhs_fn, evw, po):
            evs = []
            ln_last = ln_ev[NCH - 1]
            for k in range(NCH):
                for hh in range(2):
                    w = [evw, ln_last]
                    if k == 0:
                        w += p.ps_w(po + hh * 512, po + (hh + 1) * 512)
                    evs.append(p.op("tensor", lambda t, k=k, hh=hh, po=po: t.matmul(
                        ps[:, po + hh * 512:po + (hh + 1) * 512],
                        lhs_fn(k),
                        Lk(k)[:, hh * 512:(hh + 1) * 512],
                        start=(k == 0), stop=(k == NCH - 1)), waits=w, inc="pe"))
            return evs

        def qk_ln_ip(buf, g_col, b_col, ev_in):
            evq = p.op("scalar", lambda s: s.square(Sslot(2), buf),
                       waits=[ev_in, p.gw(("sslot", 2))], inc="se")
            mevs = []
            for hh in range(2):
                mevs.append(p.op("tensor", lambda t, hh=hh: t.matmul(
                    ps[0:1, 2048 + hh * 512:2048 + (hh + 1) * 512], oc_b[:, :],
                    buf[:, hh * 512:(hh + 1) * 512], start=True, stop=True),
                    waits=[ev_in] + p.ps_w(2048, 3072), inc="pe"))
                mevs.append(p.op("tensor", lambda t, hh=hh: t.matmul(
                    ps[0:1, 3072 + hh * 512:3072 + (hh + 1) * 512], oc_b[:, :],
                    Sslot(2)[:, hh * 512:(hh + 1) * 512], start=True, stop=True),
                    waits=[evq] + p.ps_w(3072, 4096), inc="pe"))
            ev_mu = p.op("vector", lambda v: v.tensor_scalar(
                rows[0:1, 0:1024], ps[0:1, 2048:3072], 1.0 / 128.0, None,
                op0=ALU.mult), waits=mevs + [p.gw("rows")], inc="ve")
            ev_msq = p.op("vector", lambda v: v.tensor_scalar(
                rows[0:1, 1024:2048], ps[0:1, 3072:4096], 1.0 / 128.0, None,
                op0=ALU.mult), waits=mevs, inc="ve")
            p.gs(("sslot", 2), ev_msq)
            ev_t = p.op("vector", lambda v: v.tensor_mul(
                scr[0:1, 0:1024], rows[0:1, 0:1024], rows[0:1, 0:1024]),
                waits=[ev_mu, p.gw("scr")], inc="ve")
            ev_var = p.op("vector", lambda v: v.tensor_sub(
                rows[0:1, 1024:2048], rows[0:1, 1024:2048], scr[0:1, 0:1024]),
                waits=[ev_msq, ev_t], inc="ve")
            ev_sd = p.op("scalar", lambda s: s.activation(
                out=scr[0:1, 0:1024], in_=rows[0:1, 1024:2048], func=AF.Sqrt,
                bias=eps_sb[0:1, 0:1]), waits=[ev_var, ev_memset], inc="se")
            ev_rstd = p.op("vector", lambda v: v.reciprocal(
                rows[0:1, 1024:2048], scr[0:1, 0:1024]), waits=[ev_sd], inc="ve")
            bevs = []
            for hh in range(2):
                bevs.append(p.op("tensor", lambda t, hh=hh: t.matmul(
                    ps[:, 2048 + hh * 512:2048 + (hh + 1) * 512], or_f[0:1, :],
                    rows[0:1, hh * 512:(hh + 1) * 512], start=True, stop=True),
                    waits=[ev_mu] + p.ps_w(2048, 3072), inc="pe"))
                bevs.append(p.op("tensor", lambda t, hh=hh: t.matmul(
                    ps[:, 3072 + hh * 512:3072 + (hh + 1) * 512], or_f[0:1, :],
                    rows[0:1, 1024 + hh * 512:1024 + (hh + 1) * 512],
                    start=True, stop=True),
                    waits=[ev_rstd] + p.ps_w(3072, 4096), inc="pe"))
            p.gs("rows", bevs[-1])
            ev1 = p.op("vector", lambda v: v.tensor_sub(
                scr[:, :], buf, ps[:, 2048:3072]), waits=bevs, inc="ve")
            ev2 = p.op("vector", lambda v: v.tensor_tensor(
                scr[:, :], scr[:, :], ps[:, 3072:4096], op=ALU.mult),
                waits=[ev1], inc="ve")
            p.ps_r(2048, 4096, ev2)
            ev3 = p.op("vector", lambda v, g=g_col, b=b_col: v.tensor_scalar(
                buf, scr[:, :], qn_sb[:, g:g + 1], qn_sb[:, b:b + 1],
                op0=ALU.mult, op1=ALU.add),
                waits=[ev2] + EVI, inc="ve")
            p.gs("scr", ev3)
            return ev3

        def post_chunk(buf, m, is_q):
            if is_a:
                ev = p.gw(("evac", is_q, m))
            else:
                ev = qk_ln_ip(buf, 0 if is_q else 2, 1 if is_q else 3,
                              p.gw(("evac", is_q, m)))
            ev_rope = rope_chunk(buf, ev, tab_ev)
            ev_nrm = norm2max_chunk(buf, ev_rope, (m if is_q else 16 + m))
            return ev_rope, ev_nrm

        ship_evs = []
        nrm_evs = []
        for m in range(n_kch + 1):
            if m < n_kch:
                wsl = m % 4
                evw = d("gpsimd", Wslot(wsl).rearrange("pp (k f) -> pp k f", f=128),
                        w_k_src(m), waits=[p.gw(("wslot", wsl))])
                po = (m % 2) * 1024
                mm_evs = proj_mms(
                    lambda k, wsl=wsl: Wslot(wsl)[:, k * 128:(k + 1) * 128], evw, po)
                p.gs(("wslot", wsl), mm_evs[-1])
                raw = m % 2
                if is_a:
                    ev_evac = p.op("vector", lambda v, m=m, po=po, raw=raw: v.tensor_scalar(
                        Sslot(raw), ps[:, po:po + 1024], bk_sb[:, m:m + 1], None,
                        op0=ALU.add),
                        waits=mm_evs + [p.gw(("sslot", raw))] + EVI, inc="ve")
                else:
                    ev_evac = p.op("vector", lambda v, po=po, raw=raw: v.tensor_copy(
                        Sslot(raw), ps[:, po:po + 1024]),
                        waits=mm_evs + [p.gw(("sslot", raw))], inc="ve")
                p.ps_r(po, po + 1024, ev_evac)
                p.gs(("evac", False, m), ev_evac)
            if m >= 1:
                mp = m - 1
                raw = mp % 2
                ev_rope, ev_nrm = post_chunk(Sslot(raw), mp, False)
                nrm_evs.append(ev_nrm)
                ev_ship = d("sync", kv_own.ap()[mp * 128:(mp + 1) * 128, :],
                            Sslot(raw), waits=[ev_rope, ev_nrm])
                p.gs(("sslot", raw), ev_ship)
                ship_evs.append(ev_ship)
        ev_nrm_ship = d("gpsimd", kv_own.ap()[NR:NR + 1, 0:n_kch],
                        vec16[0:1, 16:16 + n_kch], waits=nrm_evs)

        def v_stage(sl, extra=()):
            reg = sl % 2
            if is_a:
                vsrc = ap["wv_a"].rearrange(
                    "(k pp) (s f) -> pp k s f", pp=128, f=512)[:, :, sl, :]
            else:
                vsrc = ap["wqkv_b"].rearrange(
                    "(k pp) (s f) -> pp k s f", pp=128, f=128)[:, :, 20 + sl, :]
            vstg = buf_B[:, reg * 8192:reg * 8192 + 16 * fw]
            return vstg, d(
                "gpsimd", vstg.rearrange("pp (k f) -> pp k f", f=fw), vsrc,
                waits=[p.gw(("bufB", reg)), p.gw(("bufBv", reg))] + list(extra))
        vstg_evs = {0: v_stage(0), 1: v_stage(1)}
        preW_evs = []
        for k in range(4):
            preW_evs.append(d("gpsimd",
                              Wslot(k).rearrange("pp (kk f) -> pp kk f", f=128),
                              w_q_src(k), waits=[p.gw(("wslot", k))]))
        if is_a:
            ev_agK = coll("AllGather", ALU.bypass, [[0, 1, 2, 3], [4, 5, 6, 7]],
                          None, None, ship_evs + [ev_nrm_ship],
                          in_ap=kv_own.ap()[0:VOFF, :], out_ap=kA_all.ap())
        v_tile_evs = []
        for sl in range(4):
            reg = sl % 2
            vstg, evw = vstg_evs[sl]
            for tt in range(NQT):
                lo = 2048 + reg * 512
                mm = []
                for k in range(NCH):
                    w = [evw, ln_ev[NCH - 1]]
                    if k == 0:
                        w += p.ps_w(lo, lo + fw)
                    mm.append(p.op("tensor", lambda t, tt=tt, k=k, lo=lo, vstg=vstg: t.matmul(
                        ps[:, lo:lo + fw], Lk(k)[:, tt * 128:(tt + 1) * 128],
                        vstg[:, k * fw:(k + 1) * fw],
                        start=(k == 0), stop=(k == NCH - 1)), waits=w, inc="pe"))
                ac = tt * 2048 + sl * fw
                ev_evac = p.op("vector", lambda v, lo=lo, ac=ac: v.tensor_copy(
                    buf_A[:, ac:ac + fw], ps[:, lo:lo + fw]),
                    waits=[mm[-1], p.gw(("abuf", tt * 2))], inc="ve")
                p.ps_r(lo, lo + fw, ev_evac)
                v_tile_evs.append((tt, sl, ev_evac))
            p.gs(("bufBv", reg), mm[-1])
            if sl + 2 < 4:
                vstg_evs[sl + 2] = v_stage(sl + 2, extra=[mm[-1]])
        if is_a:
            v_ship_view = kv_own.ap()[VOFF:VOFF + VROWS, :].rearrange(
                "(t h) c -> t (h c)", h=2)
        else:
            v_ship_view = kv_own.ap()[VOFF:VOFF + VROWS, :].rearrange(
                "t (h c) -> (t h) c", h=2)
        vship_evs = []
        for tt, sl, ev in v_tile_evs:
            ac = tt * 2048 + sl * fw
            ev_ship = d("sync",
                        v_ship_view[tt * 128:(tt + 1) * 128, sl * fw:(sl + 1) * fw],
                        buf_A[:, ac:ac + fw], waits=[ev])
            vship_evs.append(ev_ship)
            p.gs(("abuf", tt * 2), ev_ship)
            p.gs(("abuf", tt * 2 + 1), ev_ship)

        pre_evs = {}
        for k in range(4, 8):
            dst = buf_B[:, k * 2048:(k + 1) * 2048]
            pre_evs[k] = d("gpsimd", dst.rearrange("pp (kk f) -> pp kk f", f=128),
                           w_q_src(k), waits=[p.gw(("bufBv", 1))])
        for k in range(8, 12):
            dst = buf_B[:, (k - 8) * 2048:(k - 7) * 2048]
            pre_evs[k] = d("gpsimd", dst.rearrange("pp (kk f) -> pp kk f", f=128),
                           w_q_src(k), waits=[p.gw(("bufBv", 0))])

        if is_a:
            ev_agV = coll("AllGather", ALU.bypass, [[0, 1, 2, 3], [4, 5, 6, 7]],
                          None, None, vship_evs,
                          in_ap=kv_own.ap()[VOFF:VOFF + VROWS, :], out_ap=vA_all.ap())
            ev_ag = ev_agV
        else:
            ev_ag = coll("AllGather", ALU.bypass, [[0, 1, 2, 3], [4, 5, 6, 7]],
                         kv_own, kvB_all, ship_evs + vship_evs + [ev_nrm_ship])
            ev_agK = ev_ag
            ev_agV = ev_ag

        rope_evs = []
        nq_evs = []
        LAG = 1 if is_a else 2
        for m in range(16 + LAG):
            if m < 16:
                po = (m % 2) * 1024
                if m < 4:
                    mm_evs = proj_mms(
                        lambda k, m=m: Wslot(m)[:, k * 128:(k + 1) * 128],
                        preW_evs[m], po)
                    p.gs(("wslot", m), mm_evs[-1])
                elif m < 12:
                    col0 = (m if m < 8 else m - 8) * 2048
                    lhs_fn = lambda k, col0=col0: buf_B[:, col0 + k * 128:col0 + (k + 1) * 128]
                    mm_evs = proj_mms(lhs_fn, pre_evs[m], po)
                else:
                    wsl = m % 4
                    evw = d("gpsimd", Wslot(wsl).rearrange("pp (k f) -> pp k f", f=128),
                            w_q_src(m), waits=[p.gw(("wslot", wsl))])
                    mm_evs = proj_mms(
                        lambda k, wsl=wsl: Wslot(wsl)[:, k * 128:(k + 1) * 128], evw, po)
                    p.gs(("wslot", wsl), mm_evs[-1])
                if is_a:
                    ev_evac = p.op("vector", lambda v, m=m, po=po: v.tensor_scalar(
                        Ak(m), ps[:, po:po + 1024], bq_sb[:, m:m + 1], None,
                        op0=ALU.add),
                        waits=mm_evs + [p.gw(("abuf", m))] + EVI, inc="ve")
                else:
                    ev_evac = p.op("vector", lambda v, m=m, po=po: v.tensor_copy(
                        Ak(m), ps[:, po:po + 1024]),
                        waits=mm_evs + [p.gw(("abuf", m))], inc="ve")
                p.ps_r(po, po + 1024, ev_evac)
                p.gs(("evac", True, m), ev_evac)
            if m >= LAG:
                mp = m - LAG
                ev_rope, ev_nq = post_chunk(Ak(mp), mp, True)
                rope_evs.append(ev_rope)
                nq_evs.append(ev_nq)
        p.gs("tabs", rope_evs[-1])
        tap("q_" + var, buf_A[:, :], (128, 16384), rope_evs, BF16)

        SEGK = VOFF
        k_all_t = kA_all if is_a else kvB_all
        SEGF = VOFF if is_a else 1152
        ev_mkload = d("gpsimd", vec16[0:1, 64:64 + 4 * n_kch],
                      k_all_t.ap().rearrange("(r s) q -> s r q", s=SEGF)[
                          NR:NR + 1, :, 0:n_kch],
                      waits=[ev_agK])
        ev_mk = p.op("vector", lambda v: v.reduce_max(
            vec16[0:1, 32:33], vec16[0:1, 64:64 + 4 * n_kch], axis=AX.X),
            waits=[ev_mkload], inc="ve")
        ev_b1 = p.op("vector", lambda v: v.tensor_scalar(
            vec16[0:1, 48:64], vec16[0:1, 0:16], vec16[0:1, 32:33], None,
            op0=ALU.mult), waits=[ev_mk] + nq_evs, inc="ve")
        ev_b2 = p.op("scalar", lambda s: s.activation(
            out=vec16[0:1, 0:16], in_=vec16[0:1, 48:64], func=AF.Sqrt,
            bias=eps_sb[0:1, 0:1]), waits=[ev_b1, ev_memset], inc="se")
        ev_b3 = p.op("vector", lambda v: v.tensor_scalar(
            vec16[0:1, 48:64], vec16[0:1, 0:16], -B_SAFETY * ISQ_D, None,
            op0=ALU.mult), waits=[ev_b2], inc="ve")
        ev_b4 = p.op("tensor", lambda t: t.matmul(
            ps[:, 3072:3088], or_f[0:1, :], vec16[0:1, 48:64],
            start=True, stop=True),
            waits=[ev_b3] + p.ps_w(3072, 3584), inc="pe")
        ev_negB = p.op("vector", lambda v: v.tensor_copy(
            negB_sb[:, :], ps[:, 3072:3088]), waits=[ev_b4], inc="ve")
        p.ps_r(3072, 3584, ev_negB)
        tap("negb_" + var, negB_sb[:, :], (128, 16), [ev_negB])
        p.barrier()

        z_evs = []
        k_view = k_all_t.ap().rearrange("(r s) q -> s r q", s=SEGF)
        if is_a:
            v_view = vA_all.ap().rearrange("(r s) q -> r s q", s=2048).rearrange(
                "r (t h) c -> t r (h c)", h=2)
        else:
            v_view = kvB_all.ap().rearrange("(r s) q -> r s q", s=1152)[
                :, VOFF:VOFF + VROWS, :].rearrange("r t (h c) -> (t h) r c", h=2)
        for h in range(16):
            kh = h if is_a else h // 4
            kv_evs = [None] * 8

            def emit_kv_load(kb4, kh=kh, kv_evs=kv_evs):
                j = kb4 % 2
                ksrc = k_view[kh * 128:(kh + 1) * 128, :, kb4 * 128:(kb4 + 1) * 128]
                vsrc = v_view[kb4 * 128:(kb4 + 1) * 128, :, kh * 128:(kh + 1) * 128]
                gwk = p.gw(("kpair", j))
                evk = d("sync", Kslot(j).rearrange("pp (r q) -> pp r q", q=128),
                        ksrc, waits=[gwk, ev_agK])
                evv = d("sync", Vslot(j).rearrange("pp (r f) -> pp r f", f=128),
                        vsrc, waits=[gwk, ev_agV])
                kv_evs[kb4] = (evk, evv)

            emit_kv_load(0)
            emit_kv_load(1)
            last_sums = None
            last_pv = None
            pend = None

            def emit_sums_pv(kb, regions, exp_evs, evv, j, dd):
                nonlocal last_sums, last_pv
                for (r0, r1) in regions:
                    w = exp_evs + (p.ps_w(3072 + r0, 3072 + r1) if kb == 0 else [])
                    last_sums = p.op("tensor", lambda t, r0=r0, r1=r1, kb=kb: t.matmul(
                        ps[0:1, 3072 + r0:3072 + r1], oc_b[:, :],
                        Sslot(kb % 3)[:, r0:r1],
                        start=(kb == 0), stop=(kb == NKB - 1)), waits=w, inc="pe")
                for (r0, r1) in regions:
                    w = exp_evs + [evv] + (p.ps_w(1024 + r0, 1024 + r1) if kb == 0 else [])
                    last_pv = p.op("tensor", lambda t, r0=r0, r1=r1, j=j, dd=dd, kb=kb: t.matmul(
                        ps[:, 1024 + r0:1024 + r1], Vslot(j)[:, dd * 128:(dd + 1) * 128],
                        Sslot(kb % 3)[:, r0:r1],
                        start=(kb == 0), stop=(kb == NKB - 1)), waits=w, inc="pe")
                p.gs(("sslot", kb % 3), last_pv)
                if dd == 3:
                    p.gs(("kpair", j), last_pv)
                    m0_ = kb // 4
                    if m0_ + 2 < 8:
                        emit_kv_load(m0_ + 2)

            for kb in range(NKB):
                m0 = kb // 4
                dd = kb % 4
                j = m0 % 2
                so = 0 if kb % 2 == 0 else 2048
                live0 = m0 * 128
                regions = [(live0, 512), (512, 1024)] if live0 < 512 else [(live0, 1024)]
                evk, evv = kv_evs[m0]
                sc_evs = []
                for (r0, r1) in regions:
                    w = [evk] + p.ps_w(so + r0, so + r1)
                    sc_evs.append(p.op("tensor", lambda t, so=so, r0=r0, r1=r1, j=j, dd=dd, h=h: t.matmul(
                        ps[:, so + r0:so + r1], Kslot(j)[:, dd * 128:(dd + 1) * 128],
                        Ak(h)[:, r0:r1], start=True, stop=False),
                        waits=w, inc="pe"))
                ev_mask = p.op("tensor", lambda t, so=so, live0=live0, dd=dd: t.matmul(
                    ps[:, so + live0:so + live0 + 128], id_sb[:, :],
                    mask_sb[:, dd * 128:(dd + 1) * 128], start=False, stop=True),
                    waits=sc_evs + EVI, inc="pe")
                pslot = kb % 3
                exp_evs = []
                for (r0, r1) in regions:
                    exp_evs.append(p.op("scalar", lambda s, so=so, r0=r0, r1=r1, pslot=pslot, h=h: s.activation(
                        out=Sslot(pslot)[:, r0:r1], in_=ps[:, so + r0:so + r1],
                        func=AF.Exp, bias=negB_sb[:, h:h + 1], scale=ISQ_D),
                        waits=[ev_mask, ev_negB, p.gw(("sslot", pslot))], inc="se"))
                for (r0, r1) in regions:
                    p.ps_r(so + r0, so + r1, exp_evs[-1])
                if pend is not None:
                    emit_sums_pv(*pend)
                pend = (kb, regions, exp_evs, evv, j, dd)
            emit_sums_pv(*pend)
            ev_sc = p.op("scalar", lambda s: s.copy(rows[0:1, 0:1024],
                                                    ps[0:1, 3072:4096]),
                         waits=[last_sums, p.gw("rows")], inc="se")
            rb_evs = []
            for hh in range(2):
                rb_evs.append(p.op("tensor", lambda t, hh=hh: t.matmul(
                    ps[:, 2048 + hh * 512:2048 + (hh + 1) * 512], or_f[0:1, :],
                    rows[0:1, hh * 512:(hh + 1) * 512], start=True, stop=True),
                    waits=[ev_sc] + p.ps_w(2048, 3072), inc="pe"))
            p.gs("rows", rb_evs[-1])
            ev_ln = p.op("scalar", lambda s: s.activation(
                out=ps[:, 3072:4096], in_=ps[:, 2048:3072], func=AF.Ln),
                waits=rb_evs, inc="se")
            ev_rcp = p.op("scalar", lambda s: s.activation(
                out=ps[:, 2048:3072], in_=ps[:, 3072:4096], func=AF.Exp, scale=-1.0),
                waits=[ev_ln], inc="se")
            p.ps_r(3072, 4096, ev_rcp)
            ev_zc = p.op("vector", lambda v: v.tensor_copy(scr[:, :], ps[:, 1024:2048]),
                         waits=[last_pv, p.gw("scr")], inc="ve")
            p.ps_r(1024, 2048, ev_zc)
            if is_a:
                ev_zm = p.op("vector", lambda v: v.tensor_tensor(
                    scr[:, :], scr[:, :], ps[:, 2048:3072], op=ALU.mult),
                    waits=[ev_zc, ev_rcp], inc="ve")
                ev_zn = p.op("vector", lambda v, h=h: v.tensor_scalar(
                    Bk(h), scr[:, :], bv_sb[:, h:h + 1], None, op0=ALU.add),
                    waits=[ev_zm] + EVI, inc="ve")
            else:
                ev_zn = p.op("vector", lambda v, h=h: v.tensor_tensor(
                    Bk(h), scr[:, :], ps[:, 2048:3072], op=ALU.mult),
                    waits=[ev_zc, ev_rcp], inc="ve")
            p.gs("scr", ev_zn)
            p.ps_r(2048, 3072, ev_zn)
            z_evs.append(ev_zn)
        tap("z_" + var, buf_B[:, :], (128, 16384), z_evs, BF16)
        p.barrier()

        wo_ap = ap["wo_a"] if is_a else ap["wo_b"]
        for m in range(16):
            wsl = m % 4
            wsrc = wo_ap.rearrange("(k pp) (mm f) -> pp k mm f", pp=128, f=128)[:, :, m, :]
            evw = d("gpsimd", Wslot(wsl).rearrange("pp (k f) -> pp k f", f=128),
                    wsrc, waits=[p.gw(("wslot", wsl))])
            po = (m % 2) * 1024
            mm_evs = []
            for k in range(16):
                for hh in range(2):
                    w = [evw, z_evs[15]]
                    if k == 0:
                        w += p.ps_w(po + hh * 512, po + (hh + 1) * 512)
                    mm_evs.append(p.op("tensor", lambda t, k=k, hh=hh, po=po, wsl=wsl: t.matmul(
                        ps[:, po + hh * 512:po + (hh + 1) * 512],
                        Wslot(wsl)[:, k * 128:(k + 1) * 128],
                        Bk(k)[:, hh * 512:(hh + 1) * 512],
                        start=(k == 0), stop=(k == 15)), waits=w, inc="pe"))
            p.gs(("wslot", wsl), mm_evs[-1])
            if is_a:
                ev_add = p.op("vector", lambda v, m=m, po=po: v.scalar_tensor_tensor(
                    Rk(m), ps[:, po:po + 1024], bo_sb[:, m:m + 1], Rk(m),
                    op0=ALU.add, op1=ALU.add),
                    waits=mm_evs + [resid_ev[m]] + EVI, inc="ve")
            else:
                ev_add = p.op("vector", lambda v, m=m, po=po: v.tensor_tensor(
                    Rk(m), Rk(m), ps[:, po:po + 1024], op=ALU.add),
                    waits=mm_evs + [resid_ev[m]], inc="ve")
            p.ps_r(po, po + 1024, ev_add)
            resid_ev[m] = ev_add
        p.barrier()
        for reg in range(2):
            p.gs(("bufB", reg), resid_ev[15])
            p.gs(("bufBv", reg), resid_ev[15])
        for m in range(16):
            p.gs(("abuf", m), resid_ev[15])

    def ffn_sublayer():
        emit_layernorm(True)
        p.barrier()
        if W2["ev"] is None and not DBG_NO_W2:
            W2["ev"] = coll("AllGather", ALU.bypass, [list(range(8))],
                            wpp_own, wpp_all, w2_out_evs)
        xp_evs = []
        for a2 in range(16):
            po = (a2 % 2) * 1024
            mm_evs = []
            for a in range(16):
                hsb = hp_sb if H16[a2, a] > 0 else hm_sb
                for hh in range(2):
                    w = [ln_ev[NCH - 1]] + EVI
                    if a == 0:
                        w += p.ps_w(po + hh * 512, po + (hh + 1) * 512)
                    mm_evs.append(p.op("tensor", lambda t, a=a, hh=hh, po=po, hsb=hsb: t.matmul(
                        ps[:, po + hh * 512:po + (hh + 1) * 512], hsb[:, :],
                        Lk(a)[:, hh * 512:(hh + 1) * 512],
                        start=(a == 0), stop=(a == 15)), waits=w, inc="pe"))
            ev_e = p.op("vector", lambda v, a2=a2, po=po: v.tensor_scalar(
                Bk(a2), ps[:, po:po + 1024], HAD_N1, None, op0=ALU.mult),
                waits=mm_evs + [p.gw(("bbuf", a2))], inc="ve")
            p.ps_r(po, po + 1024, ev_e)
            xp_evs.append(ev_e)
        tap("xp", buf_B[:, :], (128, 16384), xp_evs, BF16)
        p.barrier()

        for q in range(4):
            h_evs = []
            for icl in range(16):
                I = q * 16 + icl
                upsrc = ap["w_upT"].rearrange(
                    "(k pp) (i f) -> pp k i f", pp=128, f=128)[:, :, I, :]
                gtsrc = ap["w_gateT"].rearrange(
                    "(k pp) (i f) -> pp k i f", pp=128, f=128)[:, :, I, :]
                su_ = icl % 2
                sg_ = 2 + icl % 2
                evwu = d("gpsimd", Wslot(su_).rearrange("pp (k f) -> pp k f", f=128),
                         upsrc, waits=[p.gw(("wslot", su_))])
                evwg = d("gpsimd", Wslot(sg_).rearrange("pp (k f) -> pp k f", f=128),
                         gtsrc, waits=[p.gw(("wslot", sg_))])
                po_u = (icl % 2) * 1024
                po_g = 2048 + (icl % 2) * 1024
                mmu, mmg = [], []
                for k in range(16):
                    for hh in range(2):
                        wu = [evwu, xp_evs[15]]
                        wg = [evwg, xp_evs[15]]
                        if k == 0:
                            wu += p.ps_w(po_u + hh * 512, po_u + (hh + 1) * 512)
                            wg += p.ps_w(po_g + hh * 512, po_g + (hh + 1) * 512)
                        mmu.append(p.op("tensor", lambda t, k=k, hh=hh, po=po_u, su_=su_: t.matmul(
                            ps[:, po + hh * 512:po + (hh + 1) * 512],
                            Wslot(su_)[:, k * 128:(k + 1) * 128],
                            Bk(k)[:, hh * 512:(hh + 1) * 512],
                            start=(k == 0), stop=(k == 15)), waits=wu, inc="pe"))
                        mmg.append(p.op("tensor", lambda t, k=k, hh=hh, po=po_g, sg_=sg_: t.matmul(
                            ps[:, po + hh * 512:po + (hh + 1) * 512],
                            Wslot(sg_)[:, k * 128:(k + 1) * 128],
                            Bk(k)[:, hh * 512:(hh + 1) * 512],
                            start=(k == 0), stop=(k == 15)), waits=wg, inc="pe"))
                p.gs(("wslot", su_), mmu[-1])
                p.gs(("wslot", sg_), mmg[-1])
                ssl = icl % 3
                ev_silu = p.op("scalar", lambda s, po=po_g, I=I, ssl=ssl: s.activation(
                    out=Sslot(ssl), in_=ps[:, po:po + 1024], func=AF.Silu,
                    scale=wsg_sb[:, I:I + 1]),
                    waits=mmg + [p.gw(("sslot", ssl))] + EVI, inc="se")
                ev_h = p.op("vector", lambda v, po=po_u, I=I, icl=icl, ssl=ssl: v.scalar_tensor_tensor(
                    Ak(icl), ps[:, po:po + 1024], wsu_sb[:, I:I + 1], Sslot(ssl),
                    op0=ALU.mult, op1=ALU.mult),
                    waits=[ev_silu, p.gw(("abuf", icl))] + mmu + EVI, inc="ve")
                p.ps_r(po_u, po_u + 1024, ev_h)
                p.ps_r(po_g, po_g + 1024, ev_h)
                p.gs(("sslot", ssl), ev_h)
                h_evs.append(ev_h)
            for m in range(16):
                wsl = m % 4
                s_rank = m // 2
                wsrc = wpp_all.ap()[s_rank * 8192 + q * 2048:s_rank * 8192 + (q + 1) * 2048,
                                    (m % 2) * 128:(m % 2) * 128 + 128]
                wsrc = wsrc.rearrange("(i pp) f -> pp i f", pp=128)
                evw = d("sync", Wslot(wsl).rearrange("pp (i f) -> pp i f", f=128),
                        wsrc, waits=[p.gw(("wslot", wsl)), W2["ev"]])
                po = (m % 2) * 1024
                mm_evs = []
                for icl in range(16):
                    for hh in range(2):
                        w = [evw, h_evs[15]]
                        if icl == 0:
                            w += p.ps_w(po + hh * 512, po + (hh + 1) * 512)
                        mm_evs.append(p.op("tensor", lambda t, icl=icl, hh=hh, po=po, wsl=wsl: t.matmul(
                            ps[:, po + hh * 512:po + (hh + 1) * 512],
                            Wslot(wsl)[:, icl * 128:(icl + 1) * 128],
                            Ak(icl)[:, hh * 512:(hh + 1) * 512],
                            start=(icl == 0), stop=(icl == 15)), waits=w, inc="pe"))
                p.gs(("wslot", wsl), mm_evs[-1])
                ev_add = p.op("vector", lambda v, m=m, po=po: v.scalar_tensor_tensor(
                    Rk(m), ps[:, po:po + 1024], wsdp_sb[:, m:m + 1], Rk(m),
                    op0=ALU.mult, op1=ALU.add),
                    waits=mm_evs + [resid_ev[m]] + EVI, inc="ve")
                p.ps_r(po, po + 1024, ev_add)
                resid_ev[m] = ev_add
            for icl in range(16):
                p.gs(("abuf", icl), resid_ev[15])
        for reg in range(2):
            p.gs(("bufB", reg), resid_ev[15])
        p.barrier()

    attn_sublayer("a")
    tap("r_a", buf_R[:, :], (128, 16384), list(resid_ev))
    ffn_sublayer()
    tap("r_f1", buf_R[:, :], (128, 16384), list(resid_ev))
    attn_sublayer("b")
    tap("r_b", buf_R[:, :], (128, 16384), list(resid_ev))
    ffn_sublayer()

    st_evs = [d("sync", out_x[k * 128:(k + 1) * 128, :], Rk(k),
                waits=[resid_ev[k]]) for k in range(NCH)]
    p.op("gpsimd", lambda g: g.engine_nop(), waits=st_evs)

    with nc.Block() as block:
        OWN = {} if DBG_KEEP_SELF else {"tensor": "pe"}

        def emitter(name):
            own = OWN.get(name)
            def run(eng):
                seen = {}
                for fn, ws, inc, amt in p.ops[name]:
                    for s, v in sorted(ws.items()):
                        if s == own:
                            continue
                        if seen.get(s, 0) < v:
                            eng.wait_ge(sems[s], v)
                            seen[s] = v
                    inst = fn(eng)
                    if inc is not None:
                        inst.then_inc(sems[inc], amt)
            return run
        block.tensor(emitter("tensor"))
        block.vector(emitter("vector"))
        block.scalar(emitter("scalar"))
        block.gpsimd(emitter("gpsimd"))
        block.sync(emitter("sync"))

    es.close()
    return nc, dbg


_cached = {}


def kernel(**inputs):
    maps = _prep_inputs(inputs)
    key = tuple(sorted(_TAPS))
    if key not in _cached:
        _cached[key] = build_program()
    nc, dbg = _cached[key]
    res = run_bass_kernel_spmd(nc, maps, core_ids=list(range(8)))
    out = _assemble(res.results)
    if dbg:
        kernel._dbg = [{t: np.asarray(r["dbg_" + t]) for t in dbg} for r in res.results]
    return out


kernel._dbg = None
